# version 12
# baseline (speedup 1.0000x reference)
import contextlib
import numpy as np

import concourse.bass as bass
import concourse.mybir as mybir
from concourse.bass_utils import run_bass_kernel_spmd

F32 = mybir.dt.float32
BF16 = mybir.dt.bfloat16
AF = mybir.ActivationFunctionType
ALU = mybir.AluOpType
AX = mybir.AxisListType

ENGINES = ("sync", "vector", "scalar", "tensor", "gpsimd")
ESEM = {"vector": "v", "scalar": "a", "tensor": "t", "gpsimd": "g"}
NEG = -3.0e38
EPS = 1e-5
D = 1024
KC = 8
NCORES = 8
BPC = 2


class Sem:
    __slots__ = ("h", "n", "name")

    def __init__(self, h, name):
        self.h = h
        self.n = 0
        self.name = name


class Ctx:

    def __init__(self, name="prog"):
        self.nc = bass.Bass("TRN2", target_bir_lowering=False)
        self.st = contextlib.ExitStack()
        self.ops = {e: [] for e in ENGINES}
        self.s = {}
        for nm in ("ld", "st", "w", "t", "v", "a", "g"):
            self.s[nm] = Sem(self.st.enter_context(self.nc.semaphore("s_" + nm)), nm)
        self.rot = [Sem(self.st.enter_context(self.nc.semaphore(f"r{i}")), f"r{i}")
                    for i in range(12)]
        self._uid = 0

    def uid(self):
        self._uid += 1
        return self._uid

    def dram_in(self, name, shape, dtype=F32):
        return self.nc.dram_tensor(name, list(shape), dtype, kind="ExternalInput").ap()

    def dram_out(self, name, shape, dtype=F32):
        return self.nc.dram_tensor(name, list(shape), dtype, kind="ExternalOutput").ap()

    def dram_tmp(self, name, shape, dtype=F32):
        return self.nc.dram_tensor(name, list(shape), dtype, kind="Internal").ap()

    def sb(self, name, shape, dtype, stack=None):
        return (stack or self.st).enter_context(
            self.nc.sbuf_tensor(f"{name}_{self.uid()}", shape, dtype))

    def ps(self, name, shape, dtype=F32, stack=None):
        return (stack or self.st).enter_context(
            self.nc.psum_tensor(f"{name}_{self.uid()}", shape, dtype))

    def scope(self):
        return self.st.enter_context(contextlib.ExitStack())

    def op(self, eng, method, *args, inc=None, **kw):
        if inc == "auto":
            inc = [(self.s[ESEM[eng]], 1)]
        def f(E, method=method, args=args, kw=kw, inc=inc):
            ins = getattr(E, method)(*args, **kw)
            if inc:
                for s, n in inc:
                    ins.then_inc(s.h, n)
        self.ops[eng].append(f)
        if inc:
            for s, n in inc:
                s.n += n

    def wait(self, eng, sem, val=None):
        v = sem.n if val is None else val
        if v <= 0:
            return
        def f(E, sem=sem, v=v):
            E.wait_ge(sem.h, v)
        self.ops[eng].append(f)

    def cop(self, eng, method, *args, **kw):
        self.wait(eng, self.s[ESEM[eng]])
        self.op(eng, method, *args, inc="auto", **kw)

    def wait_all(self, eng, names=("w", "v", "a", "t", "g")):
        for nm in names:
            self.wait(eng, self.s[nm])

    def dma(self, eng, out, in_, sem=None, **kw):
        if sem is None:
            sem = self.s["w"] if eng == "gpsimd" else self.s["ld"]
        self.op(eng, "dma_start", inc=[(sem, 16)], out=out, in_=in_, **kw)
        return sem.n

    def barrier(self, engines=ENGINES):
        counts = [(s, s.n) for s in list(self.s.values()) + self.rot]
        for e in engines:
            for s, v in counts:
                self.wait(e, s, v)

    def finish(self):
        with self.nc.Block() as block:
            for e in ENGINES:
                lst = self.ops[e]
                getattr(block, e)(lambda E, lst=lst: [f(E) for f in lst])
        self.st.close()
        return self.nc


class Consts:
    def __init__(self, c: Ctx):
        self.ident_b = c.sb("ident_b", [128, 128], BF16)
        self.causal_add = c.sb("causal_add", [128, 128], F32)
        self.ones_row = c.sb("ones_row", [1, 128], F32)
        self.ones_col_b = c.sb("ones_col_b", [128, 1], BF16)
        self.ones_col_f = c.sb("ones_col_f", [128, 1], F32)
        self.ident_f = c.sb("ident_f", [128, 128], F32)
        g = "gpsimd"
        c.op(g, "memset", self.ident_b[:], 0.0, inc="auto")
        c.cop(g, "affine_select", out=self.ident_b[:], in_=self.ident_b[:],
              compare_op=ALU.not_equal, fill=1.0, base=0,
              pattern=[[-1, 128]], channel_multiplier=1)
        c.op(g, "memset", self.causal_add[:], 0.0, inc="auto")
        c.cop(g, "affine_select", out=self.causal_add[:], in_=self.causal_add[:],
              compare_op=ALU.is_ge, fill=NEG, base=0,
              pattern=[[-1, 128]], channel_multiplier=1)
        self.tri_keep = c.sb("tri_keep", [128, 128], F32)
        c.op(g, "memset", self.tri_keep[:], 1.0, inc="auto")
        c.cop(g, "affine_select", out=self.tri_keep[:], in_=self.tri_keep[:],
              compare_op=ALU.is_ge, fill=0.0, base=0,
              pattern=[[1, 128]], channel_multiplier=-1)
        c.op(g, "memset", self.ones_row[:], 1.0, inc="auto")
        c.op(g, "memset", self.ones_col_b[:], 1.0, inc="auto")
        c.op(g, "memset", self.ones_col_f[:], 1.0, inc="auto")
        c.op(g, "memset", self.ident_f[:], 0.0, inc="auto")
        c.cop(g, "affine_select", out=self.ident_f[:], in_=self.ident_f[:],
              compare_op=ALU.not_equal, fill=1.0, base=0,
              pattern=[[-1, 128]], channel_multiplier=1)
        self.ready = c.s["g"].n


def stats_pass(c: Ctx, x_dram, row0, ntiles, double_ln, persist, stack):
    mu_s = c.sb("mu_s", [128, ntiles], F32, persist)
    rs_s = c.sb("rs_s", [128, ntiles], F32, persist)
    var_s = c.sb("var_s", [128, ntiles], F32, stack)
    tmp_s = c.sb("tmp_s", [128, ntiles], F32, stack)
    junks = [c.sb(f"junk{i}", [128, D], BF16, stack) for i in range(3)]
    bufs = [c.sb(f"stx{i}", [128, D], F32, stack) for i in range(3)]
    v, a = c.s["v"], c.s["a"]
    vplan, aplan = {}, {}
    for t in range(ntiles):
        b = bufs[t % 3]
        rs = c.rot[t % 3]
        if t >= 3:
            c.wait("sync", v, vplan[t - 3])
            c.wait("sync", a, aplan[t - 3])
        ldp = c.dma("sync", b[:], x_dram[row0 + t * 128: row0 + (t + 1) * 128, :],
                    sem=rs)
        c.wait("vector", rs, ldp)
        c.op("vector", "reduce_sum", mu_s[:, t:t + 1], b[:], AX.X, inc="auto")
        vplan[t] = v.n
        c.wait("scalar", rs, ldp)
        if t >= 3:
            c.wait("scalar", a, aplan[t - 3])
        c.op("scalar", "activation", junks[t % 3][:], b[:], AF.Square,
             accum_out=var_s[:, t:t + 1], inc="auto")
        aplan[t] = a.n
    c.wait("vector", a)
    c.cop("vector", "tensor_scalar_mul", mu_s[:], mu_s[:], 1.0 / D)
    c.cop("vector", "tensor_scalar_mul", var_s[:], var_s[:], 1.0 / D)
    c.cop("vector", "tensor_tensor", tmp_s[:], mu_s[:], mu_s[:], ALU.mult)
    c.cop("vector", "tensor_tensor", var_s[:], var_s[:], tmp_s[:], ALU.subtract)
    c.cop("vector", "tensor_scalar_add", tmp_s[:], var_s[:], EPS)
    c.wait("scalar", c.s["v"])
    c.op("scalar", "activation", tmp_s[:], tmp_s[:], AF.Sqrt, inc="auto")
    c.wait("vector", a)
    c.cop("vector", "reciprocal", rs_s[:], tmp_s[:])
    if double_ln:
        c.cop("vector", "tensor_tensor", tmp_s[:], var_s[:], rs_s[:], ALU.mult)
        c.cop("vector", "tensor_tensor", tmp_s[:], tmp_s[:], rs_s[:], ALU.mult)
        c.cop("vector", "tensor_scalar_add", tmp_s[:], tmp_s[:], EPS)
        c.wait("scalar", c.s["v"])
        c.op("scalar", "activation", tmp_s[:], tmp_s[:], AF.Sqrt, inc="auto")
        c.wait("vector", a)
        c.cop("vector", "reciprocal", tmp_s[:], tmp_s[:])
        c.cop("vector", "tensor_tensor", rs_s[:], rs_s[:], tmp_s[:], ALU.mult)
    return mu_s, rs_s


def xn_fm_build(c: Ctx, x_dram, row0, ntiles, mu_s, rs_s, xn_fm, cn, stack,
                gamma_bc=None, beta_bc=None):
    v, a, t_ = c.s["v"], c.s["a"], c.s["t"]
    bufs = [c.sb(f"xnx{i}", [128, D], F32, stack) for i in range(2)]
    tmb = [c.sb(f"xntm{i}", [128, D], BF16, stack) for i in range(2)]
    pst = [c.ps(f"xnps{i}", [128, 1024], BF16, stack) for i in range(2)]
    c.wait("tensor", c.s["g"], cn.ready)
    c.wait("vector", c.s["v"])
    c.wait("vector", c.s["a"])
    vplan, tplan, eplan = {}, {}, {}
    for t in range(ntiles):
        b, tm, pp = bufs[t % 2], tmb[t % 2], pst[t % 2]
        rs = c.rot[3 + t % 2]
        if t >= 2:
            c.wait("sync", v, vplan[t - 2])
        ldp = c.dma("sync", b[:], x_dram[row0 + t * 128: row0 + (t + 1) * 128, :],
                    sem=rs)
        c.wait("vector", rs, ldp)
        if t >= 2:
            c.wait("vector", t_, tplan[t - 2])
        if gamma_bc is None:
            c.op("vector", "tensor_scalar", tm[:], b[:], mu_s[:, t:t + 1],
                 rs_s[:, t:t + 1], ALU.subtract, ALU.mult, inc="auto")
        else:
            c.op("vector", "tensor_scalar", tm[:], b[:], mu_s[:, t:t + 1],
                 rs_s[:, t:t + 1], ALU.subtract, ALU.mult, inc="auto")
            c.cop("vector", "tensor_tensor", tm[:], tm[:], gamma_bc[:], ALU.mult)
            c.cop("vector", "tensor_tensor", tm[:], tm[:], beta_bc[:], ALU.add)
        vplan[t] = v.n
        c.wait("tensor", v, vplan[t])
        if t >= 2:
            c.wait("tensor", a, eplan[t - 2])
        for k in range(KC):
            c.op("tensor", "transpose", pp[:, k * 128:(k + 1) * 128],
                 tm[:, k * 128:(k + 1) * 128], cn.ident_b[:],
                 inc=[(t_, 1)] if k == KC - 1 else None)
        tplan[t] = t_.n
        c.wait("scalar", t_, tplan[t])
        c.op("scalar", "copy", xn_fm[:, :, t * 128:(t + 1) * 128], pp[:],
             inc="auto")
        eplan[t] = a.n


def w_cast_load(c: Ctx, w_dram, w_sb, rows, cols):
    wr = w_dram.rearrange("(k p) f -> k p f", p=128)
    for k in range(rows // 128):
        c.dma("gpsimd", w_sb[:, k, :], wr[k])
    return c.s["w"].n


def w_t_load(c: Ctx, w_dram, wt_sb, out_dim, in_dim, cn, stack, scale=None):
    v, t_ = c.s["v"], c.s["t"]
    nrow = out_dim // 128
    nk = in_dim // 128
    bufs = [c.sb(f"wld{i}", [128, in_dim], BF16, stack) for i in range(2)]
    pst = [c.ps(f"wlps{i}", [128, 1024], BF16, stack) for i in range(2)]
    wr = w_dram.rearrange("(k p) f -> k p f", p=128)
    c.wait("tensor", c.s["g"], cn.ready)
    c.wait_all("tensor")
    c.wait("gpsimd", t_)
    tplan, eplan = {}, {}
    for r in range(nrow):
        b, pp = bufs[r % 2], pst[r % 2]
        rsm = c.rot[5 + r % 2]
        if r >= 2:
            c.wait("gpsimd", t_, tplan[r - 2])
        wldp = c.dma("gpsimd", b[:], wr[r], sem=rsm)
        c.wait("tensor", rsm, wldp)
        if r >= 2:
            c.wait("tensor", v, eplan[r - 2])
        gev = None
        for g0 in range(0, nk, 8):
            gn = min(8, nk - g0)
            if gev is not None:
                c.wait("tensor", v, gev)
            for j in range(gn):
                k = g0 + j
                c.op("tensor", "transpose", pp[:, j * 128:(j + 1) * 128],
                     b[:, k * 128:(k + 1) * 128], cn.ident_b[:],
                     inc=[(t_, 1)] if j == gn - 1 else None)
            tplan[r] = t_.n
            c.wait("vector", t_, tplan[r])
            scl = None
            if scale is not None and scale[0] <= r * 128 and (r + 1) * 128 <= scale[1]:
                scl = scale[2]
            dst = wt_sb[:, g0:g0 + gn, r * 128:(r + 1) * 128]
            if scl is None:
                c.op("vector", "tensor_copy", dst, pp[:, 0:gn * 128],
                     inc="auto")
            else:
                c.op("vector", "tensor_scalar_mul", dst, pp[:, 0:gn * 128],
                     scl, inc="auto")
            eplan[r] = v.n
            gev = v.n


def proj_fm(c: Ctx, w_sb, xn_fm, out_fm, out_chunks, T, pst, scale=None,
            col0=0):
    v, t_ = c.s["v"], c.s["t"]
    c.wait_all("tensor")
    ntt = (T + 511) // 512
    idx = 0
    eplan = {}
    for mo in range(out_chunks):
        for tt in range(ntt):
            w_ = min(512, T - tt * 512)
            pp = pst[idx % 2]
            if idx >= 2:
                c.wait("tensor", v, eplan[idx - 2])
            for k in range(KC):
                c.op("tensor", "matmul", pp[:, :w_],
                     w_sb[:, k, col0 + mo * 128:col0 + (mo + 1) * 128],
                     xn_fm[:, k, tt * 512:tt * 512 + w_],
                     start=(k == 0), stop=(k == KC - 1),
                     inc=[(t_, 1)] if k == KC - 1 else None)
            tp = t_.n
            c.wait("vector", t_, tp)
            if scale is None:
                c.op("vector", "tensor_copy",
                     out_fm[:, mo, tt * 512:tt * 512 + w_], pp[:, :w_],
                     inc="auto")
            else:
                c.op("vector", "tensor_scalar_mul",
                     out_fm[:, mo, tt * 512:tt * 512 + w_], pp[:, :w_], scale,
                     inc="auto")
            eplan[idx] = v.n
            idx += 1


def proj_tm(c: Ctx, w_sb, xn_fm, out_tm, out_dim, ntiles, pst, col0=0):
    v, t_ = c.s["v"], c.s["t"]
    c.wait_all("tensor")
    idx = 0
    eplan = {}
    for t in range(ntiles):
        for oo in range(out_dim // 512):
            pp = pst[idx % 2]
            if idx >= 2:
                c.wait("tensor", v, eplan[idx - 2])
            for k in range(KC):
                c.op("tensor", "matmul", pp[:],
                     xn_fm[:, k, t * 128:(t + 1) * 128],
                     w_sb[:, k, col0 + oo * 512:col0 + (oo + 1) * 512],
                     start=(k == 0), stop=(k == KC - 1),
                     inc=[(t_, 1)] if k == KC - 1 else None)
            tp = t_.n
            c.wait("vector", t_, tp)
            c.op("vector", "tensor_copy",
                 out_tm[:, t, oo * 512:(oo + 1) * 512], pp[:], inc="auto")
            eplan[idx] = v.n
            idx += 1


def emit_sl5(c: Ctx, cn: Consts, x_in, x_out, wk_d, wq_d, wv_d, T):
    ntiles = T // 128
    v, a, t_, stw, w = (c.s["v"], c.s["a"], c.s["t"], c.s["st"], c.s["w"])
    wst = c.scope()
    wk_sb = c.sb("wk_sb", [128, KC, D], BF16, wst)
    wq_sb = c.sb("wq_sb", [128, KC, D], BF16, wst)
    wv_sb = c.sb("wv_sb", [128, KC, D], BF16, wst)
    w_cast_load(c, wk_d, wk_sb, D, D)
    w_cast_load(c, wq_d, wq_sb, D, D)
    w_cast_load(c, wv_d, wv_sb, D, D)
    c.wait("vector", w)
    for k in range(KC):
        c.op("vector", "tensor_scalar_mul", wq_sb[:, k, :], wq_sb[:, k, :],
             1.0 / 32.0, inc="auto")

    for s in range(BPC):
        row0 = s * T
        sst = c.scope()
        xn_fm = c.sb("xn_fm", [128, KC, T], BF16, sst)
        p1 = contextlib.ExitStack()
        mu_s, rs_s = stats_pass(c, x_in, row0, ntiles, False, p1, p1)
        xn_fm_build(c, x_in, row0, ntiles, mu_s, rs_s, xn_fm, cn, p1)
        p1.close()
        c.barrier()
        k_fm = c.sb("k_fm", [128, KC, T], BF16, sst)
        q_fm = c.sb("q_fm", [128, KC, T], BF16, sst)
        v_tm = c.sb("v_tm", [128, ntiles, D], BF16, sst)
        p2 = contextlib.ExitStack()
        pst = [c.ps(f"prps{i}", [128, 512], F32, p2) for i in range(2)]
        proj_fm(c, wk_sb, xn_fm, k_fm, KC, T, pst)
        proj_fm(c, wq_sb, xn_fm, q_fm, KC, T, pst)
        proj_tm(c, wv_sb, xn_fm, v_tm, D, ntiles, pst)
        p2.close()
        c.barrier()
        p3 = contextlib.ExitStack()
        ps_s = c.ps("ps_s", [128, 2048], F32, p3)
        ps_pt = c.ps("ps_pt", [128, 1024], BF16, p3)
        ps_o = [c.ps(f"ps_o{i}", [128, 512], F32, p3) for i in range(2)]
        p_sb = c.sb("p_sb", [128, 2048], BF16, p3)
        pt_sb = c.sb("pt_sb", [128, 2048], BF16, p3)
        negm = c.sb("negm", [128, 1], F32, p3)
        rsum = c.sb("rsum", [128, 1], F32, p3)
        recip = c.sb("recip", [128, 1], F32, p3)
        xres = c.sb("xres", [128, D], F32, p3)
        xo = c.sb("xo", [128, D], F32, p3)
        c.wait_all("tensor")
        for qi in range(ntiles):
            nk = (qi + 1) * 128
            c.wait("sync", stw)
            xldp = c.dma("sync", xres[:],
                         x_in[row0 + qi * 128: row0 + (qi + 1) * 128, :],
                         sem=c.rot[7])
            for nb in range((nk + 511) // 512):
                w_ = min(512, nk - nb * 512)
                for k in range(KC):
                    c.op("tensor", "matmul", ps_s[:, nb * 512:nb * 512 + w_],
                         q_fm[:, k, qi * 128:(qi + 1) * 128],
                         k_fm[:, k, nb * 512:nb * 512 + w_],
                         start=(k == 0), stop=(k == KC - 1),
                         inc=[(t_, 1)] if k == KC - 1 else None)
            sc_done = t_.n
            c.wait("vector", t_, sc_done)
            c.op("vector", "tensor_tensor", ps_s[:, (nk - 128):nk],
                 ps_s[:, (nk - 128):nk], cn.causal_add[:], ALU.add, inc="auto")
            c.cop("vector", "reduce_max", negm[:], ps_s[:, :nk], AX.X,
                  negate=True)
            mx_done = v.n
            c.wait("scalar", v, mx_done)
            c.op("scalar", "activation", p_sb[:, :nk], ps_s[:, :nk], AF.Exp,
                 bias=negm[:], scale=1.0, accum_out=rsum[:], inc="auto")
            exp_done = a.n
            c.wait("vector", a, exp_done)
            c.op("vector", "reciprocal", recip[:], rsum[:], inc="auto")
            rec_done = v.n
            c.wait("tensor", a, exp_done)
            ptev = None
            for g0 in range(0, qi + 1, 8):
                gn = min(8, qi + 1 - g0)
                if ptev is not None:
                    c.wait("tensor", v, ptev)
                for j in range(gn):
                    kb = g0 + j
                    c.op("tensor", "transpose", ps_pt[:, j * 128:(j + 1) * 128],
                         p_sb[:, kb * 128:(kb + 1) * 128], cn.ident_b[:],
                         inc=[(t_, 1)] if j == gn - 1 else None)
                ptp = t_.n
                c.wait("vector", t_, ptp)
                for j in range(gn):
                    kb = g0 + j
                    c.op("vector", "tensor_copy",
                         pt_sb[:, kb * 128:(kb + 1) * 128],
                         ps_pt[:, j * 128:(j + 1) * 128], inc="auto")
                ptev = v.n
            c.wait("tensor", v, ptev)
            for half in range(2):
                for kb in range(qi + 1):
                    c.op("tensor", "matmul", ps_o[half][:],
                         pt_sb[:, kb * 128:(kb + 1) * 128],
                         v_tm[:, kb, half * 512:(half + 1) * 512],
                         start=(kb == 0), stop=(kb == qi),
                         inc=[(t_, 1)] if kb == qi else None)
            pv_done = t_.n
            c.wait("vector", t_, pv_done)
            c.wait("vector", c.rot[7], xldp)
            c.wait("vector", v, rec_done)
            c.wait("vector", stw)
            for half in range(2):
                c.op("vector", "scalar_tensor_tensor",
                     xo[:, half * 512:(half + 1) * 512], ps_o[half][:],
                     recip[:], xres[:, half * 512:(half + 1) * 512],
                     ALU.mult, ALU.add, inc="auto")
            ev_done = v.n
            c.wait("sync", v, ev_done)
            c.dma("sync", x_out[row0 + qi * 128: row0 + (qi + 1) * 128, :],
                  xo[:], sem=stw)
        p3.close()
        c.barrier()
        sst.close()
    wst.close()


def _finish(c: Ctx):
    c.wait("sync", c.s["st"])
    return c.finish()


def build_sl5(T=2048):
    c = Ctx("sl5")
    cn = Consts(c)
    x_in = c.dram_in("x", [BPC * T, D])
    wk = c.dram_in("Wk", [D, D])
    wq = c.dram_in("Wq", [D, D])
    wv = c.dram_in("Wv", [D, D])
    x_out = c.dram_out("y", [BPC * T, D])
    emit_sl5(c, cn, x_in, x_out, wk, wq, wv, T)
    return _finish(c)


_BUILD_CACHE = {}


def _get_prog(name, builder):
    if name not in _BUILD_CACHE:
        _BUILD_CACHE[name] = builder(T=2048)
    return _BUILD_CACHE[name]


def kernel(**inputs):
    return kernel_fused(**inputs)


def kernel_unfused(**inputs):
    f32 = lambda k: np.ascontiguousarray(np.asarray(inputs[k], dtype=np.float32))
    x = f32("L0_attn_a_x")
    T = 2048
    shards = [np.ascontiguousarray(x[2 * cc:2 * cc + 2].reshape(BPC * T, D))
              for cc in range(NCORES)]
    launches = [
        ("sl1", build_sl1, {"w_qkv": "L0_attn_a_w_qkv",
                            "gamma": "L0_attn_a_gamma",
                            "beta": "L0_attn_a_beta"}),
        ("ffn", build_ffn, {"w1": "L0_ffn_w1", "b1": "L0_ffn_b1",
                            "w2": "L0_ffn_w2", "b2": "L0_ffn_b2",
                            "ln1_g": "L0_ffn_ln1_g", "ln1_b": "L0_ffn_ln1_b",
                            "ln2_g": "L0_ffn_ln2_g", "ln2_b": "L0_ffn_ln2_b"}),
        ("perf", build_perf, {"omega": "L0_attn_b_omega", "wq": "L0_attn_b_wq",
                              "wk": "L0_attn_b_wk", "wv": "L0_attn_b_wv",
                              "wo": "L0_attn_b_wo", "bo": "L0_attn_b_bo"}),
        ("ffn", build_ffn, {"w1": "L0_ffn_w1", "b1": "L0_ffn_b1",
                            "w2": "L0_ffn_w2", "b2": "L0_ffn_b2",
                            "ln1_g": "L0_ffn_ln1_g", "ln1_b": "L0_ffn_ln1_b",
                            "ln2_g": "L0_ffn_ln2_g", "ln2_b": "L0_ffn_ln2_b"}),
        ("sl5", build_sl5, {"Wk": "L1_attn_a_Wk", "Wq": "L1_attn_a_Wq",
                            "Wv": "L1_attn_a_Wv"}),
        ("moe", build_moe, {"w_router": "L1_ffn_w_router",
                            "w_gating": "L1_ffn_w_gating",
                            "w_gate": "L1_ffn_w_gate", "w_up": "L1_ffn_w_up",
                            "w_down": "L1_ffn_w_down"}),
        ("sl7", build_sl7, {"in_proj": "L1_attn_b_in_proj_weight",
                            "out_proj": "L1_attn_b_out_proj_weight"}),
        ("moe", build_moe, {"w_router": "L1_ffn_w_router",
                            "w_gating": "L1_ffn_w_gating",
                            "w_gate": "L1_ffn_w_gate", "w_up": "L1_ffn_w_up",
                            "w_down": "L1_ffn_w_down"}),
    ]
    for name, builder, wmap in launches:
        nc = _get_prog(name, builder)
        wvals = {k: f32(v) for k, v in wmap.items()}
        in_maps = [dict(wvals, x=shards[cc]) for cc in range(NCORES)]
        res = run_bass_kernel_spmd(nc, in_maps, core_ids=list(range(NCORES)))
        shards = [np.ascontiguousarray(np.asarray(res.results[cc]["y"],
                                                  dtype=np.float32))
                  for cc in range(NCORES)]
    return np.concatenate([s.reshape(1, BPC, T, D) for s in shards],
                          axis=0).reshape(16, T, D)


def row_broadcast(c: Ctx, vec_dram, n, bc_sb, cn, stack, rsl=8):
    v, a, t_ = c.s["v"], c.s["a"], c.s["t"]
    row = c.sb("bcrow", [1, n], F32, stack)
    pp = c.ps("bcps", [128, 512], F32, stack)
    c.wait("sync", c.s["t"])
    c.wait("sync", c.s["v"])
    c.wait("sync", c.s["a"])
    ldp = c.dma("sync", row[:], vec_dram[None, :], sem=c.rot[rsl])
    c.wait("tensor", c.rot[rsl], ldp)
    c.wait("tensor", c.s["g"], cn.ready)
    c.wait_all("tensor")
    prev_ev = None
    for hh in range(n // 512):
        if prev_ev is not None:
            c.wait("tensor", v, prev_ev)
        c.op("tensor", "matmul", pp[:], cn.ones_row[:],
             row[:, hh * 512:(hh + 1) * 512], start=True, stop=True,
             inc=[(t_, 1)])
        tp = t_.n
        c.wait("vector", t_, tp)
        c.op("vector", "tensor_copy", bc_sb[:, hh * 512:(hh + 1) * 512], pp[:],
             inc="auto")
        prev_ev = v.n


def attn_mh(c: Ctx, cn: Consts, q_fm, k_fm, v_tm, T, finish_tile, stack):
    v, a, t_ = c.s["v"], c.s["a"], c.s["t"]
    ntiles = T // 128
    ps_s = c.ps("mh_s", [128, 2048], F32, stack)
    ps_pt = c.ps("mh_pt", [128, 2048], BF16, stack)
    ps_out = c.ps("mh_o", [128, 1024], F32, stack)
    p_sb = c.sb("mh_p", [128, 2048], BF16, stack)
    pt_sb = c.sb("mh_ptsb", [128, 2048], BF16, stack)
    negm = c.sb("mh_negm", [128, 1], F32, stack)
    rsum = c.sb("mh_rsum", [128, 1], F32, stack)
    recips = c.sb("mh_recips", [128, 16], F32, stack)
    c.wait_all("tensor")
    fin_prev = None
    for qi in range(ntiles):
        nk = (qi + 1) * 128
        for h in range(16):
            ch, pb = h // 2, 64 * (h % 2)
            if fin_prev is not None:
                c.wait("tensor", v, fin_prev)
                fin_prev = None
            for nb in range((nk + 511) // 512):
                w_ = min(512, nk - nb * 512)
                c.op("tensor", "matmul", ps_s[:, nb * 512:nb * 512 + w_],
                     q_fm[pb:pb + 64, ch, qi * 128:(qi + 1) * 128],
                     k_fm[pb:pb + 64, ch, nb * 512:nb * 512 + w_],
                     start=True, stop=True,
                     tile_position=(pb, 0),
                     inc=[(t_, 1)] if nb == (nk + 511) // 512 - 1 else None)
            sc_done = t_.n
            c.wait("vector", t_, sc_done)
            c.op("vector", "tensor_tensor", ps_s[:, (nk - 128):nk],
                 ps_s[:, (nk - 128):nk], cn.causal_add[:], ALU.add, inc="auto")
            c.cop("vector", "reduce_max", negm[:], ps_s[:, :nk], AX.X,
                  negate=True)
            mx_done = c.s["v"].n
            c.wait("scalar", v, mx_done)
            c.op("scalar", "activation", p_sb[:, :nk], ps_s[:, :nk], AF.Exp,
                 bias=negm[:], scale=1.0, accum_out=rsum[:], inc="auto")
            exp_done = c.s["a"].n
            c.wait("vector", a, exp_done)
            c.cop("vector", "reciprocal", recips[:, h:h + 1], rsum[:])
            c.wait("tensor", a, exp_done)
            ptev = None
            for g0 in range(0, qi + 1, 8):
                gn = min(8, qi + 1 - g0)
                if ptev is not None:
                    c.wait("tensor", v, ptev)
                for j in range(gn):
                    kb = g0 + j
                    c.op("tensor", "transpose",
                         ps_pt[:, j * 128:(j + 1) * 128],
                         p_sb[:, kb * 128:(kb + 1) * 128], cn.ident_b[:],
                         inc=[(t_, 1)] if j == gn - 1 else None)
                ptp = t_.n
                c.wait("vector", t_, ptp)
                for j in range(gn):
                    kb = g0 + j
                    c.op("vector", "tensor_copy",
                         pt_sb[:, kb * 128:(kb + 1) * 128],
                         ps_pt[:, j * 128:(j + 1) * 128], inc="auto")
                ptev = c.s["v"].n
            c.wait("tensor", v, ptev)
            first_of_bank = h % 8 == 0
            last_of_bank = h % 8 == 7
            for kb in range(qi + 1):
                c.op("tensor", "matmul", ps_out[:, 64 * h:64 * h + 64],
                     pt_sb[:, kb * 128:(kb + 1) * 128],
                     v_tm[:, kb, 64 * h:64 * h + 64],
                     start=(first_of_bank and kb == 0),
                     stop=(last_of_bank and kb == qi),
                     inc=[(t_, 1)] if kb == qi else None)
        pv_done = c.s["t"].n
        fin_prev = finish_tile(qi, ps_out, recips, pv_done)
    return fin_prev


def emit_attn16(c: Ctx, cn: Consts, x_in, x_out, wqkv_d, T, mode,
                gamma_d=None, beta_d=None, wo_d=None):
    ntiles = T // 128
    v, a, t_, stw = c.s["v"], c.s["a"], c.s["t"], c.s["st"]
    wst = c.scope()
    wqkvT = c.sb("wqkvT", [128, KC, 3 * D], BF16, wst)
    pwl = contextlib.ExitStack()
    w_t_load(c, wqkv_d, wqkvT, 3 * D, D, cn, pwl, scale=(0, D, 0.125))
    pwl.close()
    gamma_bc = beta_bc = woT = None
    if mode == "sl1":
        gamma_bc = c.sb("gamma_bc", [128, D], F32, wst)
        beta_bc = c.sb("beta_bc", [128, D], F32, wst)
        pbc = contextlib.ExitStack()
        row_broadcast(c, gamma_d, D, gamma_bc, cn, pbc, rsl=8)
        row_broadcast(c, beta_d, D, beta_bc, cn, pbc, rsl=9)
        pbc.close()
    c.barrier()

    for s in range(BPC):
        row0 = s * T
        sst = c.scope()
        xn_fm = c.sb("xn_fm", [128, KC, T], BF16, sst)
        p1 = contextlib.ExitStack()
        mu_s, rs_s = stats_pass(c, x_in, row0, ntiles, mode == "sl1", p1, p1)
        xn_fm_build(c, x_in, row0, ntiles, mu_s, rs_s, xn_fm, cn, p1,
                    gamma_bc=gamma_bc, beta_bc=beta_bc)
        p1.close()
        c.barrier()
        p23 = contextlib.ExitStack()
        q_fm = c.sb("q_fm", [128, KC, T], BF16, p23)
        k_fm = c.sb("k_fm", [128, KC, T], BF16, p23)
        v_tm = c.sb("v_tm", [128, ntiles, D], BF16, p23)
        pps = contextlib.ExitStack()
        pst = [c.ps(f"prps{i}", [128, 512], F32, pps) for i in range(2)]
        proj_fm(c, wqkvT, xn_fm, q_fm, KC, T, pst, col0=0)
        proj_fm(c, wqkvT, xn_fm, k_fm, KC, T, pst, col0=D)
        proj_tm(c, wqkvT, xn_fm, v_tm, D, ntiles, pst, col0=2 * D)
        pps.close()
        c.barrier()
        if mode == "sl1":
            xres = c.sb("xres", [128, D], F32, p23)
            xo = c.sb("xo", [128, D], F32, p23)

            def finish_tile(qi, ps_out, recips, pv_done, rec_done=None):
                c.wait("sync", stw)
                xldp = c.dma("sync", xres[:],
                             x_in[row0 + qi * 128: row0 + (qi + 1) * 128, :],
                             sem=c.rot[7])
                c.wait("vector", t_, pv_done)
                c.wait("vector", c.rot[7], xldp)
                c.wait("vector", stw)
                if rec_done:
                    c.wait("vector", v, rec_done)
                for h in range(16):
                    c.op("vector", "scalar_tensor_tensor",
                         xo[:, 64 * h:64 * h + 64], ps_out[:, 64 * h:64 * h + 64],
                         recips[:, h:h + 1], xres[:, 64 * h:64 * h + 64],
                         ALU.mult, ALU.add, inc="auto")
                ev = v.n
                c.wait("sync", v, ev)
                c.dma("sync", x_out[row0 + qi * 128: row0 + (qi + 1) * 128, :],
                      xo[:], sem=stw)
                return ev

            attn_mh2(c, cn, q_fm, k_fm, v_tm, T, finish_tile, p23)
            p23.close()
            c.barrier()
        else:
            at_tm = c.sb("at_tm", [128, D], BF16, p23)
            at_ps = c.ps("at_ps", [128, 1024], BF16, p23)
            attnout_fm = xn_fm

            def finish_tile(qi, ps_out, recips, pv_done, rec_done=None):
                c.wait("vector", t_, pv_done)
                c.wait("vector", t_)
                if rec_done:
                    c.wait("vector", v, rec_done)
                for h in range(16):
                    c.op("vector", "tensor_scalar_mul",
                         at_tm[:, 64 * h:64 * h + 64],
                         ps_out[:, 64 * h:64 * h + 64],
                         recips[:, h:h + 1], inc="auto")
                ev = v.n
                c.wait("tensor", v, ev)
                c.wait("tensor", a)
                for k in range(KC):
                    c.op("tensor", "transpose", at_ps[:, k * 128:(k + 1) * 128],
                         at_tm[:, k * 128:(k + 1) * 128], cn.ident_b[:],
                         inc=[(t_, 1)] if k == KC - 1 else None)
                tp = t_.n
                c.wait("scalar", t_, tp)
                c.op("scalar", "copy",
                     attnout_fm[:, :, qi * 128:(qi + 1) * 128], at_ps[:],
                     inc="auto")
                return ev

            attn_mh2(c, cn, q_fm, k_fm, v_tm, T, finish_tile, p23,
                     pt_banks=1)
            p23.close()
            c.barrier()
            p4 = contextlib.ExitStack()
            woT = c.sb("woT", [128, KC, D], BF16, p4)
            pwl2 = contextlib.ExitStack()
            w_t_load(c, wo_d, woT, D, D, cn, pwl2)
            pwl2.close()
            c.barrier()
            po = [c.ps(f"po{i}", [128, 512], F32, p4) for i in range(2)]
            xres = c.sb("xres", [128, D], F32, p4)
            xo = c.sb("xo", [128, D], F32, p4)
            c.wait_all("tensor")
            for t in range(ntiles):
                c.wait("sync", stw)
                xldp = c.dma("sync", xres[:],
                             x_in[row0 + t * 128: row0 + (t + 1) * 128, :],
                             sem=c.rot[7])
                for half in range(2):
                    pp = po[half]
                    for k in range(KC):
                        c.op("tensor", "matmul", pp[:],
                             attnout_fm[:, k, t * 128:(t + 1) * 128],
                             woT[:, k, half * 512:(half + 1) * 512],
                             start=(k == 0), stop=(k == KC - 1),
                             inc=[(t_, 1)] if k == KC - 1 else None)
                mmp = t_.n
                c.wait("vector", t_, mmp)
                c.wait("vector", c.rot[7], xldp)
                c.wait("vector", stw)
                for half in range(2):
                    c.op("vector", "tensor_tensor",
                         xo[:, half * 512:(half + 1) * 512], po[half][:],
                         xres[:, half * 512:(half + 1) * 512], ALU.add,
                         inc="auto")
                ev = v.n
                c.wait("sync", v, ev)
                c.dma("sync", x_out[row0 + t * 128: row0 + (t + 1) * 128, :],
                      xo[:], sem=stw)
                c.wait("tensor", v, ev)
            p4.close()
            c.barrier()
        sst.close()
    wst.close()


def build_sl1(T=2048):
    c = Ctx("sl1")
    cn = Consts(c)
    x_in = c.dram_in("x", [BPC * T, D])
    wqkv = c.dram_in("w_qkv", [3 * D, D])
    gamma = c.dram_in("gamma", [D])
    beta = c.dram_in("beta", [D])
    x_out = c.dram_out("y", [BPC * T, D])
    emit_attn16(c, cn, x_in, x_out, wqkv, T, "sl1", gamma_d=gamma, beta_d=beta)
    return _finish(c)


def build_sl7(T=2048):
    c = Ctx("sl7")
    cn = Consts(c)
    x_in = c.dram_in("x", [BPC * T, D])
    wqkv = c.dram_in("in_proj", [3 * D, D])
    wo = c.dram_in("out_proj", [D, D])
    x_out = c.dram_out("y", [BPC * T, D])
    emit_attn16(c, cn, x_in, x_out, wqkv, T, "sl7", wo_d=wo)
    return _finish(c)


def col_load(c: Ctx, vec_dram, n, out_sb, cn, stack, rsl=8):
    v, t_ = c.s["v"], c.s["t"]
    m = n // 128
    row = c.sb("clrow", [m, 128], F32, stack)
    pp = c.ps("clps", [128, 128], F32, stack)
    c.wait("sync", c.s["t"])
    c.wait("sync", c.s["v"])
    c.wait("sync", c.s["a"])
    ldp = c.dma("sync", row[:], vec_dram.rearrange("(m p) -> m p", p=128),
                sem=c.rot[rsl])
    c.wait("tensor", c.rot[rsl], ldp)
    c.wait("tensor", c.s["g"], cn.ready)
    c.wait_all("tensor")
    c.op("tensor", "transpose", pp[:, 0:m], row[:], cn.ident_f[0:m, 0:m],
         inc=[(t_, 1)])
    tp = t_.n
    c.wait("vector", t_, tp)
    c.op("vector", "tensor_copy", out_sb[:], pp[:, 0:m], inc="auto")


def w_quant_t_load(c: Ctx, w_dram, wt_sb, out_dim, in_dim, cn, persist, stack):
    v, a, t_ = c.s["v"], c.s["a"], c.s["t"]
    nrow = out_dim // 128
    nk = in_dim // 128
    ch_bc = c.sb("ch_bc", [128, 1], F32, persist)
    tau_bc = c.sb("tau_bc", [128, 1], F32, stack)
    ntau_bc = c.sb("ntau_bc", [128, 1], F32, stack)
    acc = c.sb("qacc", [128, nrow], F32, stack)
    tot = c.sb("qtot", [1, 1], F32, stack)
    colsum = c.sb("qcolsum", [128, 1], F32, stack)
    bufs = [c.sb(f"qwl{i}", [128, in_dim], F32, stack) for i in range(2)]
    sgn = [c.sb(f"qsg{i}", [128, in_dim], BF16, stack) for i in range(2)]
    qtl = [c.sb(f"qtl{i}", [128, in_dim], BF16, stack) for i in range(2)]
    pst = [c.ps(f"qps{i}", [128, 1024], BF16, stack) for i in range(2)]
    ppf = c.ps("qppf", [128, 512], F32, stack)
    wr = w_dram.rearrange("(k p) f -> k p f", p=128)
    c.wait("tensor", c.s["g"], cn.ready)
    c.wait_all("tensor")
    c.wait("sync", t_)
    c.wait("sync", v)
    c.wait("sync", a)
    vplan = {}
    for r in range(nrow):
        b = bufs[r % 2]
        rsm = c.rot[5 + r % 2]
        if r >= 2:
            c.wait("sync", v, vplan[r - 2])
        ldp = c.dma("sync", b[:], wr[r], sem=rsm)
        c.wait("vector", rsm, ldp)
        c.op("vector", "reduce_sum", acc[:, r:r + 1], b[:], AX.X,
             apply_absolute_value=True, inc="auto")
        vplan[r] = v.n
    c.cop("vector", "reduce_sum", colsum[:], acc[:], AX.X)
    cs_done = v.n
    c.wait("tensor", v, cs_done)
    c.op("tensor", "matmul", ppf[0:1, 0:1], cn.ones_col_f[:], colsum[:],
         start=True, stop=True, inc=[(t_, 1)])
    red_done = t_.n
    c.wait("vector", t_, red_done)
    c.cop("vector", "tensor_scalar_mul", tot[:], ppf[0:1, 0:1],
          1.0 / (out_dim * in_dim))
    c.cop("vector", "tensor_scalar_max", tot[:], tot[:], 1e-5)
    tot_done = v.n
    c.wait("tensor", v, tot_done)
    c.op("tensor", "matmul", ppf[:, 1:2], cn.ones_row[:], tot[:],
         start=True, stop=True, inc=[(t_, 1)])
    bc_done = t_.n
    c.wait("vector", t_, bc_done)
    c.cop("vector", "tensor_scalar_mul", tau_bc[:], ppf[:, 1:2], 0.5)
    c.cop("vector", "tensor_copy", ch_bc[:], tau_bc[:])
    c.cop("vector", "tensor_scalar_mul", ntau_bc[:], tau_bc[:], -1.0)
    tau_done = v.n
    c.wait("scalar", v, tau_done)
    c.wait("sync", v)
    vplan, aplan, tplan, eplan = {}, {}, {}, {}
    for r in range(nrow):
        b, sg, qt, pp = bufs[r % 2], sgn[r % 2], qtl[r % 2], pst[r % 2]
        rsm = c.rot[5 + r % 2]
        if r >= 2:
            c.wait("sync", v, vplan[r - 2])
            c.wait("sync", a, aplan[r - 2])
        ldp = c.dma("sync", b[:], wr[r], sem=rsm)
        c.wait("scalar", rsm, ldp)
        if r >= 2:
            c.wait("scalar", v, vplan[r - 2])
            c.wait("scalar", t_, tplan[r - 2])
        c.op("scalar", "activation", sg[:], b[:], AF.Sign, bias=ntau_bc[:],
             inc="auto")
        c.op("scalar", "activation", qt[:], b[:], AF.Sign, bias=tau_bc[:],
             inc="auto")
        aplan[r] = a.n
        c.wait("vector", a, aplan[r])
        if r >= 2:
            c.wait("vector", t_, tplan[r - 2])
        c.op("vector", "tensor_tensor", qt[:], qt[:], sg[:], ALU.add,
             inc="auto")
        vplan[r] = v.n
        c.wait("tensor", v, vplan[r])
        if r >= 2:
            c.wait("tensor", a, eplan[r - 2])
        gev = None
        for g0 in range(0, nk, 8):
            gn = min(8, nk - g0)
            if gev is not None:
                c.wait("tensor", a, gev)
            for j in range(gn):
                k = g0 + j
                c.op("tensor", "transpose", pp[:, j * 128:(j + 1) * 128],
                     qt[:, k * 128:(k + 1) * 128], cn.ident_b[:],
                     inc=[(t_, 1)] if j == gn - 1 else None)
            tplan[r] = t_.n
            c.wait("scalar", t_, tplan[r])
            c.op("scalar", "copy", wt_sb[:, g0:g0 + gn, r * 128:(r + 1) * 128],
                 pp[:, 0:gn * 128], inc="auto")
            eplan[r] = a.n
            gev = a.n
    return ch_bc


def emit_ffn(c: Ctx, cn: Consts, x_in, x_out, w1_d, b1_d, w2_d, b2_d,
             g1_d, b1g_d, g2_d, b2g_d, T):
    H = 4 * D
    HC = H // 128
    ntiles = BPC * T // 128
    v, a, t_, stw = c.s["v"], c.s["a"], c.s["t"], c.s["st"]
    wst = c.scope()
    w1T = c.sb("w1T", [128, KC, H], BF16, wst)
    pw = contextlib.ExitStack()
    c1h = w_quant_t_load(c, w1_d, w1T, H, D, cn, wst, pw)
    pw.close()
    c.barrier()
    w2T = c.sb("w2T", [128, HC, D], BF16, wst)
    pw2 = contextlib.ExitStack()
    c2h = w_quant_t_load(c, w2_d, w2T, D, H, cn, wst, pw2)
    pw2.close()
    c.barrier()
    g1_bc = c.sb("g1_bc", [128, D], BF16, wst)
    b1g_bc = c.sb("b1g_bc", [128, D], BF16, wst)
    b2_bc = c.sb("b2_bc", [128, D], F32, wst)
    b1_fm = c.sb("b1_fm", [128, HC], F32, wst)
    g2_fm = c.sb("g2_fm", [128, HC], F32, wst)
    b2g_fm = c.sb("b2g_fm", [128, HC], F32, wst)
    pbc = contextlib.ExitStack()
    row_broadcast(c, g1_d, D, g1_bc, cn, pbc, rsl=8)
    row_broadcast(c, b1g_d, D, b1g_bc, cn, pbc, rsl=9)
    row_broadcast(c, b2_d, D, b2_bc, cn, pbc, rsl=10)
    pbc.close()
    pcl = contextlib.ExitStack()
    col_load(c, b1_d, H, b1_fm, cn, pcl, rsl=8)
    col_load(c, g2_d, H, g2_fm, cn, pcl, rsl=9)
    col_load(c, b2g_d, H, b2g_fm, cn, pcl, rsl=10)
    pcl.close()
    c.barrier()

    sst = c.scope()
    p1 = contextlib.ExitStack()
    mu_s, rs_s = stats_pass(c, x_in, 0, ntiles, True, sst, p1)
    p1.close()
    c.barrier()

    lst = contextlib.ExitStack()
    h_bf = c.sb("h_bf", [128, HC, 512], BF16, lst)
    a_fm = c.sb("a_fm", [128, KC, 512], BF16, lst)
    xbuf = [c.sb(f"fx{i}", [128, D], F32, lst) for i in range(2)]
    atm = [c.sb(f"fatm{i}", [128, D], BF16, lst) for i in range(2)]
    hsq = c.sb("hsq", [128, 512], BF16, lst)
    t1s = c.sb("t1s", [128, 512], F32, lst)
    row_mu2 = c.sb("row_mu2", [1, 512], F32, lst)
    row_rs2 = c.sb("row_rs2", [1, 512], F32, lst)
    mu2_bc = c.sb("mu2_bc", [128, 512], F32, lst)
    rs2_bc = c.sb("rs2_bc", [128, 512], F32, lst)
    xres, xo = xbuf[0], xbuf[1]
    pst = [c.ps(f"fps{i}", [128, 512], F32, lst) for i in range(2)]
    ps_tp = c.ps("fptp", [128, 1024], BF16, lst)
    ps_sum = c.ps("fsum", [1, 512], F32, lst)
    ps_sumsq = c.ps("fsumsq", [1, 512], F32, lst)
    ps_bc = c.ps("fbc", [128, 512], F32, lst)

    for tg in range(ntiles // 4):
        c.barrier()
        bdone = {}
        for xt in range(4):
            t = tg * 4 + xt
            b, tm = xbuf[xt % 2], atm[xt % 2]
            rsm = c.rot[xt % 2]
            if xt >= 2:
                c.wait("sync", v, bdone[xt - 2])
            ldp = c.dma("sync", b[:], x_in[t * 128:(t + 1) * 128, :], sem=rsm)
            c.wait("vector", rsm, ldp)
            if xt >= 2:
                c.wait("vector", t_, tpl)
            c.op("vector", "tensor_scalar", tm[:], b[:], mu_s[:, t:t + 1],
                 rs_s[:, t:t + 1], ALU.subtract, ALU.mult, inc="auto")
            c.cop("vector", "tensor_tensor", tm[:], tm[:], g1_bc[:], ALU.mult)
            c.cop("vector", "tensor_tensor", tm[:], tm[:], b1g_bc[:], ALU.add)
            tmp_done = v.n
            bdone[xt] = v.n
            c.wait("tensor", v, tmp_done)
            if xt >= 1:
                c.wait("tensor", a, epl)
            for k in range(KC):
                c.op("tensor", "transpose", ps_tp[:, k * 128:(k + 1) * 128],
                     tm[:, k * 128:(k + 1) * 128], cn.ident_b[:],
                     inc=[(t_, 1)] if k == KC - 1 else None)
            tpl = t_.n
            c.wait("scalar", t_, tpl)
            for k in range(KC):
                c.op("scalar", "copy", a_fm[:, k, xt * 128:(xt + 1) * 128],
                     ps_tp[:, k * 128:(k + 1) * 128], inc="auto")
            epl = a.n
        c.wait_all("tensor")
        mmev = {}
        for mo in range(HC):
            pp = pst[mo % 2]
            if mo >= 2:
                c.wait("tensor", a, mmev[mo - 2])
            for k in range(KC):
                c.op("tensor", "matmul", pp[:],
                     w1T[:, k, mo * 128:(mo + 1) * 128], a_fm[:, k, :],
                     start=(k == 0), stop=(k == KC - 1),
                     inc=[(t_, 1)] if k == KC - 1 else None)
            mmp = t_.n
            c.wait("scalar", t_, mmp)
            c.op("scalar", "activation", h_bf[:, mo, :], pp[:], AF.Gelu,
                 bias=b1_fm[:, mo:mo + 1], scale=c1h[:], inc="auto")
            hev = a.n
            c.wait("scalar", t_)
            c.wait("scalar", a, hev)
            c.op("scalar", "activation", hsq[:], h_bf[:, mo, :], AF.Square,
                 inc="auto")
            sqev = a.n
            mmev[mo] = sqev
            c.wait("tensor", a, sqev)
            c.op("tensor", "matmul", ps_sum[:], cn.ones_col_b[:],
                 h_bf[:, mo, :], start=(mo == 0), stop=(mo == HC - 1))
            c.op("tensor", "matmul", ps_sumsq[:], cn.ones_col_b[:], hsq[:],
                 start=(mo == 0), stop=(mo == HC - 1),
                 inc=[(t_, 1)])
        stat_done = t_.n
        c.wait("vector", t_, stat_done)
        c.cop("vector", "tensor_scalar_mul", row_mu2[:], ps_sum[:], 1.0 / H)
        c.cop("vector", "tensor_scalar_mul", row_rs2[:], ps_sumsq[:], 1.0 / H)
        c.cop("vector", "tensor_tensor", t1s[0:1, :], row_mu2[:], row_mu2[:],
              ALU.mult)
        c.cop("vector", "tensor_tensor", row_rs2[:], row_rs2[:], t1s[0:1, :],
              ALU.subtract)
        c.cop("vector", "tensor_scalar_add", row_rs2[:], row_rs2[:], EPS)
        c.wait("scalar", v)
        c.op("scalar", "activation", row_rs2[:], row_rs2[:], AF.Sqrt,
             inc="auto")
        c.wait("vector", a)
        c.cop("vector", "reciprocal", row_rs2[:], row_rs2[:])
        rows_done = v.n
        c.wait("tensor", v, rows_done)
        c.op("tensor", "matmul", ps_bc[:], cn.ones_row[:], row_mu2[:],
             start=True, stop=True, inc=[(t_, 1)])
        bc1 = t_.n
        c.wait("vector", t_, bc1)
        c.cop("vector", "tensor_copy", mu2_bc[:], ps_bc[:])
        mbc_done = v.n
        c.wait("tensor", v, mbc_done)
        c.op("tensor", "matmul", ps_bc[:], cn.ones_row[:], row_rs2[:],
             start=True, stop=True, inc=[(t_, 1)])
        bc2 = t_.n
        c.wait("vector", t_, bc2)
        c.cop("vector", "tensor_copy", rs2_bc[:], ps_bc[:])
        for mo in range(HC):
            c.cop("vector", "tensor_tensor", t1s[:], h_bf[:, mo, :], mu2_bc[:],
                  ALU.subtract)
            c.cop("vector", "tensor_tensor", t1s[:], t1s[:], rs2_bc[:],
                  ALU.mult)
            c.cop("vector", "tensor_scalar", h_bf[:, mo, :], t1s[:],
                  g2_fm[:, mo:mo + 1], b2g_fm[:, mo:mo + 1], ALU.mult, ALU.add)
        h2_done = v.n
        c.wait_all("tensor")
        c.wait("sync", v)
        c.wait("sync", a)
        c.wait("sync", t_)
        for ts in range(4):
            t = tg * 4 + ts
            c.wait("sync", stw)
            xldp = c.dma("sync", xres[:], x_in[t * 128:(t + 1) * 128, :],
                         sem=c.rot[7])
            for half in range(2):
                pp = pst[half]
                for hf in range(HC):
                    c.op("tensor", "matmul", pp[:],
                         h_bf[:, hf, ts * 128:(ts + 1) * 128],
                         w2T[:, hf, half * 512:(half + 1) * 512],
                         start=(hf == 0), stop=(hf == HC - 1),
                         inc=[(t_, 1)] if hf == HC - 1 else None)
            mm2p = t_.n
            c.wait("vector", t_, mm2p)
            c.wait("vector", c.rot[7], xldp)
            c.wait("vector", stw)
            for half in range(2):
                c.op("vector", "scalar_tensor_tensor",
                     xo[:, half * 512:(half + 1) * 512], pst[half][:],
                     c2h[:], xres[:, half * 512:(half + 1) * 512],
                     ALU.mult, ALU.add, inc="auto")
                c.cop("vector", "tensor_tensor",
                      xo[:, half * 512:(half + 1) * 512],
                      xo[:, half * 512:(half + 1) * 512],
                      b2_bc[:, half * 512:(half + 1) * 512], ALU.add)
            ev = v.n
            c.wait("sync", v, ev)
            c.dma("sync", x_out[t * 128:(t + 1) * 128, :], xo[:], sem=stw)
            c.wait("tensor", v, ev)
    lst.close()
    sst.close()
    wst.close()
    c.barrier()


def build_ffn(T=2048):
    c = Ctx("ffn")
    cn = Consts(c)
    x_in = c.dram_in("x", [BPC * T, D])
    w1 = c.dram_in("w1", [4 * D, D])
    b1 = c.dram_in("b1", [4 * D])
    w2 = c.dram_in("w2", [D, 4 * D])
    b2 = c.dram_in("b2", [D])
    g1 = c.dram_in("ln1_g", [D])
    b1g = c.dram_in("ln1_b", [D])
    g2 = c.dram_in("ln2_g", [4 * D])
    b2g = c.dram_in("ln2_b", [4 * D])
    x_out = c.dram_out("y", [BPC * T, D])
    emit_ffn(c, cn, x_in, x_out, w1, b1, w2, b2, g1, b1g, g2, b2g, T)
    return _finish(c)


def emit_moe(c: Ctx, cn: Consts, x_in, x_out, wr_d, wg_d, wgate_d, wup_d,
             wdown_d, T):
    NE, H = 8, 4 * D
    ntiles = T // 128
    v, a, t_, stw = c.s["v"], c.s["a"], c.s["t"], c.s["st"]
    wst = c.scope()
    wrg = c.sb("wrg", [128, KC, 16], F32, wst)
    sw_bc = c.sb("sw_bc", [128, 16], F32, wst)
    sel = c.sb("sel", [8, 1024], F32, wst)
    c.wait("sync", t_)
    c.wait("sync", v)
    c.wait("sync", a)
    c.dma("sync", wrg[:, :, 0:8],
          wr_d.rearrange("(k p) e -> p k e", p=128), sem=c.rot[8])
    ldp = c.dma("sync", wrg[:, :, 8:16],
                wg_d.rearrange("(k p) e -> p k e", p=128), sem=c.rot[8])
    c.op("gpsimd", "memset", sel[:], 1.0, inc="auto")
    c.cop("gpsimd", "affine_select", out=sel[:], in_=sel[:],
          compare_op=ALU.is_ge, fill=0.0, base=0,
          pattern=[[1, 1024]], channel_multiplier=-128)
    c.cop("gpsimd", "affine_select", out=sel[:], in_=sel[:],
          compare_op=ALU.is_ge, fill=0.0, base=127,
          pattern=[[-1, 1024]], channel_multiplier=128)
    selr = c.s["g"].n
    psw = contextlib.ExitStack()
    ppA = c.ps("ppA", [128, 512], F32, psw)
    swcol = c.sb("swcol", [16, 1], F32, psw)
    swrow = c.sb("swrow", [1, 16], F32, psw)
    c.wait("tensor", c.rot[8], ldp)
    c.wait("tensor", c.s["g"], cn.ready)
    c.wait_all("tensor")
    for k in range(KC):
        c.op("tensor", "matmul", ppA[0:16, 0:1], wrg[:, k, :],
             cn.ones_col_f[:], start=(k == 0), stop=(k == KC - 1),
             inc=[(t_, 1)] if k == KC - 1 else None)
    c.wait("scalar", t_)
    c.op("scalar", "copy", swcol[:], ppA[0:16, 0:1], inc="auto")
    c.wait("tensor", a)
    c.op("tensor", "transpose", ppA[0:1, 64:80], swcol[:],
         cn.ident_f[0:16, 0:16], inc=[(t_, 1)])
    c.wait("scalar", t_)
    c.op("scalar", "copy", swrow[:], ppA[0:1, 64:80], inc="auto")
    c.wait("tensor", a)
    c.op("tensor", "matmul", ppA[:, 128:144], cn.ones_row[:], swrow[:],
         start=True, stop=True, inc=[(t_, 1)])
    c.wait("vector", t_)
    c.op("vector", "tensor_copy", sw_bc[:], ppA[:, 128:144], inc="auto")
    psw.close()
    c.barrier()

    for s in range(BPC):
        row0 = s * T
        sst = c.scope()
        x_tm = c.sb("x_tm", [128, ntiles, D], F32, sst)
        xn_fm = c.sb("xn_fm", [128, KC, T], BF16, sst)
        cw_fm = c.sb("cw_fm", [8, T], F32, sst)
        mu_s = c.sb("mu_s", [128, ntiles], F32, sst)
        rs_s = c.sb("rs_s", [128, ntiles], F32, sst)
        nr1_s = c.sb("nr1_s", [128, ntiles], F32, sst)
        p1 = contextlib.ExitStack()
        var_s = c.sb("var_s", [128, ntiles], F32, p1)
        tmp_s = c.sb("tmp_s", [128, ntiles], F32, p1)
        junks = [c.sb(f"mjunk{i}", [128, D], BF16, p1) for i in range(2)]
        vplan, aplan = {}, {}
        for t in range(ntiles):
            rsm = c.rot[t % 2]
            if t >= 2:
                c.wait("sync", v, vplan[t - 2])
                c.wait("sync", a, aplan[t - 2])
            ldp = c.dma("sync", x_tm[:, t, :],
                        x_in[row0 + t * 128: row0 + (t + 1) * 128, :], sem=rsm)
            c.wait("vector", rsm, ldp)
            c.op("vector", "reduce_sum", mu_s[:, t:t + 1], x_tm[:, t, :],
                 AX.X, inc="auto")
            vplan[t] = v.n
            c.wait("scalar", rsm, ldp)
            if t >= 2:
                c.wait("scalar", a, aplan[t - 2])
            c.op("scalar", "activation", junks[t % 2][:], x_tm[:, t, :],
                 AF.Square, accum_out=var_s[:, t:t + 1], inc="auto")
            aplan[t] = a.n
        c.wait("vector", a)
        c.cop("vector", "tensor_scalar_mul", mu_s[:], mu_s[:], 1.0 / D)
        c.cop("vector", "tensor_scalar_mul", var_s[:], var_s[:], 1.0 / D)
        c.cop("vector", "tensor_tensor", tmp_s[:], mu_s[:], mu_s[:], ALU.mult)
        c.cop("vector", "tensor_tensor", var_s[:], var_s[:], tmp_s[:],
              ALU.subtract)
        c.cop("vector", "tensor_scalar_add", tmp_s[:], var_s[:], EPS)
        c.wait("scalar", v)
        c.op("scalar", "activation", tmp_s[:], tmp_s[:], AF.Sqrt, inc="auto")
        c.wait("vector", a)
        c.cop("vector", "reciprocal", rs_s[:], tmp_s[:])
        c.cop("vector", "tensor_scalar_mul", nr1_s[:], rs_s[:], -1.0)
        tmb = [c.sb(f"mxntm{i}", [128, D], BF16, p1) for i in range(2)]
        pst2 = [c.ps(f"mxnps{i}", [128, 1024], BF16, p1) for i in range(2)]
        c.wait("tensor", c.s["g"], cn.ready)
        c.wait("vector", v)
        tplan, eplan = {}, {}
        for t in range(ntiles):
            tm, pp = tmb[t % 2], pst2[t % 2]
            if t >= 2:
                c.wait("vector", t_, tplan[t - 2])
            c.op("vector", "tensor_scalar", tm[:], x_tm[:, t, :],
                 mu_s[:, t:t + 1], rs_s[:, t:t + 1], ALU.subtract, ALU.mult,
                 inc="auto")
            vp = v.n
            c.wait("tensor", v, vp)
            if t >= 2:
                c.wait("tensor", a, eplan[t - 2])
            for k in range(KC):
                c.op("tensor", "transpose", pp[:, k * 128:(k + 1) * 128],
                     tm[:, k * 128:(k + 1) * 128], cn.ident_b[:],
                     inc=[(t_, 1)] if k == KC - 1 else None)
            tplan[t] = t_.n
            c.wait("scalar", t_, tplan[t])
            c.op("scalar", "copy", xn_fm[:, :, t * 128:(t + 1) * 128], pp[:],
                 inc="auto")
            eplan[t] = a.n
        p1.close()
        c.barrier()
        p2 = contextlib.ExitStack()
        pxf = c.ps("pxf", [128, 1024], F32, p2)
        pl16 = c.ps("pl16", [128, 512], F32, p2)
        pcw = c.ps("pcw", [128, 512], F32, p2)
        xf_sb = c.sb("xf_sb", [128, KC, 128], F32, p2)
        lg = c.sb("lg", [128, 16], F32, p2)
        lg10 = c.sb("lg10", [128, 16], F32, p2)
        nm8 = c.sb("nm8", [128, 1], F32, p2)
        rs8 = c.sb("rs8", [128, 1], F32, p2)
        cw_tm = c.sb("cw_tm", [128, 8], F32, p2)
        c.wait_all("tensor")
        prev_lg = None
        prev_cw = None
        for t in range(ntiles):
            prev_ev = None
            for g0 in range(0, KC, 4):
                if prev_ev is not None:
                    c.wait("tensor", v, prev_ev)
                for j in range(4):
                    k = g0 + j
                    c.op("tensor", "transpose", pxf[:, j * 128:(j + 1) * 128],
                         x_tm[:, t, k * 128:(k + 1) * 128], cn.ident_f[:],
                         inc=[(t_, 1)] if j == 3 else None)
                tp = t_.n
                c.wait("vector", t_, tp)
                for j in range(4):
                    k = g0 + j
                    c.op("vector", "tensor_copy", xf_sb[:, k, :],
                         pxf[:, j * 128:(j + 1) * 128], inc="auto")
                prev_ev = v.n
            c.wait("tensor", v, prev_ev)
            if prev_lg is not None:
                c.wait("tensor", v, prev_lg)
            for k in range(KC):
                c.op("tensor", "matmul", pl16[:, 0:16], xf_sb[:, k, :],
                     wrg[:, k, :], start=(k == 0), stop=(k == KC - 1),
                     inc=[(t_, 1)] if k == KC - 1 else None)
            lgp = t_.n
            c.wait("vector", t_, lgp)
            c.cop("vector", "scalar_tensor_tensor", lg[:], sw_bc[:],
                  mu_s[:, t:t + 1], pl16[:, 0:16], ALU.mult, ALU.subtract)
            prev_lg = v.n
            c.cop("vector", "tensor_scalar_mul", lg[:], lg[:],
                  nr1_s[:, t:t + 1])
            c.cop("vector", "reduce_max", nm8[:], lg[:, 0:8], AX.X,
                  negate=True)
            c.wait("scalar", v)
            c.op("scalar", "activation", lg[:, 0:8], lg[:, 0:8], AF.Exp,
                 bias=nm8[:], accum_out=rs8[:], inc="auto")
            c.wait("vector", a)
            c.cop("vector", "reciprocal", rs8[:], rs8[:])
            c.cop("vector", "tensor_scalar_mul", lg[:, 0:8], lg[:, 0:8],
                  rs8[:])
            c.cop("vector", "tensor_scalar_mul", lg10[:, 8:16], lg[:, 8:16],
                  10.0)
            c.cop("vector", "reduce_max", nm8[:], lg10[:, 8:16], AX.X,
                  negate=True)
            c.wait("scalar", v)
            c.op("scalar", "activation", lg10[:, 8:16], lg10[:, 8:16], AF.Exp,
                 bias=nm8[:], accum_out=rs8[:], inc="auto")
            c.wait("vector", a)
            c.cop("vector", "reciprocal", rs8[:], rs8[:])
            c.cop("vector", "tensor_scalar_mul", lg10[:, 8:16],
                  lg10[:, 8:16], rs8[:])
            c.cop("vector", "tensor_tensor", cw_tm[:], lg[:, 0:8],
                  lg10[:, 8:16], ALU.mult)
            c.cop("vector", "reduce_sum", rs8[:], cw_tm[:], AX.X)
            c.cop("vector", "tensor_scalar_add", rs8[:], rs8[:], 1e-9)
            c.cop("vector", "reciprocal", rs8[:], rs8[:])
            c.cop("vector", "tensor_scalar_mul", cw_tm[:], cw_tm[:], rs8[:])
            cw_done = v.n
            c.wait("tensor", v, cw_done)
            if prev_cw is not None:
                c.wait("tensor", a, prev_cw)
            c.op("tensor", "transpose", pcw[0:8, 0:128], cw_tm[:],
                 cn.ident_f[:], inc=[(t_, 1)])
            ctp = t_.n
            c.wait("scalar", t_, ctp)
            c.op("scalar", "copy", cw_fm[:, t * 128:(t + 1) * 128],
                 pcw[0:8, 0:128], inc="auto")
            prev_cw = c.s["a"].n
        p2.close()
        c.barrier()
        p3 = contextlib.ExitStack()
        wgs = [c.sb(f"wgs{i}", [128, KC, 512], BF16, p3) for i in range(2)]
        wus = [c.sb(f"wus{i}", [128, KC, 512], BF16, p3) for i in range(2)]
        wds = [c.sb(f"wds{i}", [128, 4, D], BF16, p3) for i in range(2)]
        h_sb = c.sb("h_sb", [128, 4, T], BF16, p3)
        cw_bc = c.sb("cw_bc", [128, T], BF16, p3)
        sg_sb = c.sb("sg_sb", [128, 512], BF16, p3)
        uc_sb = c.sb("uc_sb", [128, 512], F32, p3)
        ps_g = [c.ps(f"mpg{i}", [128, 512], F32, p3) for i in range(2)]
        ps_u = [c.ps(f"mpu{i}", [128, 512], F32, p3) for i in range(2)]
        ps_d = [c.ps(f"mpd{i}", [128, 512], F32, p3) for i in range(2)]
        ps_cw = c.ps("mpcw", [128, 512], F32, p3)
        c.wait("gpsimd", c.s["g"], selr)
        gidx = 0
        dwplan = {}
        for e in range(NE):
            c.wait_all("tensor")
            prev_ev = None
            for tt in range(T // 512):
                if prev_ev is not None:
                    c.wait("tensor", v, prev_ev)
                c.op("tensor", "matmul", ps_cw[:],
                     sel[:, e * 128:(e + 1) * 128],
                     cw_fm[:, tt * 512:(tt + 1) * 512],
                     start=True, stop=True, inc=[(t_, 1)])
                tp = t_.n
                c.wait("vector", t_, tp)
                c.op("vector", "tensor_copy",
                     cw_bc[:, tt * 512:(tt + 1) * 512], ps_cw[:], inc="auto")
                prev_ev = v.n
            cwb_done = v.n
            for grp in range(8):
                par = gidx % 2
                rsm = c.rot[10 + par]
                wg, wu, wd = wgs[par], wus[par], wds[par]
                if gidx >= 2:
                    c.wait("gpsimd", t_, dwplan[gidx - 2])
                c.dma("gpsimd", wg[:],
                      wgate_d[e].rearrange("(k p) f -> p k f", p=128)
                      [:, :, grp * 512:(grp + 1) * 512], sem=rsm)
                c.dma("gpsimd", wu[:],
                      wup_d[e].rearrange("(k p) f -> p k f", p=128)
                      [:, :, grp * 512:(grp + 1) * 512], sem=rsm)
                wldp = c.dma("gpsimd", wd[:],
                             wdown_d[e][grp * 512:(grp + 1) * 512, :]
                             .rearrange("(k p) f -> p k f", p=128), sem=rsm)
                c.wait("tensor", rsm, wldp)
                c.wait("tensor", v, cwb_done)
                hplan = {}
                sidx = 0
                for mp in range(4):
                    for tt in range(T // 512):
                        pg, pu = ps_g[sidx % 2], ps_u[sidx % 2]
                        if sidx >= 2:
                            c.wait("tensor", v, hplan[sidx - 2])
                        for k in range(KC):
                            c.op("tensor", "matmul", pg[:],
                                 wg[:, k, mp * 128:(mp + 1) * 128],
                                 xn_fm[:, k, tt * 512:(tt + 1) * 512],
                                 start=(k == 0), stop=(k == KC - 1),
                                 inc=[(t_, 1)] if k == KC - 1 else None)
                        gp = t_.n
                        for k in range(KC):
                            c.op("tensor", "matmul", pu[:],
                                 wu[:, k, mp * 128:(mp + 1) * 128],
                                 xn_fm[:, k, tt * 512:(tt + 1) * 512],
                                 start=(k == 0), stop=(k == KC - 1),
                                 inc=[(t_, 1)] if k == KC - 1 else None)
                        up = t_.n
                        c.wait("scalar", t_, gp)
                        c.wait("scalar", a)
                        c.wait("scalar", v)
                        c.op("scalar", "activation", sg_sb[:], pg[:], AF.Silu,
                             inc="auto")
                        sgp = c.s["a"].n
                        c.wait("vector", t_, up)
                        c.wait("vector", v)
                        c.op("vector", "tensor_tensor", uc_sb[:], pu[:],
                             cw_bc[:, tt * 512:(tt + 1) * 512], ALU.mult,
                             inc="auto")
                        c.wait("vector", a, sgp)
                        c.cop("vector", "tensor_tensor",
                              h_sb[:, mp, tt * 512:(tt + 1) * 512], uc_sb[:],
                              sg_sb[:], ALU.mult)
                        hplan[sidx] = v.n
                        sidx += 1
                h_done = v.n
                c.wait("tensor", v, h_done)
                dplan = {}
                didx = 0
                for ts in range(ntiles):
                    for half in range(2):
                        pd = ps_d[didx % 2]
                        if didx >= 2:
                            c.wait("tensor", v, dplan[didx - 2])
                        for hf in range(4):
                            c.op("tensor", "matmul", pd[:],
                                 h_sb[:, hf, ts * 128:(ts + 1) * 128],
                                 wd[:, hf, half * 512:(half + 1) * 512],
                                 start=(hf == 0), stop=(hf == 3),
                                 inc=[(t_, 1)] if hf == 3 else None)
                        dp = t_.n
                        c.wait("vector", t_, dp)
                        c.op("vector", "tensor_tensor",
                             x_tm[:, ts, half * 512:(half + 1) * 512],
                             x_tm[:, ts, half * 512:(half + 1) * 512],
                             pd[:], ALU.add, inc="auto")
                        dplan[didx] = v.n
                        didx += 1
                dwplan[gidx] = t_.n
                gidx += 1
        p3.close()
        c.barrier()
        for t in range(ntiles):
            c.dma("sync", x_out[row0 + t * 128:row0 + (t + 1) * 128, :],
                  x_tm[:, t, :], sem=stw)
        c.barrier()
        sst.close()
    wst.close()


def build_moe(T=2048):
    c = Ctx("moe")
    cn = Consts(c)
    x_in = c.dram_in("x", [BPC * T, D])
    wr = c.dram_in("w_router", [D, 8])
    wg = c.dram_in("w_gating", [D, 8])
    wgate = c.dram_in("w_gate", [8, D, 4 * D])
    wup = c.dram_in("w_up", [8, D, 4 * D])
    wdown = c.dram_in("w_down", [8, 4 * D, D])
    x_out = c.dram_out("y", [BPC * T, D])
    emit_moe(c, cn, x_in, x_out, wr, wg, wgate, wup, wdown, T)
    return _finish(c)


PERF_STOP = None


def _pcol(h):
    return (h % 2) * 512 + (h // 2) * 64


def emit_perf(c: Ctx, cn: Consts, x_in, x_out, omega_d, wq_d, wk_d, wv_d,
              wo_d, bo_d, T):
    ntiles = T // 128
    NH = 16
    LN8 = float(np.log(8.0))
    v, a, t_, stw = c.s["v"], c.s["a"], c.s["t"], c.s["st"]
    wst = c.scope()
    wqT = c.sb("wqT", [128, KC, D], BF16, wst)
    wkT = c.sb("wkT", [128, KC, D], BF16, wst)
    wvT = c.sb("wvT", [128, KC, D], BF16, wst)
    woT = c.sb("woT", [128, KC, D], BF16, wst)
    omg2 = c.sb("omg2", [128, 64], BF16, wst)
    mln8 = c.sb("mln8", [128, 1], F32, wst)
    for wd_, wt_ in ((wq_d, wqT), (wk_d, wkT), (wv_d, wvT), (wo_d, woT)):
        pwl = contextlib.ExitStack()
        w_t_load(c, wd_, wt_, D, D, cn, pwl)
        pwl.close()
    c.op("gpsimd", "memset", mln8[:], -LN8, inc="auto")
    pom = contextlib.ExitStack()
    om_f = c.sb("om_f", [64, 64], F32, pom)
    om_b = c.sb("om_b", [64, 64], BF16, pom)
    pot = c.ps("pot", [128, 128], BF16, pom)
    c.wait("sync", t_)
    c.wait("sync", v)
    c.wait("sync", a)
    ldp = c.dma("sync", om_f[:], omega_d[:], sem=c.rot[8])
    c.wait("vector", c.rot[8], ldp)
    c.cop("vector", "tensor_copy", om_b[:], om_f[:])
    omc = v.n
    c.wait("tensor", v, omc)
    c.wait("tensor", c.s["g"], cn.ready)
    c.wait_all("tensor")
    c.op("tensor", "transpose", pot[0:64, 0:64], om_b[:],
         cn.ident_b[0:64, 0:64], inc=[(t_, 1)])
    otp = t_.n
    c.wait("scalar", t_, otp)
    c.op("scalar", "copy", omg2[0:64, :], pot[0:64, 0:64], inc="auto")
    c.op("scalar", "copy", omg2[64:128, :], pot[0:64, 0:64], inc="auto")
    pom.close()
    c.barrier()
    if PERF_STOP == "w":
        return

    for s in range(BPC):
        row0 = s * T
        sst = c.scope()
        xn_fm = c.sb("xn_fm", [128, KC, T], BF16, sst)
        p1 = contextlib.ExitStack()
        mu_s, rs_s = stats_pass(c, x_in, row0, ntiles, False, p1, p1)
        xn_fm_build(c, x_in, row0, ntiles, mu_s, rs_s, xn_fm, cn, p1)
        p1.close()
        c.barrier()
        if PERF_STOP == "p1":
            sst.close()
            continue
        p3 = contextlib.ExitStack()
        ps_dl = [c.ps(f"psdl{i}", [128, 128], F32, p3) for i in range(2)]
        ps_phi = c.ps("ps_phi", [128, 1024], F32, p3)
        ps_ad = c.ps("ps_ad", [128, 512], F32, p3)
        ps_out = c.ps("ps_out", [128, 1024], F32, p3)
        ps_att = c.ps("ps_att", [128, 1024], BF16, p3)
        attnout_fm = xn_fm
        k_tm = c.sb("k_tm", [128, D], BF16, p3)
        v_tm = c.sb("v_tm", [128, D], BF16, p3)
        q_tm = c.sb("q_tm", [128, D], BF16, p3)
        kfm = c.sb("kfm", [128, KC, 128], BF16, p3)
        qfm = c.sb("qfm", [128, KC, 128], BF16, p3)
        phq = c.sb("phq", [128, KC, 128], BF16, p3)
        phqT = c.sb("phqT", [128, NH * 64], BF16, p3)
        phkT = c.sb("phkT", [128, NH * 64], BF16, p3)
        phkF = c.sb("phkF", [128, KC, 128], BF16, p3)
        at_sb = c.sb("at_sb", [128, 128], BF16, p3)
        S_sb = c.sb("S_sb", [128, KC, 65], F32, p3)
        S_bf = c.sb("S_bf", [128, KC, 65], BF16, p3)
        nrmk = c.sb("nrmk", [128, NH], F32, p3)
        kbias = c.sb("kbias", [128, NH], F32, p3)
        nrmq = c.sb("nrmq", [128, NH], F32, p3)
        qbias = c.sb("qbias", [128, NH], F32, p3)
        ebk = c.sb("ebk", [128, NH], F32, p3)
        ebq = c.sb("ebq", [128, NH], F32, p3)
        njunk = c.sb("njunk", [128, D], BF16, p3)
        den_rows = c.sb("den_rows", [1, NH * 128], F32, p3)
        den_tm = c.sb("den_tm", [16, 128], F32, p3)
        recip_c = c.sb("recip_c", [128, 16], F32, p3)
        at_tm = c.sb("at_tm", [128, D], BF16, p3)
        c.wait_all("tensor")
        c.wait_all("scalar")
        c.wait_all("vector")
        c.op("vector", "memset", S_sb[:], 0.0, inc="auto")
        prev_rtp = None
        nchunks = 1 if PERF_STOP == "chunk0" else ntiles
        for i in range(nchunks):
            kvev = []
            for di, (dst, wslab) in enumerate(((k_tm, wkT), (v_tm, wvT),
                                               (q_tm, wqT))):
                for oo in range(2):
                    pp = ps_out[:, (oo * 512):(oo * 512 + 512)]
                    for k in range(KC):
                        c.op("tensor", "matmul", pp,
                             xn_fm[:, k, i * 128:(i + 1) * 128],
                             wslab[:, k, oo * 512:(oo + 1) * 512],
                             start=(k == 0), stop=(k == KC - 1),
                             inc=[(t_, 1)] if k == KC - 1 else None)
                    tp = t_.n
                    c.wait("vector", t_, tp)
                    c.op("vector", "tensor_copy",
                         dst[:, oo * 512:(oo + 1) * 512], pp, inc="auto")
                    kvev.append(v.n)
                c.wait("tensor", v, kvev[-1])
            kv_done = v.n
            if PERF_STOP == "kv":
                break
            c.wait("scalar", v, kv_done)
            c.wait("scalar", a)
            c.op("scalar", "activation", njunk[:], k_tm[:], AF.Square,
                 inc="auto")
            sqk = c.s["a"].n
            c.wait("vector", a, sqk)
            c.cop("vector", "reduce_sum", nrmk[:],
                  njunk[:].rearrange("p (h d) -> p h d", h=NH), AX.X)
            c.wait("scalar", v)
            c.op("scalar", "activation", njunk[:], q_tm[:], AF.Square,
                 inc="auto")
            sqq = c.s["a"].n
            c.wait("vector", a, sqq)
            c.cop("vector", "reduce_sum", nrmq[:],
                  njunk[:].rearrange("p (h d) -> p h d", h=NH), AX.X)
            c.cop("vector", "tensor_scalar", kbias[:], nrmk[:], -0.5, -LN8,
                  ALU.mult, ALU.add)
            c.cop("vector", "tensor_scalar", qbias[:], nrmq[:], -0.5, -LN8,
                  ALU.mult, ALU.add)
            kb_done = v.n
            c.wait("scalar", v, kb_done)
            c.wait("scalar", a)
            c.op("scalar", "activation", ebk[:],
                 kbias[:].rearrange("p (c two) -> p two c", two=2), AF.Exp,
                 inc="auto")
            c.wait("scalar", a)
            c.op("scalar", "activation", ebq[:],
                 qbias[:].rearrange("p (c two) -> p two c", two=2), AF.Exp,
                 inc="auto")
            eb_done = c.s["a"].n
            if PERF_STOP == "nrm":
                break
            c.wait("tensor", v, kv_done)
            for src_tm, dst_fm, ee in ((k_tm, kfm, "scalar"),
                                       (q_tm, qfm, "vector")):
                c.wait("tensor", a)
                c.wait("tensor", v)
                for h in range(NH):
                    ch, pb = h // 2, 64 * (h % 2)
                    c.op("tensor", "transpose",
                         ps_att[pb:pb + 64, ch * 128:(ch + 1) * 128],
                         src_tm[:, h * 64:(h + 1) * 64], cn.ident_b[:],
                         tile_position=(0, pb), inc=[(t_, 1)])
                tp_ = t_.n
                c.wait(ee, t_, tp_)
                meth = "copy" if ee == "scalar" else "tensor_copy"
                c.op(ee, meth, dst_fm[:], ps_att[:].rearrange(
                    "p (ch f) -> p ch f", ch=KC), inc="auto")
            kfm_done = c.s["a"].n
            kfm_done_v = v.n
            if PERF_STOP == "tfm":
                break
            c.wait("tensor", c.s["a"], kfm_done)
            c.wait("tensor", v, kfm_done_v)
            for h in range(NH):
                ch, pb = h // 2, 64 * (h % 2)
                c.op("tensor", "matmul", ps_phi[:, _pcol(h):_pcol(h) + 64],
                     qfm[pb:pb + 64, ch, :], omg2[pb:pb + 64, :],
                     start=True, stop=True,
                     tile_position=(pb, 0),
                     inc=[(t_, 1)] if h == NH - 1 else None)
            phqT_p = t_.n
            if PERF_STOP == "phqmm":
                break
            c.wait("scalar", t_, phqT_p)
            c.wait("scalar", a, eb_done)
            c.op("scalar", "activation", phqT[:], ps_phi[:], AF.Exp,
                 inc="auto")
            phqT_e = c.s["a"].n
            c.wait("vector", a, phqT_e)
            from concourse.bass import broadcast_tensor_aps as _bta
            _a1, _a2 = _bta(phqT[:].rearrange("p (j d) -> p j d", d=64),
                            ebq[:].rearrange("p (j o) -> p j o", o=1))
            c.cop("vector", "tensor_tensor", _a1, _a1, _a2, ALU.mult)
            phqT_done = v.n
            c.wait("tensor", v, phqT_done)
            for h in range(NH):
                ch, pb = h // 2, 64 * (h % 2)
                c.op("tensor", "transpose",
                     ps_att[pb:pb + 64, ch * 128:(ch + 1) * 128],
                     phqT[:, _pcol(h):_pcol(h) + 64], cn.ident_b[:],
                     tile_position=(0, pb), inc=[(t_, 1)])
            phq_tp = t_.n
            c.wait("scalar", t_, phq_tp)
            c.op("scalar", "copy", phq[:],
                 ps_att[:].rearrange("p (ch f) -> p ch f", ch=KC), inc="auto")
            phq_done = c.s["a"].n
            if PERF_STOP == "phq":
                break
            c.wait("tensor", c.s["a"], phq_done)
            for h in range(NH):
                ch, pb = h // 2, 64 * (h % 2)
                c.op("tensor", "matmul", ps_phi[:, _pcol(h):_pcol(h) + 64],
                     kfm[pb:pb + 64, ch, :], omg2[pb:pb + 64, :],
                     start=True, stop=True,
                     tile_position=(pb, 0),
                     inc=[(t_, 1)] if h == NH - 1 else None)
            phk_p = t_.n
            c.wait("scalar", t_, phk_p)
            c.op("scalar", "activation", phkT[:], ps_phi[:], AF.Exp,
                 inc="auto")
            phkT_e = c.s["a"].n
            c.wait("vector", a, phkT_e)
            _b1, _b2 = _bta(phkT[:].rearrange("p (j d) -> p j d", d=64),
                            ebk[:].rearrange("p (j o) -> p j o", o=1))
            c.cop("vector", "tensor_tensor", _b1, _b1, _b2, ALU.mult)
            phkT_done = v.n
            if PERF_STOP == "phk":
                break
            c.wait("tensor", v, phkT_done)
            for h in range(NH):
                ch, pb = h // 2, 64 * (h % 2)
                c.op("tensor", "transpose",
                     ps_att[pb:pb + 64, ch * 128:(ch + 1) * 128],
                     phkT[:, _pcol(h):_pcol(h) + 64], cn.ident_b[:],
                     tile_position=(0, pb), inc=[(t_, 1)])
            phf_tp = t_.n
            c.wait("scalar", t_, phf_tp)
            c.op("scalar", "copy", phkF[:],
                 ps_att[:].rearrange("p (ch f) -> p ch f", ch=KC), inc="auto")
            phf_done = c.s["a"].n
            if PERF_STOP == "phf":
                break
            if i > 0:
                c.wait("scalar", c.s["v"])
                c.op("scalar", "copy", S_bf[:], S_sb[:], inc="auto")
            sbf_done = c.s["a"].n
            c.wait_all("tensor")
            prev_den_ev = None
            sadd_pl = {}
            for h in range(NH):
                ch, pb = h // 2, 64 * (h % 2)
                if prev_den_ev is not None:
                    c.wait("tensor", a, prev_den_ev)
                c.op("tensor", "matmul", ps_ad[:, 0:128],
                     phkF[pb:pb + 64, ch, :], phq[pb:pb + 64, ch, :],
                     start=True, stop=True, tile_position=(pb, 0),
                     inc=[(t_, 1)])
                at_p = t_.n
                c.wait("vector", t_, at_p)
                c.wait("vector", a)
                c.op("vector", "tensor_tensor", at_sb[:], ps_ad[:, 0:128],
                     cn.tri_keep[:], ALU.mult, inc="auto")
                at_ev = v.n
                c.wait("tensor", v, at_ev)
                first_of_bank = h % 8 == 0
                last_of_bank = h % 8 == 7
                nmm = 2 if i > 0 else 1
                c.op("tensor", "matmul", ps_out[:, h * 64:(h + 1) * 64],
                     at_sb[:], v_tm[:, h * 64:(h + 1) * 64],
                     start=first_of_bank, stop=(last_of_bank and nmm == 1))
                if i > 0:
                    c.op("tensor", "matmul", ps_out[:, h * 64:(h + 1) * 64],
                         phq[pb:pb + 64, ch, :], S_bf[pb:pb + 64, ch, 0:64],
                         start=False, stop=last_of_bank,
                         tile_position=(pb, 0))
                c.op("tensor", "matmul", ps_ad[0:1, 128:256], cn.ones_col_b[:],
                     at_sb[:], start=True, stop=(i == 0))
                if i > 0:
                    c.op("tensor", "matmul", ps_ad[0:1, 128:256],
                         S_bf[pb:pb + 64, ch, 64:65], phq[pb:pb + 64, ch, :],
                         start=False, stop=True, tile_position=(pb, 0))
                pdl = ps_dl[h % 2]
                if h >= 2:
                    c.wait("tensor", v, sadd_pl[h - 2])
                c.op("tensor", "matmul", pdl[pb:pb + 64, 0:64],
                     phkT[:, _pcol(h):_pcol(h) + 64],
                     v_tm[:, h * 64:(h + 1) * 64],
                     start=True, stop=False, tile_position=(0, pb))
                c.op("tensor", "matmul", pdl[pb:pb + 64, 64:65],
                     phkT[:, _pcol(h):_pcol(h) + 64], cn.ones_col_b[:],
                     start=False, stop=True, tile_position=(0, pb),
                     inc=[(t_, 1)])
                upd_p = t_.n
                c.wait("vector", t_, upd_p)
                c.op("vector", "tensor_tensor", S_sb[pb:pb + 64, ch, :],
                     S_sb[pb:pb + 64, ch, :], pdl[pb:pb + 64, 0:65], ALU.add,
                     inc="auto")
                sadd_pl[h] = v.n
                c.wait("scalar", t_, upd_p)
                if i > 0:
                    c.wait("scalar", a, sbf_done)
                c.op("scalar", "copy", den_rows[0:1, h * 128:(h + 1) * 128],
                     ps_ad[0:1, 128:256], inc="auto")
                prev_den_ev = c.s["a"].n
            heads_done = t_.n
            if PERF_STOP == "heads":
                break
            c.wait("gpsimd", a, prev_den_ev)
            if prev_rtp is not None:
                c.wait("gpsimd", t_, prev_rtp)
            dldp = c.dma("gpsimd", den_tm[:],
                         den_rows.rearrange("o (h t) -> o h t", h=16),
                         sem=c.rot[9])
            c.wait("tensor", c.rot[9], dldp)
            c.op("tensor", "transpose", ps_ad[:, 256:272], den_tm[:],
                 cn.ident_f[0:16, 0:16], inc=[(t_, 1)])
            rtp = t_.n
            prev_rtp = rtp
            c.wait("vector", t_, rtp)
            c.cop("vector", "tensor_scalar_add", recip_c[:], ps_ad[:, 256:272],
                  1e-6)
            c.cop("vector", "reciprocal", recip_c[:], recip_c[:])
            rc_done = v.n
            if PERF_STOP == "recip":
                break
            c.wait("vector", t_, heads_done)
            c.wait("vector", v, rc_done)
            for h in range(NH):
                c.op("vector", "tensor_scalar_mul", at_tm[:, h * 64:(h + 1) * 64],
                     ps_out[:, h * 64:(h + 1) * 64], recip_c[:, h:h + 1],
                     inc="auto")
            att_ev = v.n
            c.wait("tensor", v, att_ev)
            c.wait("tensor", a, phf_done)
            for k in range(KC):
                c.op("tensor", "transpose", ps_att[:, k * 128:(k + 1) * 128],
                     at_tm[:, k * 128:(k + 1) * 128], cn.ident_b[:],
                     inc=[(t_, 1)] if k == KC - 1 else None)
            aot = t_.n
            c.wait("scalar", t_, aot)
            c.op("scalar", "copy", attnout_fm[:, :, i * 128:(i + 1) * 128],
                 ps_att[:], inc="auto")
        p3.close()
        c.barrier()
        if PERF_STOP in ("chunk0", "scan"):
            sst.close()
            continue
        p4 = contextlib.ExitStack()
        po = [c.ps(f"po{i}", [128, 512], F32, p4) for i in range(2)]
        bo_bc = c.sb("bo_bc", [128, D], F32, p4)
        xres = c.sb("xres", [128, D], F32, p4)
        xo = c.sb("xo", [128, D], F32, p4)
        row_broadcast(c, bo_d, D, bo_bc, cn, p4, rsl=8)
        c.wait_all("tensor")
        for t in range(ntiles):
            c.wait("sync", stw)
            xldp = c.dma("sync", xres[:],
                         x_in[row0 + t * 128: row0 + (t + 1) * 128, :],
                         sem=c.rot[7])
            for half in range(2):
                pp = po[half]
                for k in range(KC):
                    c.op("tensor", "matmul", pp[:],
                         attnout_fm[:, k, t * 128:(t + 1) * 128],
                         woT[:, k, half * 512:(half + 1) * 512],
                         start=(k == 0), stop=(k == KC - 1),
                         inc=[(t_, 1)] if k == KC - 1 else None)
            mmp = t_.n
            c.wait("vector", t_, mmp)
            c.wait("vector", c.rot[7], xldp)
            c.wait("vector", stw)
            for half in range(2):
                c.op("vector", "tensor_tensor",
                     xo[:, half * 512:(half + 1) * 512], po[half][:],
                     xres[:, half * 512:(half + 1) * 512], ALU.add, inc="auto")
                c.cop("vector", "tensor_tensor",
                      xo[:, half * 512:(half + 1) * 512],
                      xo[:, half * 512:(half + 1) * 512],
                      bo_bc[:, half * 512:(half + 1) * 512], ALU.add)
            ev = v.n
            c.wait("sync", v, ev)
            c.dma("sync", x_out[row0 + t * 128: row0 + (t + 1) * 128, :],
                  xo[:], sem=stw)
            c.wait("tensor", v, ev)
        p4.close()
        c.barrier()
        sst.close()
    wst.close()


def build_perf(T=2048):
    c = Ctx("perf")
    cn = Consts(c)
    x_in = c.dram_in("x", [BPC * T, D])
    omega = c.dram_in("omega", [64, 64])
    wq = c.dram_in("wq", [D, D])
    wk = c.dram_in("wk", [D, D])
    wv = c.dram_in("wv", [D, D])
    wo = c.dram_in("wo", [D, D])
    bo = c.dram_in("bo", [D])
    x_out = c.dram_out("y", [BPC * T, D])
    emit_perf(c, cn, x_in, x_out, omega, wq, wk, wv, wo, bo, T)
    return _finish(c)


def build_full(T=2048):
    c = Ctx("full")
    cn = Consts(c)
    x_in = c.dram_in("x", [BPC * T, D])
    w = {}
    w["w_qkv"] = c.dram_in("w_qkv", [3 * D, D])
    w["gamma"] = c.dram_in("gamma", [D])
    w["beta"] = c.dram_in("beta", [D])
    w["w1"] = c.dram_in("w1", [4 * D, D])
    w["b1"] = c.dram_in("b1", [4 * D])
    w["w2"] = c.dram_in("w2", [D, 4 * D])
    w["b2"] = c.dram_in("b2", [D])
    w["ln1_g"] = c.dram_in("ln1_g", [D])
    w["ln1_b"] = c.dram_in("ln1_b", [D])
    w["ln2_g"] = c.dram_in("ln2_g", [4 * D])
    w["ln2_b"] = c.dram_in("ln2_b", [4 * D])
    w["omega"] = c.dram_in("omega", [64, 64])
    w["wq"] = c.dram_in("wq", [D, D])
    w["wk"] = c.dram_in("wk", [D, D])
    w["wv"] = c.dram_in("wv", [D, D])
    w["wo"] = c.dram_in("wo", [D, D])
    w["bo"] = c.dram_in("bo", [D])
    w["Wk"] = c.dram_in("Wk", [D, D])
    w["Wq"] = c.dram_in("Wq", [D, D])
    w["Wv"] = c.dram_in("Wv", [D, D])
    w["w_router"] = c.dram_in("w_router", [D, 8])
    w["w_gating"] = c.dram_in("w_gating", [D, 8])
    w["w_gate"] = c.dram_in("w_gate", [8, D, 4 * D])
    w["w_up"] = c.dram_in("w_up", [8, D, 4 * D])
    w["w_down"] = c.dram_in("w_down", [8, 4 * D, D])
    w["in_proj"] = c.dram_in("in_proj", [3 * D, D])
    w["out_proj"] = c.dram_in("out_proj", [D, D])
    y = c.dram_out("y", [BPC * T, D])
    xA = c.dram_tmp("xA", [BPC * T, D])
    xB = c.dram_tmp("xB", [BPC * T, D])

    emit_attn16(c, cn, x_in, xA, w["w_qkv"], T, "sl1",
                gamma_d=w["gamma"], beta_d=w["beta"])
    c.barrier()
    emit_ffn2(c, cn, xA, xB, w["w1"], w["b1"], w["w2"], w["b2"],
              w["ln1_g"], w["ln1_b"], w["ln2_g"], w["ln2_b"], T)
    c.barrier()
    emit_perf(c, cn, xB, xA, w["omega"], w["wq"], w["wk"], w["wv"],
              w["wo"], w["bo"], T)
    c.barrier()
    emit_ffn2(c, cn, xA, xB, w["w1"], w["b1"], w["w2"], w["b2"],
              w["ln1_g"], w["ln1_b"], w["ln2_g"], w["ln2_b"], T)
    c.barrier()
    emit_sl5(c, cn, xB, xA, w["Wk"], w["Wq"], w["Wv"], T)
    c.barrier()
    emit_moe(c, cn, xA, xB, w["w_router"], w["w_gating"], w["w_gate"],
             w["w_up"], w["w_down"], T)
    c.barrier()
    emit_attn16(c, cn, xB, xA, w["in_proj"], T, "sl7", wo_d=w["out_proj"])
    c.barrier()
    emit_moe(c, cn, xA, y, w["w_router"], w["w_gating"], w["w_gate"],
             w["w_up"], w["w_down"], T)
    return _finish(c)


_FUSED_CACHE = {}


def kernel_fused(**inputs):
    f32 = lambda k: np.ascontiguousarray(np.asarray(inputs[k], dtype=np.float32))
    x = f32("L0_attn_a_x")
    T = 2048
    shards = [np.ascontiguousarray(x[2 * cc:2 * cc + 2].reshape(BPC * T, D))
              for cc in range(NCORES)]
    if "nc" not in _FUSED_CACHE:
        _FUSED_CACHE["nc"] = build_full(T=T)
    nc = _FUSED_CACHE["nc"]
    names = {"w_qkv": "L0_attn_a_w_qkv", "gamma": "L0_attn_a_gamma",
             "beta": "L0_attn_a_beta",
             "w1": "L0_ffn_w1", "b1": "L0_ffn_b1", "w2": "L0_ffn_w2",
             "b2": "L0_ffn_b2", "ln1_g": "L0_ffn_ln1_g",
             "ln1_b": "L0_ffn_ln1_b", "ln2_g": "L0_ffn_ln2_g",
             "ln2_b": "L0_ffn_ln2_b",
             "omega": "L0_attn_b_omega", "wq": "L0_attn_b_wq",
             "wk": "L0_attn_b_wk", "wv": "L0_attn_b_wv",
             "wo": "L0_attn_b_wo", "bo": "L0_attn_b_bo",
             "Wk": "L1_attn_a_Wk", "Wq": "L1_attn_a_Wq", "Wv": "L1_attn_a_Wv",
             "w_router": "L1_ffn_w_router", "w_gating": "L1_ffn_w_gating",
             "w_gate": "L1_ffn_w_gate", "w_up": "L1_ffn_w_up",
             "w_down": "L1_ffn_w_down",
             "in_proj": "L1_attn_b_in_proj_weight",
             "out_proj": "L1_attn_b_out_proj_weight"}
    wvals = {k: f32(v) for k, v in names.items()}
    in_maps = [dict(wvals, x=shards[cc]) for cc in range(NCORES)]
    res = run_bass_kernel_spmd(nc, in_maps, core_ids=list(range(NCORES)))
    outs = [np.ascontiguousarray(np.asarray(res.results[cc]["y"],
                                            dtype=np.float32))
            for cc in range(NCORES)]
    return np.concatenate([o.reshape(1, BPC, T, D) for o in outs],
                          axis=0).reshape(16, T, D)


def emit_ffn2(c: Ctx, cn: Consts, x_in, x_out, w1_d, b1_d, w2_d, b2_d,
              g1_d, b1g_d, g2_d, b2g_d, T):
    H = 4 * D
    HC = H // 128
    ntiles = BPC * T // 128
    v, a, t_, stw = c.s["v"], c.s["a"], c.s["t"], c.s["st"]
    wst = c.scope()
    w1T = c.sb("w1T", [128, KC, H], BF16, wst)
    pw = contextlib.ExitStack()
    c1h = w_quant_t_load(c, w1_d, w1T, H, D, cn, wst, pw)
    pw.close()
    c.barrier()
    w2T = c.sb("w2T", [128, HC, D], BF16, wst)
    pw2 = contextlib.ExitStack()
    c2h = w_quant_t_load(c, w2_d, w2T, D, H, cn, wst, pw2)
    pw2.close()
    c.barrier()
    g1_bc = c.sb("g1_bc", [128, D], BF16, wst)
    b1g_bc = c.sb("b1g_bc", [128, D], BF16, wst)
    b1_bc = c.sb("b1_bc", [128, H], BF16, wst)
    combo_bc = c.sb("combo_bc", [128, D], F32, wst)
    g2_fm = c.sb("g2_fm", [128, HC], F32, wst)
    b2g_fm = c.sb("b2g_fm", [128, HC], F32, wst)
    pbc = contextlib.ExitStack()
    row_broadcast(c, g1_d, D, g1_bc, cn, pbc, rsl=8)
    row_broadcast(c, b1g_d, D, b1g_bc, cn, pbc, rsl=9)
    row_broadcast(c, b1_d, H, b1_bc, cn, pbc, rsl=10)
    pbc.close()
    pcl = contextlib.ExitStack()
    col_load(c, g2_d, H, g2_fm, cn, pcl, rsl=8)
    col_load(c, b2g_d, H, b2g_fm, cn, pcl, rsl=9)
    pcl.close()
    c.barrier()
    pp2 = contextlib.ExitStack()
    psb = c.ps("psb", [128, 512], F32, pp2)
    bw2_row = c.sb("bw2_row", [1, D], F32, pp2)
    b2_row = c.sb("b2_row", [1, D], F32, pp2)
    b2g_b = c.sb("b2g_b", [128, HC], BF16, pp2)
    c.cop("vector", "tensor_copy", b2g_b[:], b2g_fm[:])
    c.wait("sync", t_)
    c.wait("sync", v)
    c.wait("sync", a)
    ldb2 = c.dma("sync", b2_row[:], b2_d[None, :], sem=c.rot[8])
    c.wait_all("tensor")
    c.wait("tensor", v)
    for half in range(2):
        for hf in range(HC):
            c.op("tensor", "matmul", psb[0:1, 0:512],
                 b2g_b[:, hf:hf + 1], w2T[:, hf, half * 512:(half + 1) * 512],
                 start=(hf == 0), stop=(hf == HC - 1),
                 inc=[(t_, 1)] if hf == HC - 1 else None)
        tp = t_.n
        c.wait("vector", t_, tp)
        c.op("vector", "tensor_copy", bw2_row[0:1, half * 512:(half + 1) * 512],
             psb[0:1, 0:512], inc="auto")
        c.wait("tensor", v)
    for hf in range(HC):
        c.op("vector", "tensor_scalar_mul", w2T[:, hf, :], w2T[:, hf, :],
             g2_fm[:, hf:hf + 1], inc="auto")
    c.wait("vector", c.rot[8], ldb2)
    c.cop("vector", "tensor_scalar_mul", bw2_row[:], bw2_row[:], c2h[0:1, :])
    c.cop("vector", "tensor_tensor", bw2_row[:], bw2_row[:], b2_row[:],
          ALU.add)
    bwp = v.n
    c.wait("tensor", v, bwp)
    prev = None
    for half in range(2):
        if prev is not None:
            c.wait("tensor", v, prev)
        c.op("tensor", "matmul", psb[:, 0:512], cn.ones_row[:],
             bw2_row[0:1, half * 512:(half + 1) * 512], start=True, stop=True,
             inc=[(t_, 1)])
        tp = t_.n
        c.wait("vector", t_, tp)
        c.op("vector", "tensor_copy", combo_bc[:, half * 512:(half + 1) * 512],
             psb[:, 0:512], inc="auto")
        prev = v.n
    pp2.close()
    c.barrier()

    sst = c.scope()
    p1 = contextlib.ExitStack()
    mu_s, rs_s = stats_pass(c, x_in, 0, ntiles, True, sst, p1)
    p1.close()
    c.barrier()

    lst = contextlib.ExitStack()
    a_fm = c.sb("a_fm", [128, KC, 128], BF16, lst)
    h1_tm = [c.sb(f"h1tm{i}", [128, H], BF16, lst) for i in range(2)]
    h2_fm = c.sb("h2_fm", [128, HC, 128], BF16, lst)
    xbuf = [c.sb(f"fx{i}", [128, D], F32, lst) for i in range(2)]
    atm = c.sb("fatm", [128, D], BF16, lst)
    hjunk = c.sb("hjunk", [128, H], BF16, lst)
    s1 = c.sb("s1", [128, 1], F32, lst)
    s2 = c.sb("s2", [128, 1], F32, lst)
    mu2 = c.sb("mu2c", [128, 1], F32, lst)
    rs2 = c.sb("rs2c", [128, 1], F32, lst)
    xo = c.sb("fxo", [128, D], F32, lst)
    pst = [c.ps(f"fps{i}", [128, 512], F32, lst) for i in range(2)]
    ps_tp = [c.ps(f"ftp{i}", [128, 1024], BF16, lst) for i in range(2)]
    c.barrier()
    mm2ev = [0, 0]
    prev_afm = None
    prev_h2fm = None
    prev_mm1 = 0
    resid_v = {}
    h2tp_slot = {}
    for t in range(ntiles):
        b = xbuf[t % 2]
        h1 = h1_tm[t % 2]
        rsm = c.rot[t % 2]
        if t >= 2:
            c.wait("sync", v, resid_v[t - 2])
        ldp = c.dma("sync", b[:], x_in[t * 128:(t + 1) * 128, :], sem=rsm)
        c.wait("vector", rsm, ldp)
        if t >= 1:
            c.wait("vector", t_, prev_atp)
        c.op("vector", "tensor_scalar", atm[:], b[:], mu_s[:, t:t + 1],
             rs_s[:, t:t + 1], ALU.subtract, ALU.mult, inc="auto")
        c.cop("vector", "tensor_tensor", atm[:], atm[:], g1_bc[:], ALU.mult)
        c.cop("vector", "tensor_tensor", atm[:], atm[:], b1g_bc[:], ALU.add)
        a_done = v.n
        c.wait("tensor", v, a_done)
        if t >= 1:
            c.wait("tensor", a, prev_afm)
        for k in range(KC):
            c.op("tensor", "transpose", ps_tp[0][:, k * 128:(k + 1) * 128],
                 atm[:, k * 128:(k + 1) * 128], cn.ident_b[:],
                 inc=[(t_, 1)] if k == KC - 1 else None)
        prev_atp = t_.n
        c.wait("scalar", t_, prev_atp)
        if t >= 1:
            c.wait("scalar", v, prev_mm1)
        c.op("scalar", "copy", a_fm[:], ps_tp[0][:], inc="auto")
        prev_afm = a.n
        c.wait("tensor", a, prev_afm)
        mm1ev = {}
        mm1ev_v = {}
        for hf8 in range(8):
            pp = pst[hf8 % 2]
            if hf8 >= 2:
                c.wait("tensor", v, mm1ev_v[hf8 - 2])
            elif t >= 1:
                c.wait("tensor", v, mm2ev[hf8])
            for k in range(KC):
                c.op("tensor", "matmul", pp[:],
                     a_fm[:, k, :], w1T[:, k, hf8 * 512:(hf8 + 1) * 512],
                     start=(k == 0), stop=(k == KC - 1),
                     inc=[(t_, 1)] if k == KC - 1 else None)
            mmp = t_.n
            c.wait("vector", t_, mmp)
            if t >= 2 and hf8 == 0:
                c.wait("vector", a)
                c.wait("vector", t_, h2tp_slot[t % 2])
            c.op("vector", "scalar_tensor_tensor",
                 h1[:, hf8 * 512:(hf8 + 1) * 512], pp[:], c1h[:],
                 b1_bc[:, hf8 * 512:(hf8 + 1) * 512], ALU.mult, ALU.add,
                 inc="auto")
            sttp = v.n
            mm1ev_v[hf8] = sttp
            c.wait("scalar", v, sttp)
            c.op("scalar", "activation", h1[:, hf8 * 512:(hf8 + 1) * 512],
                 h1[:, hf8 * 512:(hf8 + 1) * 512], AF.Gelu, inc="auto")
            mm1ev[hf8] = c.s["a"].n
        prev_mm1 = v.n
        gelu_done = c.s["a"].n
        c.wait("vector", a, gelu_done)
        c.cop("vector", "reduce_sum", s1[:], h1[:], AX.X)
        c.wait("scalar", a, gelu_done)
        c.wait("scalar", v)
        c.op("scalar", "activation", hjunk[:], h1[:], AF.Square,
             accum_out=s2[:], inc="auto")
        sq_done = a.n
        c.wait("vector", a, sq_done)
        c.cop("vector", "tensor_scalar_mul", mu2[:], s1[:], 1.0 / H)
        c.cop("vector", "tensor_scalar_mul", s2[:], s2[:], 1.0 / H)
        c.cop("vector", "tensor_tensor", rs2[:], mu2[:], mu2[:], ALU.mult)
        c.cop("vector", "tensor_tensor", rs2[:], s2[:], rs2[:], ALU.subtract)
        c.cop("vector", "tensor_scalar_add", rs2[:], rs2[:], EPS)
        c.wait("scalar", v)
        c.op("scalar", "activation", rs2[:], rs2[:], AF.Sqrt, inc="auto")
        c.wait("vector", a)
        c.cop("vector", "reciprocal", rs2[:], rs2[:])
        c.cop("vector", "tensor_scalar", h1[:], h1[:], mu2[:], rs2[:],
              ALU.subtract, ALU.mult)
        h2_done = v.n
        c.wait("tensor", v, h2_done)
        if t >= 1:
            c.wait("tensor", v, prev_h2fm)
            c.wait("tensor", a, prev_h2fm_a)
        for g0 in range(0, HC, 8):
            if g0 > 0:
                c.wait("tensor", v, h2ev_v)
                c.wait("tensor", a, h2ev_a)
            for j in range(8):
                hf = g0 + j
                c.op("tensor", "transpose", ps_tp[1][:, j * 128:(j + 1) * 128],
                     h1[:, hf * 128:(hf + 1) * 128], cn.ident_b[:],
                     inc=[(t_, 1)] if j == 7 else None)
            h2tp = t_.n
            c.wait("scalar", t_, h2tp)
            c.op("scalar", "copy", h2_fm[:, g0:g0 + 8, :], ps_tp[1][:],
                 inc="auto")
            h2ev_v = v.n
            h2ev_a = c.s["a"].n
        h2tp_slot[t % 2] = t_.n
        prev_h2fm = v.n
        prev_h2fm_a = c.s["a"].n
        c.wait("tensor", v, prev_h2fm)
        c.wait("tensor", a, prev_h2fm_a)
        c.wait("sync", stw)
        c.wait("sync", v, a_done)
        xldp = c.dma("sync", b[:], x_in[t * 128:(t + 1) * 128, :],
                     sem=c.rot[2 + t % 2])
        for half in range(2):
            pp = pst[half]
            c.wait("tensor", v, mm2ev[half])
            for hf in range(HC):
                c.op("tensor", "matmul", pp[:],
                     h2_fm[:, hf, :], w2T[:, hf, half * 512:(half + 1) * 512],
                     start=(hf == 0), stop=(hf == HC - 1),
                     inc=[(t_, 1)] if hf == HC - 1 else None)
            mm2p = t_.n
            c.wait("vector", t_, mm2p)
            c.wait("vector", c.rot[2 + t % 2], xldp)
            c.wait("vector", stw)
            c.op("vector", "scalar_tensor_tensor",
                 xo[:, half * 512:(half + 1) * 512], pp[:], c2h[:],
                 b[:, half * 512:(half + 1) * 512], ALU.mult, ALU.add,
                 inc="auto")
            c.cop("vector", "tensor_tensor",
                  xo[:, half * 512:(half + 1) * 512],
                  xo[:, half * 512:(half + 1) * 512],
                  combo_bc[:, half * 512:(half + 1) * 512], ALU.add)
            mm2ev[half] = v.n
        ev = v.n
        resid_v[t] = v.n
        c.wait("sync", v, ev)
        c.dma("sync", x_out[t * 128:(t + 1) * 128, :], xo[:], sem=stw)
    lst.close()
    sst.close()
    wst.close()
    c.barrier()


def build_ffn2(T=2048):
    c = Ctx("ffn2")
    cn = Consts(c)
    x_in = c.dram_in("x", [BPC * T, D])
    w1 = c.dram_in("w1", [4 * D, D])
    b1 = c.dram_in("b1", [4 * D])
    w2 = c.dram_in("w2", [D, 4 * D])
    b2 = c.dram_in("b2", [D])
    g1 = c.dram_in("ln1_g", [D])
    b1g = c.dram_in("ln1_b", [D])
    g2 = c.dram_in("ln2_g", [4 * D])
    b2g = c.dram_in("ln2_b", [4 * D])
    x_out = c.dram_out("y", [BPC * T, D])
    emit_ffn2(c, cn, x_in, x_out, w1, b1, w2, b2, g1, b1g, g2, b2g, T)
    return _finish(c)


def attn_mh2(c: Ctx, cn: Consts, q_fm, k_fm, v_tm, T, finish_tile, stack,
             pt_banks=2):
    v, a, t_ = c.s["v"], c.s["a"], c.s["t"]
    ntiles = T // 128
    ps_s = c.ps("mh_s", [128, 2048], F32, stack)
    ps_pt = c.ps("mh_pt", [128, 1024 * pt_banks], BF16, stack)
    ps_out = c.ps("mh_o", [128, 1024], F32, stack)
    p_sb = [c.sb(f"mh_p{i}", [128, 2048], BF16, stack) for i in range(2)]
    pt_sb = c.sb("mh_ptsb", [128, 2048], BF16, stack)
    negm = c.sb("mh_negm", [128, 2], F32, stack)
    pmax = c.sb("mh_pmax", [128, 8], F32, stack)
    rsums = c.sb("mh_rsums", [128, 16], F32, stack)
    recips = c.sb("mh_recips", [128, 16], F32, stack)
    c.wait_all("tensor")
    fin_prev = None
    exp_done = {}
    mx_done = {}
    ptp = {}
    ptev = {}
    pvp = {}

    def softmax_dve(qi, h, sc_chunks):
        nk = (qi + 1) * 128
        nbs = len(sc_chunks)
        par = h % 2
        if h >= 2:
            c.wait("vector", a, exp_done[h - 2])
        for nb in range(nbs):
            w_ = min(512, nk - nb * 512)
            c.wait("vector", t_, sc_chunks[nb])
            if nb == nbs - 1:
                c.op("vector", "tensor_tensor", ps_s[:, (nk - 128):nk],
                     ps_s[:, (nk - 128):nk], cn.causal_add[:], ALU.add,
                     inc="auto")
                c.wait("vector", v)
            c.op("vector", "reduce_max", pmax[:, par * 4 + nb:par * 4 + nb + 1],
                 ps_s[:, nb * 512:nb * 512 + w_], AX.X, inc="auto")
        c.cop("vector", "reduce_max", negm[:, par:par + 1],
              pmax[:, par * 4:par * 4 + nbs], AX.X, negate=True)
        mx_done[h] = c.s["v"].n

    def softmax_act(qi, h):
        nk = (qi + 1) * 128
        par = h % 2
        c.wait("scalar", v, mx_done[h])
        if h >= 2:
            c.wait("scalar", t_, ptp.get(h - 2, 0))
        if rec_prev[0]:
            c.wait("scalar", v, rec_prev[0])
        c.op("scalar", "activation", p_sb[par][:, :nk], ps_s[:, :nk], AF.Exp,
             bias=negm[:, par:par + 1], scale=1.0,
             accum_out=rsums[:, h:h + 1], inc="auto")
        exp_done[h] = c.s["a"].n

    def stage_ptpv(qi, h):
        nk1 = qi + 1
        par = h % 2
        c.wait("tensor", a, exp_done[h])
        for bi, g0 in enumerate(range(0, nk1, 8)):
            gn = min(8, nk1 - g0)
            bank = (pt_bank[0] + bi) % pt_banks
            ee = "vector" if bank == 0 else "scalar"
            es = v if bank == 0 else a
            if ptev_bank[bank]:
                c.wait("tensor", es, ptev_bank[bank])
            for j in range(gn):
                kb = g0 + j
                c.op("tensor", "transpose",
                     ps_pt[:, bank * 1024 + j * 128:bank * 1024 + (j + 1) * 128],
                     p_sb[par][:, kb * 128:(kb + 1) * 128], cn.ident_b[:],
                     inc=[(t_, 1)] if j == gn - 1 else None)
            ptp[h] = t_.n
            c.wait(ee, t_, ptp[h])
            meth = "tensor_copy" if ee == "vector" else "copy"
            c.op(ee, meth, pt_sb[:, g0 * 128:(g0 + gn) * 128],
                 ps_pt[:, bank * 1024:bank * 1024 + gn * 128], inc="auto")
            ptev_bank[bank] = es.n
        pt_bank[0] = (pt_bank[0] + (nk1 + 7) // 8) % pt_banks
        c.wait("tensor", v, ptev_bank[0])
        c.wait("tensor", a, ptev_bank[1])
        first_of_bank = h % 8 == 0
        last_of_bank = h % 8 == 7
        for kb in range(nk1):
            c.op("tensor", "matmul", ps_out[:, 64 * h:64 * h + 64],
                 pt_sb[:, kb * 128:(kb + 1) * 128],
                 v_tm[:, kb, 64 * h:64 * h + 64],
                 start=(first_of_bank and kb == 0),
                 stop=(last_of_bank and kb == qi),
                 inc=[(t_, 1)] if kb == qi else None)
        pvp[h] = t_.n

    rec_prev = [0]
    ptev_bank = [0, 0]
    pt_bank = [0]
    for qi in range(ntiles):
        nk = (qi + 1) * 128
        sc_done = {}
        for h in range(16):
            ch, pb = h // 2, 64 * (h % 2)
            if h >= 1:
                c.wait("tensor", a, exp_done[h - 1])
            elif qi > 0:
                c.wait("tensor", a, exp_done[15])
            if fin_prev is not None:
                c.wait("tensor", v, fin_prev)
                fin_prev = None
            sc_chunks = []
            for nb in range((nk + 511) // 512):
                w_ = min(512, nk - nb * 512)
                c.op("tensor", "matmul", ps_s[:, nb * 512:nb * 512 + w_],
                     q_fm[pb:pb + 64, ch, qi * 128:(qi + 1) * 128],
                     k_fm[pb:pb + 64, ch, nb * 512:nb * 512 + w_],
                     start=True, stop=True, tile_position=(pb, 0),
                     inc=[(t_, 1)])
                sc_chunks.append(t_.n)
            sc_done[h] = t_.n
            softmax_dve(qi, h, sc_chunks)
            if h >= 1:
                stage_ptpv(qi, h - 1)
            softmax_act(qi, h)
        stage_ptpv(qi, 15)
        c.wait("vector", a, exp_done[15])
        c.cop("vector", "reciprocal", recips[:], rsums[:])
        rec_prev[0] = c.s["v"].n
        fin_prev = finish_tile(qi, ps_out, recips, pvp[15],
                               rec_done=rec_prev[0])
    return fin_prev


# version 13
# speedup vs baseline: 1.0917x; 1.0917x over previous
import contextlib
import numpy as np

import concourse.bass as bass
import concourse.mybir as mybir
from concourse.bass_utils import run_bass_kernel_spmd

F32 = mybir.dt.float32
BF16 = mybir.dt.bfloat16
AF = mybir.ActivationFunctionType
ALU = mybir.AluOpType
AX = mybir.AxisListType

ENGINES = ("sync", "vector", "scalar", "tensor", "gpsimd")
ESEM = {"vector": "v", "scalar": "a", "tensor": "t", "gpsimd": "g"}
NEG = -3.0e38
EPS = 1e-5
D = 1024
KC = 8
NCORES = 8
BPC = 2


class Sem:
    __slots__ = ("h", "n", "name")

    def __init__(self, h, name):
        self.h = h
        self.n = 0
        self.name = name


class Ctx:

    def __init__(self, name="prog"):
        self.nc = bass.Bass("TRN2", target_bir_lowering=False)
        self.st = contextlib.ExitStack()
        self.ops = {e: [] for e in ENGINES}
        self.s = {}
        for nm in ("ld", "st", "w", "t", "v", "a", "g"):
            self.s[nm] = Sem(self.st.enter_context(self.nc.semaphore("s_" + nm)), nm)
        self.rot = [Sem(self.st.enter_context(self.nc.semaphore(f"r{i}")), f"r{i}")
                    for i in range(12)]
        self._uid = 0

    def uid(self):
        self._uid += 1
        return self._uid

    def dram_in(self, name, shape, dtype=F32):
        return self.nc.dram_tensor(name, list(shape), dtype, kind="ExternalInput").ap()

    def dram_out(self, name, shape, dtype=F32):
        return self.nc.dram_tensor(name, list(shape), dtype, kind="ExternalOutput").ap()

    def dram_tmp(self, name, shape, dtype=F32):
        return self.nc.dram_tensor(name, list(shape), dtype, kind="Internal").ap()

    def sb(self, name, shape, dtype, stack=None):
        return (stack or self.st).enter_context(
            self.nc.sbuf_tensor(f"{name}_{self.uid()}", shape, dtype))

    def ps(self, name, shape, dtype=F32, stack=None):
        return (stack or self.st).enter_context(
            self.nc.psum_tensor(f"{name}_{self.uid()}", shape, dtype))

    def scope(self):
        return self.st.enter_context(contextlib.ExitStack())

    def op(self, eng, method, *args, inc=None, **kw):
        if inc == "auto":
            inc = [(self.s[ESEM[eng]], 1)]
        def f(E, method=method, args=args, kw=kw, inc=inc):
            ins = getattr(E, method)(*args, **kw)
            if inc:
                for s, n in inc:
                    ins.then_inc(s.h, n)
        self.ops[eng].append(f)
        if inc:
            for s, n in inc:
                s.n += n

    def wait(self, eng, sem, val=None):
        v = sem.n if val is None else val
        if v <= 0:
            return
        def f(E, sem=sem, v=v):
            E.wait_ge(sem.h, v)
        self.ops[eng].append(f)

    def cop(self, eng, method, *args, **kw):
        self.wait(eng, self.s[ESEM[eng]])
        self.op(eng, method, *args, inc="auto", **kw)

    def wait_all(self, eng, names=("w", "v", "a", "t", "g")):
        for nm in names:
            self.wait(eng, self.s[nm])

    def dma(self, eng, out, in_, sem=None, **kw):
        if sem is None:
            sem = self.s["w"] if eng == "gpsimd" else self.s["ld"]
        self.op(eng, "dma_start", inc=[(sem, 16)], out=out, in_=in_, **kw)
        return sem.n

    def barrier(self, engines=ENGINES):
        counts = [(s, s.n) for s in list(self.s.values()) + self.rot]
        for e in engines:
            for s, v in counts:
                self.wait(e, s, v)

    def finish(self):
        with self.nc.Block() as block:
            for e in ENGINES:
                lst = self.ops[e]
                getattr(block, e)(lambda E, lst=lst: [f(E) for f in lst])
        self.st.close()
        return self.nc


class Consts:
    def __init__(self, c: Ctx):
        self.ident_b = c.sb("ident_b", [128, 128], BF16)
        self.causal_add = c.sb("causal_add", [128, 128], F32)
        self.ones_row = c.sb("ones_row", [1, 128], F32)
        self.ones_col_b = c.sb("ones_col_b", [128, 1], BF16)
        self.ones_col_f = c.sb("ones_col_f", [128, 1], F32)
        self.ident_f = c.sb("ident_f", [128, 128], F32)
        g = "gpsimd"
        c.op(g, "memset", self.ident_b[:], 0.0, inc="auto")
        c.cop(g, "affine_select", out=self.ident_b[:], in_=self.ident_b[:],
              compare_op=ALU.not_equal, fill=1.0, base=0,
              pattern=[[-1, 128]], channel_multiplier=1)
        c.op(g, "memset", self.causal_add[:], 0.0, inc="auto")
        c.cop(g, "affine_select", out=self.causal_add[:], in_=self.causal_add[:],
              compare_op=ALU.is_ge, fill=NEG, base=0,
              pattern=[[-1, 128]], channel_multiplier=1)
        self.tri_keep = c.sb("tri_keep", [128, 128], F32)
        c.op(g, "memset", self.tri_keep[:], 1.0, inc="auto")
        c.cop(g, "affine_select", out=self.tri_keep[:], in_=self.tri_keep[:],
              compare_op=ALU.is_ge, fill=0.0, base=0,
              pattern=[[1, 128]], channel_multiplier=-1)
        c.op(g, "memset", self.ones_row[:], 1.0, inc="auto")
        c.op(g, "memset", self.ones_col_b[:], 1.0, inc="auto")
        c.op(g, "memset", self.ones_col_f[:], 1.0, inc="auto")
        c.op(g, "memset", self.ident_f[:], 0.0, inc="auto")
        c.cop(g, "affine_select", out=self.ident_f[:], in_=self.ident_f[:],
              compare_op=ALU.not_equal, fill=1.0, base=0,
              pattern=[[-1, 128]], channel_multiplier=1)
        self.ready = c.s["g"].n


def stats_pass(c: Ctx, x_dram, row0, ntiles, double_ln, persist, stack):
    mu_s = c.sb("mu_s", [128, ntiles], F32, persist)
    rs_s = c.sb("rs_s", [128, ntiles], F32, persist)
    var_s = c.sb("var_s", [128, ntiles], F32, stack)
    tmp_s = c.sb("tmp_s", [128, ntiles], F32, stack)
    junks = [c.sb(f"junk{i}", [128, D], BF16, stack) for i in range(3)]
    bufs = [c.sb(f"stx{i}", [128, D], F32, stack) for i in range(3)]
    v, a = c.s["v"], c.s["a"]
    vplan, aplan = {}, {}
    for t in range(ntiles):
        b = bufs[t % 3]
        rs = c.rot[t % 3]
        if t >= 3:
            c.wait("sync", v, vplan[t - 3])
            c.wait("sync", a, aplan[t - 3])
        ldp = c.dma("sync", b[:], x_dram[row0 + t * 128: row0 + (t + 1) * 128, :],
                    sem=rs)
        c.wait("vector", rs, ldp)
        c.op("vector", "reduce_sum", mu_s[:, t:t + 1], b[:], AX.X, inc="auto")
        vplan[t] = v.n
        c.wait("scalar", rs, ldp)
        if t >= 3:
            c.wait("scalar", a, aplan[t - 3])
        c.op("scalar", "activation", junks[t % 3][:], b[:], AF.Square,
             accum_out=var_s[:, t:t + 1], inc="auto")
        aplan[t] = a.n
    c.wait("vector", a)
    c.cop("vector", "tensor_scalar_mul", mu_s[:], mu_s[:], 1.0 / D)
    c.cop("vector", "tensor_scalar_mul", var_s[:], var_s[:], 1.0 / D)
    c.cop("vector", "tensor_tensor", tmp_s[:], mu_s[:], mu_s[:], ALU.mult)
    c.cop("vector", "tensor_tensor", var_s[:], var_s[:], tmp_s[:], ALU.subtract)
    c.cop("vector", "tensor_scalar_add", tmp_s[:], var_s[:], EPS)
    c.wait("scalar", c.s["v"])
    c.op("scalar", "activation", tmp_s[:], tmp_s[:], AF.Sqrt, inc="auto")
    c.wait("vector", a)
    c.cop("vector", "reciprocal", rs_s[:], tmp_s[:])
    if double_ln:
        c.cop("vector", "tensor_tensor", tmp_s[:], var_s[:], rs_s[:], ALU.mult)
        c.cop("vector", "tensor_tensor", tmp_s[:], tmp_s[:], rs_s[:], ALU.mult)
        c.cop("vector", "tensor_scalar_add", tmp_s[:], tmp_s[:], EPS)
        c.wait("scalar", c.s["v"])
        c.op("scalar", "activation", tmp_s[:], tmp_s[:], AF.Sqrt, inc="auto")
        c.wait("vector", a)
        c.cop("vector", "reciprocal", tmp_s[:], tmp_s[:])
        c.cop("vector", "tensor_tensor", rs_s[:], rs_s[:], tmp_s[:], ALU.mult)
    return mu_s, rs_s


def xn_fm_build(c: Ctx, x_dram, row0, ntiles, mu_s, rs_s, xn_fm, cn, stack,
                gamma_bc=None, beta_bc=None):
    v, a, t_ = c.s["v"], c.s["a"], c.s["t"]
    bufs = [c.sb(f"xnx{i}", [128, D], F32, stack) for i in range(2)]
    tmb = [c.sb(f"xntm{i}", [128, D], BF16, stack) for i in range(2)]
    pst = [c.ps(f"xnps{i}", [128, 1024], BF16, stack) for i in range(2)]
    c.wait("tensor", c.s["g"], cn.ready)
    c.wait("vector", c.s["v"])
    c.wait("vector", c.s["a"])
    vplan, tplan, eplan = {}, {}, {}
    for t in range(ntiles):
        b, tm, pp = bufs[t % 2], tmb[t % 2], pst[t % 2]
        rs = c.rot[3 + t % 2]
        if t >= 2:
            c.wait("sync", v, vplan[t - 2])
        ldp = c.dma("sync", b[:], x_dram[row0 + t * 128: row0 + (t + 1) * 128, :],
                    sem=rs)
        c.wait("vector", rs, ldp)
        if t >= 2:
            c.wait("vector", t_, tplan[t - 2])
        if gamma_bc is None:
            c.op("vector", "tensor_scalar", tm[:], b[:], mu_s[:, t:t + 1],
                 rs_s[:, t:t + 1], ALU.subtract, ALU.mult, inc="auto")
        else:
            c.op("vector", "tensor_scalar", tm[:], b[:], mu_s[:, t:t + 1],
                 rs_s[:, t:t + 1], ALU.subtract, ALU.mult, inc="auto")
            c.cop("vector", "tensor_tensor", tm[:], tm[:], gamma_bc[:], ALU.mult)
            c.cop("vector", "tensor_tensor", tm[:], tm[:], beta_bc[:], ALU.add)
        vplan[t] = v.n
        c.wait("tensor", v, vplan[t])
        if t >= 2:
            c.wait("tensor", a, eplan[t - 2])
        for k in range(KC):
            c.op("tensor", "transpose", pp[:, k * 128:(k + 1) * 128],
                 tm[:, k * 128:(k + 1) * 128], cn.ident_b[:],
                 inc=[(t_, 1)] if k == KC - 1 else None)
        tplan[t] = t_.n
        c.wait("scalar", t_, tplan[t])
        c.op("scalar", "copy", xn_fm[:, :, t * 128:(t + 1) * 128], pp[:],
             inc="auto")
        eplan[t] = a.n


def w_cast_load(c: Ctx, w_dram, w_sb, rows, cols):
    wr = w_dram.rearrange("(k p) f -> k p f", p=128)
    for k in range(rows // 128):
        c.dma("gpsimd", w_sb[:, k, :], wr[k])
    return c.s["w"].n


def w_t_load(c: Ctx, w_dram, wt_sb, out_dim, in_dim, cn, stack, scale=None):
    v, t_ = c.s["v"], c.s["t"]
    nrow = out_dim // 128
    nk = in_dim // 128
    bufs = [c.sb(f"wld{i}", [128, in_dim], BF16, stack) for i in range(2)]
    pst = [c.ps(f"wlps{i}", [128, 1024], BF16, stack) for i in range(2)]
    wr = w_dram.rearrange("(k p) f -> k p f", p=128)
    c.wait("tensor", c.s["g"], cn.ready)
    c.wait_all("tensor")
    c.wait("gpsimd", t_)
    tplan, eplan = {}, {}
    for r in range(nrow):
        b, pp = bufs[r % 2], pst[r % 2]
        rsm = c.rot[5 + r % 2]
        if r >= 2:
            c.wait("gpsimd", t_, tplan[r - 2])
        wldp = c.dma("gpsimd", b[:], wr[r], sem=rsm)
        c.wait("tensor", rsm, wldp)
        if r >= 2:
            c.wait("tensor", v, eplan[r - 2])
        gev = None
        for g0 in range(0, nk, 8):
            gn = min(8, nk - g0)
            if gev is not None:
                c.wait("tensor", v, gev)
            for j in range(gn):
                k = g0 + j
                c.op("tensor", "transpose", pp[:, j * 128:(j + 1) * 128],
                     b[:, k * 128:(k + 1) * 128], cn.ident_b[:],
                     inc=[(t_, 1)] if j == gn - 1 else None)
            tplan[r] = t_.n
            c.wait("vector", t_, tplan[r])
            scl = None
            if scale is not None and scale[0] <= r * 128 and (r + 1) * 128 <= scale[1]:
                scl = scale[2]
            dst = wt_sb[:, g0:g0 + gn, r * 128:(r + 1) * 128]
            if scl is None:
                c.op("vector", "tensor_copy", dst, pp[:, 0:gn * 128],
                     inc="auto")
            else:
                c.op("vector", "tensor_scalar_mul", dst, pp[:, 0:gn * 128],
                     scl, inc="auto")
            eplan[r] = v.n
            gev = v.n


def proj_fm(c: Ctx, w_sb, xn_fm, out_fm, out_chunks, T, pst, scale=None,
            col0=0):
    v, t_ = c.s["v"], c.s["t"]
    c.wait_all("tensor")
    ntt = (T + 511) // 512
    idx = 0
    eplan = {}
    for mo in range(out_chunks):
        for tt in range(ntt):
            w_ = min(512, T - tt * 512)
            pp = pst[idx % 2]
            if idx >= 2:
                c.wait("tensor", v, eplan[idx - 2])
            for k in range(KC):
                c.op("tensor", "matmul", pp[:, :w_],
                     w_sb[:, k, col0 + mo * 128:col0 + (mo + 1) * 128],
                     xn_fm[:, k, tt * 512:tt * 512 + w_],
                     start=(k == 0), stop=(k == KC - 1),
                     inc=[(t_, 1)] if k == KC - 1 else None)
            tp = t_.n
            c.wait("vector", t_, tp)
            if scale is None:
                c.op("vector", "tensor_copy",
                     out_fm[:, mo, tt * 512:tt * 512 + w_], pp[:, :w_],
                     inc="auto")
            else:
                c.op("vector", "tensor_scalar_mul",
                     out_fm[:, mo, tt * 512:tt * 512 + w_], pp[:, :w_], scale,
                     inc="auto")
            eplan[idx] = v.n
            idx += 1


def proj_tm(c: Ctx, w_sb, xn_fm, out_tm, out_dim, ntiles, pst, col0=0):
    v, t_ = c.s["v"], c.s["t"]
    c.wait_all("tensor")
    idx = 0
    eplan = {}
    for t in range(ntiles):
        for oo in range(out_dim // 512):
            pp = pst[idx % 2]
            if idx >= 2:
                c.wait("tensor", v, eplan[idx - 2])
            for k in range(KC):
                c.op("tensor", "matmul", pp[:],
                     xn_fm[:, k, t * 128:(t + 1) * 128],
                     w_sb[:, k, col0 + oo * 512:col0 + (oo + 1) * 512],
                     start=(k == 0), stop=(k == KC - 1),
                     inc=[(t_, 1)] if k == KC - 1 else None)
            tp = t_.n
            c.wait("vector", t_, tp)
            c.op("vector", "tensor_copy",
                 out_tm[:, t, oo * 512:(oo + 1) * 512], pp[:], inc="auto")
            eplan[idx] = v.n
            idx += 1


def emit_sl5(c: Ctx, cn: Consts, x_in, x_out, wk_d, wq_d, wv_d, T):
    ntiles = T // 128
    v, a, t_, stw, w = (c.s["v"], c.s["a"], c.s["t"], c.s["st"], c.s["w"])
    wst = c.scope()
    wk_sb = c.sb("wk_sb", [128, KC, D], BF16, wst)
    wq_sb = c.sb("wq_sb", [128, KC, D], BF16, wst)
    wv_sb = c.sb("wv_sb", [128, KC, D], BF16, wst)
    w_cast_load(c, wk_d, wk_sb, D, D)
    w_cast_load(c, wq_d, wq_sb, D, D)
    w_cast_load(c, wv_d, wv_sb, D, D)
    c.wait("vector", w)
    for k in range(KC):
        c.op("vector", "tensor_scalar_mul", wq_sb[:, k, :], wq_sb[:, k, :],
             1.0 / 32.0, inc="auto")

    for s in range(BPC):
        row0 = s * T
        sst = c.scope()
        xn_fm = c.sb("xn_fm", [128, KC, T], BF16, sst)
        p1 = contextlib.ExitStack()
        mu_s, rs_s = stats_pass(c, x_in, row0, ntiles, False, p1, p1)
        xn_fm_build(c, x_in, row0, ntiles, mu_s, rs_s, xn_fm, cn, p1)
        p1.close()
        c.barrier()
        k_fm = c.sb("k_fm", [128, KC, T], BF16, sst)
        q_fm = c.sb("q_fm", [128, KC, T], BF16, sst)
        v_tm = c.sb("v_tm", [128, ntiles, D], BF16, sst)
        p2 = contextlib.ExitStack()
        pst = [c.ps(f"prps{i}", [128, 512], F32, p2) for i in range(2)]
        proj_fm(c, wk_sb, xn_fm, k_fm, KC, T, pst)
        proj_fm(c, wq_sb, xn_fm, q_fm, KC, T, pst)
        proj_tm(c, wv_sb, xn_fm, v_tm, D, ntiles, pst)
        p2.close()
        c.barrier()
        p3 = contextlib.ExitStack()
        ps_s = c.ps("ps_s", [128, 2048], F32, p3)
        ps_pt = c.ps("ps_pt", [128, 1024], BF16, p3)
        ps_o = [c.ps(f"ps_o{i}", [128, 512], F32, p3) for i in range(2)]
        p_sb = c.sb("p_sb", [128, 2048], BF16, p3)
        pt_sb = c.sb("pt_sb", [128, 2048], BF16, p3)
        negm = c.sb("negm", [128, 1], F32, p3)
        rsum = c.sb("rsum", [128, 1], F32, p3)
        recip = c.sb("recip", [128, 1], F32, p3)
        xres = c.sb("xres", [128, D], F32, p3)
        xo = c.sb("xo", [128, D], F32, p3)
        c.wait_all("tensor")
        for qi in range(ntiles):
            nk = (qi + 1) * 128
            c.wait("sync", stw)
            xldp = c.dma("sync", xres[:],
                         x_in[row0 + qi * 128: row0 + (qi + 1) * 128, :],
                         sem=c.rot[7])
            for nb in range((nk + 511) // 512):
                w_ = min(512, nk - nb * 512)
                for k in range(KC):
                    c.op("tensor", "matmul", ps_s[:, nb * 512:nb * 512 + w_],
                         q_fm[:, k, qi * 128:(qi + 1) * 128],
                         k_fm[:, k, nb * 512:nb * 512 + w_],
                         start=(k == 0), stop=(k == KC - 1),
                         inc=[(t_, 1)] if k == KC - 1 else None)
            sc_done = t_.n
            c.wait("vector", t_, sc_done)
            c.op("vector", "tensor_tensor", ps_s[:, (nk - 128):nk],
                 ps_s[:, (nk - 128):nk], cn.causal_add[:], ALU.add, inc="auto")
            c.cop("vector", "reduce_max", negm[:], ps_s[:, :nk], AX.X,
                  negate=True)
            mx_done = v.n
            c.wait("scalar", v, mx_done)
            c.op("scalar", "activation", p_sb[:, :nk], ps_s[:, :nk], AF.Exp,
                 bias=negm[:], scale=1.0, accum_out=rsum[:], inc="auto")
            exp_done = a.n
            c.wait("vector", a, exp_done)
            c.op("vector", "reciprocal", recip[:], rsum[:], inc="auto")
            rec_done = v.n
            c.wait("tensor", a, exp_done)
            ptev = None
            for g0 in range(0, qi + 1, 8):
                gn = min(8, qi + 1 - g0)
                if ptev is not None:
                    c.wait("tensor", v, ptev)
                for j in range(gn):
                    kb = g0 + j
                    c.op("tensor", "transpose", ps_pt[:, j * 128:(j + 1) * 128],
                         p_sb[:, kb * 128:(kb + 1) * 128], cn.ident_b[:],
                         inc=[(t_, 1)] if j == gn - 1 else None)
                ptp = t_.n
                c.wait("vector", t_, ptp)
                for j in range(gn):
                    kb = g0 + j
                    c.op("vector", "tensor_copy",
                         pt_sb[:, kb * 128:(kb + 1) * 128],
                         ps_pt[:, j * 128:(j + 1) * 128], inc="auto")
                ptev = v.n
            c.wait("tensor", v, ptev)
            for half in range(2):
                for kb in range(qi + 1):
                    c.op("tensor", "matmul", ps_o[half][:],
                         pt_sb[:, kb * 128:(kb + 1) * 128],
                         v_tm[:, kb, half * 512:(half + 1) * 512],
                         start=(kb == 0), stop=(kb == qi),
                         inc=[(t_, 1)] if kb == qi else None)
            pv_done = t_.n
            c.wait("vector", t_, pv_done)
            c.wait("vector", c.rot[7], xldp)
            c.wait("vector", v, rec_done)
            c.wait("vector", stw)
            for half in range(2):
                c.op("vector", "scalar_tensor_tensor",
                     xo[:, half * 512:(half + 1) * 512], ps_o[half][:],
                     recip[:], xres[:, half * 512:(half + 1) * 512],
                     ALU.mult, ALU.add, inc="auto")
            ev_done = v.n
            c.wait("sync", v, ev_done)
            c.dma("sync", x_out[row0 + qi * 128: row0 + (qi + 1) * 128, :],
                  xo[:], sem=stw)
        p3.close()
        c.barrier()
        sst.close()
    wst.close()


def _finish(c: Ctx):
    c.wait("sync", c.s["st"])
    return c.finish()


def build_sl5(T=2048):
    c = Ctx("sl5")
    cn = Consts(c)
    x_in = c.dram_in("x", [BPC * T, D])
    wk = c.dram_in("Wk", [D, D])
    wq = c.dram_in("Wq", [D, D])
    wv = c.dram_in("Wv", [D, D])
    x_out = c.dram_out("y", [BPC * T, D])
    emit_sl5(c, cn, x_in, x_out, wk, wq, wv, T)
    return _finish(c)


_BUILD_CACHE = {}


def _get_prog(name, builder):
    if name not in _BUILD_CACHE:
        _BUILD_CACHE[name] = builder(T=2048)
    return _BUILD_CACHE[name]


def kernel(**inputs):
    return kernel_fused(**inputs)


def kernel_unfused(**inputs):
    f32 = lambda k: np.ascontiguousarray(np.asarray(inputs[k], dtype=np.float32))
    x = f32("L0_attn_a_x")
    T = 2048
    shards = [np.ascontiguousarray(x[2 * cc:2 * cc + 2].reshape(BPC * T, D))
              for cc in range(NCORES)]
    launches = [
        ("sl1", build_sl1, {"w_qkv": "L0_attn_a_w_qkv",
                            "gamma": "L0_attn_a_gamma",
                            "beta": "L0_attn_a_beta"}),
        ("ffn", build_ffn, {"w1": "L0_ffn_w1", "b1": "L0_ffn_b1",
                            "w2": "L0_ffn_w2", "b2": "L0_ffn_b2",
                            "ln1_g": "L0_ffn_ln1_g", "ln1_b": "L0_ffn_ln1_b",
                            "ln2_g": "L0_ffn_ln2_g", "ln2_b": "L0_ffn_ln2_b"}),
        ("perf", build_perf, {"omega": "L0_attn_b_omega", "wq": "L0_attn_b_wq",
                              "wk": "L0_attn_b_wk", "wv": "L0_attn_b_wv",
                              "wo": "L0_attn_b_wo", "bo": "L0_attn_b_bo"}),
        ("ffn", build_ffn, {"w1": "L0_ffn_w1", "b1": "L0_ffn_b1",
                            "w2": "L0_ffn_w2", "b2": "L0_ffn_b2",
                            "ln1_g": "L0_ffn_ln1_g", "ln1_b": "L0_ffn_ln1_b",
                            "ln2_g": "L0_ffn_ln2_g", "ln2_b": "L0_ffn_ln2_b"}),
        ("sl5", build_sl5, {"Wk": "L1_attn_a_Wk", "Wq": "L1_attn_a_Wq",
                            "Wv": "L1_attn_a_Wv"}),
        ("moe", build_moe, {"w_router": "L1_ffn_w_router",
                            "w_gating": "L1_ffn_w_gating",
                            "w_gate": "L1_ffn_w_gate", "w_up": "L1_ffn_w_up",
                            "w_down": "L1_ffn_w_down"}),
        ("sl7", build_sl7, {"in_proj": "L1_attn_b_in_proj_weight",
                            "out_proj": "L1_attn_b_out_proj_weight"}),
        ("moe", build_moe, {"w_router": "L1_ffn_w_router",
                            "w_gating": "L1_ffn_w_gating",
                            "w_gate": "L1_ffn_w_gate", "w_up": "L1_ffn_w_up",
                            "w_down": "L1_ffn_w_down"}),
    ]
    for name, builder, wmap in launches:
        nc = _get_prog(name, builder)
        wvals = {k: f32(v) for k, v in wmap.items()}
        in_maps = [dict(wvals, x=shards[cc]) for cc in range(NCORES)]
        res = run_bass_kernel_spmd(nc, in_maps, core_ids=list(range(NCORES)))
        shards = [np.ascontiguousarray(np.asarray(res.results[cc]["y"],
                                                  dtype=np.float32))
                  for cc in range(NCORES)]
    return np.concatenate([s.reshape(1, BPC, T, D) for s in shards],
                          axis=0).reshape(16, T, D)


def row_broadcast(c: Ctx, vec_dram, n, bc_sb, cn, stack, rsl=8):
    v, a, t_ = c.s["v"], c.s["a"], c.s["t"]
    row = c.sb("bcrow", [1, n], F32, stack)
    pp = c.ps("bcps", [128, 512], F32, stack)
    c.wait("sync", c.s["t"])
    c.wait("sync", c.s["v"])
    c.wait("sync", c.s["a"])
    ldp = c.dma("sync", row[:], vec_dram[None, :], sem=c.rot[rsl])
    c.wait("tensor", c.rot[rsl], ldp)
    c.wait("tensor", c.s["g"], cn.ready)
    c.wait_all("tensor")
    prev_ev = None
    for hh in range(n // 512):
        if prev_ev is not None:
            c.wait("tensor", v, prev_ev)
        c.op("tensor", "matmul", pp[:], cn.ones_row[:],
             row[:, hh * 512:(hh + 1) * 512], start=True, stop=True,
             inc=[(t_, 1)])
        tp = t_.n
        c.wait("vector", t_, tp)
        c.op("vector", "tensor_copy", bc_sb[:, hh * 512:(hh + 1) * 512], pp[:],
             inc="auto")
        prev_ev = v.n


def attn_mh(c: Ctx, cn: Consts, q_fm, k_fm, v_tm, T, finish_tile, stack):
    v, a, t_ = c.s["v"], c.s["a"], c.s["t"]
    ntiles = T // 128
    ps_s = c.ps("mh_s", [128, 2048], F32, stack)
    ps_pt = c.ps("mh_pt", [128, 2048], BF16, stack)
    ps_out = c.ps("mh_o", [128, 1024], F32, stack)
    p_sb = c.sb("mh_p", [128, 2048], BF16, stack)
    pt_sb = c.sb("mh_ptsb", [128, 2048], BF16, stack)
    negm = c.sb("mh_negm", [128, 1], F32, stack)
    rsum = c.sb("mh_rsum", [128, 1], F32, stack)
    recips = c.sb("mh_recips", [128, 16], F32, stack)
    c.wait_all("tensor")
    fin_prev = None
    for qi in range(ntiles):
        nk = (qi + 1) * 128
        for h in range(16):
            ch, pb = h // 2, 64 * (h % 2)
            if fin_prev is not None:
                c.wait("tensor", v, fin_prev)
                fin_prev = None
            for nb in range((nk + 511) // 512):
                w_ = min(512, nk - nb * 512)
                c.op("tensor", "matmul", ps_s[:, nb * 512:nb * 512 + w_],
                     q_fm[pb:pb + 64, ch, qi * 128:(qi + 1) * 128],
                     k_fm[pb:pb + 64, ch, nb * 512:nb * 512 + w_],
                     start=True, stop=True,
                     tile_position=(pb, 0),
                     inc=[(t_, 1)] if nb == (nk + 511) // 512 - 1 else None)
            sc_done = t_.n
            c.wait("vector", t_, sc_done)
            c.op("vector", "tensor_tensor", ps_s[:, (nk - 128):nk],
                 ps_s[:, (nk - 128):nk], cn.causal_add[:], ALU.add, inc="auto")
            c.cop("vector", "reduce_max", negm[:], ps_s[:, :nk], AX.X,
                  negate=True)
            mx_done = c.s["v"].n
            c.wait("scalar", v, mx_done)
            c.op("scalar", "activation", p_sb[:, :nk], ps_s[:, :nk], AF.Exp,
                 bias=negm[:], scale=1.0, accum_out=rsum[:], inc="auto")
            exp_done = c.s["a"].n
            c.wait("vector", a, exp_done)
            c.cop("vector", "reciprocal", recips[:, h:h + 1], rsum[:])
            c.wait("tensor", a, exp_done)
            ptev = None
            for g0 in range(0, qi + 1, 8):
                gn = min(8, qi + 1 - g0)
                if ptev is not None:
                    c.wait("tensor", v, ptev)
                for j in range(gn):
                    kb = g0 + j
                    c.op("tensor", "transpose",
                         ps_pt[:, j * 128:(j + 1) * 128],
                         p_sb[:, kb * 128:(kb + 1) * 128], cn.ident_b[:],
                         inc=[(t_, 1)] if j == gn - 1 else None)
                ptp = t_.n
                c.wait("vector", t_, ptp)
                for j in range(gn):
                    kb = g0 + j
                    c.op("vector", "tensor_copy",
                         pt_sb[:, kb * 128:(kb + 1) * 128],
                         ps_pt[:, j * 128:(j + 1) * 128], inc="auto")
                ptev = c.s["v"].n
            c.wait("tensor", v, ptev)
            first_of_bank = h % 8 == 0
            last_of_bank = h % 8 == 7
            for kb in range(qi + 1):
                c.op("tensor", "matmul", ps_out[:, 64 * h:64 * h + 64],
                     pt_sb[:, kb * 128:(kb + 1) * 128],
                     v_tm[:, kb, 64 * h:64 * h + 64],
                     start=(first_of_bank and kb == 0),
                     stop=(last_of_bank and kb == qi),
                     inc=[(t_, 1)] if kb == qi else None)
        pv_done = c.s["t"].n
        fin_prev = finish_tile(qi, ps_out, recips, pv_done)
    return fin_prev


def emit_attn16(c: Ctx, cn: Consts, x_in, x_out, wqkv_d, T, mode,
                gamma_d=None, beta_d=None, wo_d=None):
    ntiles = T // 128
    v, a, t_, stw = c.s["v"], c.s["a"], c.s["t"], c.s["st"]
    wst = c.scope()
    wqkvT = c.sb("wqkvT", [128, KC, 3 * D], BF16, wst)
    pwl = contextlib.ExitStack()
    w_t_load(c, wqkv_d, wqkvT, 3 * D, D, cn, pwl, scale=(0, D, 0.125))
    pwl.close()
    gamma_bc = beta_bc = woT = None
    if mode == "sl1":
        gamma_bc = c.sb("gamma_bc", [128, D], F32, wst)
        beta_bc = c.sb("beta_bc", [128, D], F32, wst)
        pbc = contextlib.ExitStack()
        row_broadcast(c, gamma_d, D, gamma_bc, cn, pbc, rsl=8)
        row_broadcast(c, beta_d, D, beta_bc, cn, pbc, rsl=9)
        pbc.close()
    c.barrier()

    for s in range(BPC):
        row0 = s * T
        sst = c.scope()
        xn_fm = c.sb("xn_fm", [128, KC, T], BF16, sst)
        p1 = contextlib.ExitStack()
        mu_s, rs_s = stats_pass(c, x_in, row0, ntiles, mode == "sl1", p1, p1)
        xn_fm_build(c, x_in, row0, ntiles, mu_s, rs_s, xn_fm, cn, p1,
                    gamma_bc=gamma_bc, beta_bc=beta_bc)
        p1.close()
        c.barrier()
        p23 = contextlib.ExitStack()
        q_fm = c.sb("q_fm", [128, KC, T], BF16, p23)
        k_fm = c.sb("k_fm", [128, KC, T], BF16, p23)
        v_tm = c.sb("v_tm", [128, ntiles, D], BF16, p23)
        pps = contextlib.ExitStack()
        pst = [c.ps(f"prps{i}", [128, 512], F32, pps) for i in range(2)]
        proj_fm(c, wqkvT, xn_fm, q_fm, KC, T, pst, col0=0)
        proj_fm(c, wqkvT, xn_fm, k_fm, KC, T, pst, col0=D)
        proj_tm(c, wqkvT, xn_fm, v_tm, D, ntiles, pst, col0=2 * D)
        pps.close()
        c.barrier()
        if mode == "sl1":
            xres = c.sb("xres", [128, D], F32, p23)
            xo = c.sb("xo", [128, D], F32, p23)

            def finish_tile(qi, ps_out, recips, pv_done, rec_done=None):
                c.wait("sync", stw)
                xldp = c.dma("sync", xres[:],
                             x_in[row0 + qi * 128: row0 + (qi + 1) * 128, :],
                             sem=c.rot[7])
                c.wait("vector", t_, pv_done)
                c.wait("vector", c.rot[7], xldp)
                c.wait("vector", stw)
                if rec_done:
                    c.wait("vector", v, rec_done)
                for h in range(16):
                    c.op("vector", "scalar_tensor_tensor",
                         xo[:, 64 * h:64 * h + 64], ps_out[:, 64 * h:64 * h + 64],
                         recips[:, h:h + 1], xres[:, 64 * h:64 * h + 64],
                         ALU.mult, ALU.add, inc="auto")
                ev = v.n
                c.wait("sync", v, ev)
                c.dma("sync", x_out[row0 + qi * 128: row0 + (qi + 1) * 128, :],
                      xo[:], sem=stw)
                return ev

            attn_mh2(c, cn, q_fm, k_fm, v_tm, T, finish_tile, p23)
            p23.close()
            c.barrier()
        else:
            at_tm = c.sb("at_tm", [128, D], BF16, p23)
            at_ps = c.ps("at_ps", [128, 1024], BF16, p23)
            attnout_fm = xn_fm

            def finish_tile(qi, ps_out, recips, pv_done, rec_done=None):
                c.wait("vector", t_, pv_done)
                c.wait("vector", t_)
                if rec_done:
                    c.wait("vector", v, rec_done)
                for h in range(16):
                    c.op("vector", "tensor_scalar_mul",
                         at_tm[:, 64 * h:64 * h + 64],
                         ps_out[:, 64 * h:64 * h + 64],
                         recips[:, h:h + 1], inc="auto")
                ev = v.n
                c.wait("tensor", v, ev)
                c.wait("tensor", a)
                for k in range(KC):
                    c.op("tensor", "transpose", at_ps[:, k * 128:(k + 1) * 128],
                         at_tm[:, k * 128:(k + 1) * 128], cn.ident_b[:],
                         inc=[(t_, 1)] if k == KC - 1 else None)
                tp = t_.n
                c.wait("scalar", t_, tp)
                c.op("scalar", "copy",
                     attnout_fm[:, :, qi * 128:(qi + 1) * 128], at_ps[:],
                     inc="auto")
                return ev

            attn_mh2(c, cn, q_fm, k_fm, v_tm, T, finish_tile, p23,
                     pt_banks=1)
            p23.close()
            c.barrier()
            p4 = contextlib.ExitStack()
            woT = c.sb("woT", [128, KC, D], BF16, p4)
            pwl2 = contextlib.ExitStack()
            w_t_load(c, wo_d, woT, D, D, cn, pwl2)
            pwl2.close()
            c.barrier()
            po = [c.ps(f"po{i}", [128, 512], F32, p4) for i in range(2)]
            xres = c.sb("xres", [128, D], F32, p4)
            xo = c.sb("xo", [128, D], F32, p4)
            c.wait_all("tensor")
            for t in range(ntiles):
                c.wait("sync", stw)
                xldp = c.dma("sync", xres[:],
                             x_in[row0 + t * 128: row0 + (t + 1) * 128, :],
                             sem=c.rot[7])
                for half in range(2):
                    pp = po[half]
                    for k in range(KC):
                        c.op("tensor", "matmul", pp[:],
                             attnout_fm[:, k, t * 128:(t + 1) * 128],
                             woT[:, k, half * 512:(half + 1) * 512],
                             start=(k == 0), stop=(k == KC - 1),
                             inc=[(t_, 1)] if k == KC - 1 else None)
                mmp = t_.n
                c.wait("vector", t_, mmp)
                c.wait("vector", c.rot[7], xldp)
                c.wait("vector", stw)
                for half in range(2):
                    c.op("vector", "tensor_tensor",
                         xo[:, half * 512:(half + 1) * 512], po[half][:],
                         xres[:, half * 512:(half + 1) * 512], ALU.add,
                         inc="auto")
                ev = v.n
                c.wait("sync", v, ev)
                c.dma("sync", x_out[row0 + t * 128: row0 + (t + 1) * 128, :],
                      xo[:], sem=stw)
                c.wait("tensor", v, ev)
            p4.close()
            c.barrier()
        sst.close()
    wst.close()


def build_sl1(T=2048):
    c = Ctx("sl1")
    cn = Consts(c)
    x_in = c.dram_in("x", [BPC * T, D])
    wqkv = c.dram_in("w_qkv", [3 * D, D])
    gamma = c.dram_in("gamma", [D])
    beta = c.dram_in("beta", [D])
    x_out = c.dram_out("y", [BPC * T, D])
    emit_attn16(c, cn, x_in, x_out, wqkv, T, "sl1", gamma_d=gamma, beta_d=beta)
    return _finish(c)


def build_sl7(T=2048):
    c = Ctx("sl7")
    cn = Consts(c)
    x_in = c.dram_in("x", [BPC * T, D])
    wqkv = c.dram_in("in_proj", [3 * D, D])
    wo = c.dram_in("out_proj", [D, D])
    x_out = c.dram_out("y", [BPC * T, D])
    emit_attn16(c, cn, x_in, x_out, wqkv, T, "sl7", wo_d=wo)
    return _finish(c)


def col_load(c: Ctx, vec_dram, n, out_sb, cn, stack, rsl=8):
    v, t_ = c.s["v"], c.s["t"]
    m = n // 128
    row = c.sb("clrow", [m, 128], F32, stack)
    pp = c.ps("clps", [128, 128], F32, stack)
    c.wait("sync", c.s["t"])
    c.wait("sync", c.s["v"])
    c.wait("sync", c.s["a"])
    ldp = c.dma("sync", row[:], vec_dram.rearrange("(m p) -> m p", p=128),
                sem=c.rot[rsl])
    c.wait("tensor", c.rot[rsl], ldp)
    c.wait("tensor", c.s["g"], cn.ready)
    c.wait_all("tensor")
    c.op("tensor", "transpose", pp[:, 0:m], row[:], cn.ident_f[0:m, 0:m],
         inc=[(t_, 1)])
    tp = t_.n
    c.wait("vector", t_, tp)
    c.op("vector", "tensor_copy", out_sb[:], pp[:, 0:m], inc="auto")


def w_quant_t_load(c: Ctx, w_dram, wt_sb, out_dim, in_dim, cn, persist, stack):
    v, a, t_ = c.s["v"], c.s["a"], c.s["t"]
    nrow = out_dim // 128
    nk = in_dim // 128
    ch_bc = c.sb("ch_bc", [128, 1], F32, persist)
    tau_bc = c.sb("tau_bc", [128, 1], F32, stack)
    ntau_bc = c.sb("ntau_bc", [128, 1], F32, stack)
    acc = c.sb("qacc", [128, nrow], F32, stack)
    tot = c.sb("qtot", [1, 1], F32, stack)
    colsum = c.sb("qcolsum", [128, 1], F32, stack)
    bufs = [c.sb(f"qwl{i}", [128, in_dim], F32, stack) for i in range(2)]
    sgn = [c.sb(f"qsg{i}", [128, in_dim], BF16, stack) for i in range(2)]
    qtl = [c.sb(f"qtl{i}", [128, in_dim], BF16, stack) for i in range(2)]
    pst = [c.ps(f"qps{i}", [128, 1024], BF16, stack) for i in range(2)]
    ppf = c.ps("qppf", [128, 512], F32, stack)
    wr = w_dram.rearrange("(k p) f -> k p f", p=128)
    c.wait("tensor", c.s["g"], cn.ready)
    c.wait_all("tensor")
    c.wait("sync", t_)
    c.wait("sync", v)
    c.wait("sync", a)
    vplan = {}
    for r in range(nrow):
        b = bufs[r % 2]
        rsm = c.rot[5 + r % 2]
        if r >= 2:
            c.wait("sync", v, vplan[r - 2])
        ldp = c.dma("sync", b[:], wr[r], sem=rsm)
        c.wait("vector", rsm, ldp)
        c.op("vector", "reduce_sum", acc[:, r:r + 1], b[:], AX.X,
             apply_absolute_value=True, inc="auto")
        vplan[r] = v.n
    c.cop("vector", "reduce_sum", colsum[:], acc[:], AX.X)
    cs_done = v.n
    c.wait("tensor", v, cs_done)
    c.op("tensor", "matmul", ppf[0:1, 0:1], cn.ones_col_f[:], colsum[:],
         start=True, stop=True, inc=[(t_, 1)])
    red_done = t_.n
    c.wait("vector", t_, red_done)
    c.cop("vector", "tensor_scalar_mul", tot[:], ppf[0:1, 0:1],
          1.0 / (out_dim * in_dim))
    c.cop("vector", "tensor_scalar_max", tot[:], tot[:], 1e-5)
    tot_done = v.n
    c.wait("tensor", v, tot_done)
    c.op("tensor", "matmul", ppf[:, 1:2], cn.ones_row[:], tot[:],
         start=True, stop=True, inc=[(t_, 1)])
    bc_done = t_.n
    c.wait("vector", t_, bc_done)
    c.cop("vector", "tensor_scalar_mul", tau_bc[:], ppf[:, 1:2], 0.5)
    c.cop("vector", "tensor_copy", ch_bc[:], tau_bc[:])
    c.cop("vector", "tensor_scalar_mul", ntau_bc[:], tau_bc[:], -1.0)
    tau_done = v.n
    c.wait("scalar", v, tau_done)
    c.wait("sync", v)
    vplan, aplan, tplan, eplan = {}, {}, {}, {}
    for r in range(nrow):
        b, sg, qt, pp = bufs[r % 2], sgn[r % 2], qtl[r % 2], pst[r % 2]
        rsm = c.rot[5 + r % 2]
        if r >= 2:
            c.wait("sync", v, vplan[r - 2])
            c.wait("sync", a, aplan[r - 2])
        ldp = c.dma("sync", b[:], wr[r], sem=rsm)
        c.wait("scalar", rsm, ldp)
        if r >= 2:
            c.wait("scalar", v, vplan[r - 2])
            c.wait("scalar", t_, tplan[r - 2])
        c.op("scalar", "activation", sg[:], b[:], AF.Sign, bias=ntau_bc[:],
             inc="auto")
        c.op("scalar", "activation", qt[:], b[:], AF.Sign, bias=tau_bc[:],
             inc="auto")
        aplan[r] = a.n
        c.wait("vector", a, aplan[r])
        if r >= 2:
            c.wait("vector", t_, tplan[r - 2])
        c.op("vector", "tensor_tensor", qt[:], qt[:], sg[:], ALU.add,
             inc="auto")
        vplan[r] = v.n
        c.wait("tensor", v, vplan[r])
        if r >= 2:
            c.wait("tensor", a, eplan[r - 2])
        gev = None
        for g0 in range(0, nk, 8):
            gn = min(8, nk - g0)
            if gev is not None:
                c.wait("tensor", a, gev)
            for j in range(gn):
                k = g0 + j
                c.op("tensor", "transpose", pp[:, j * 128:(j + 1) * 128],
                     qt[:, k * 128:(k + 1) * 128], cn.ident_b[:],
                     inc=[(t_, 1)] if j == gn - 1 else None)
            tplan[r] = t_.n
            c.wait("scalar", t_, tplan[r])
            c.op("scalar", "copy", wt_sb[:, g0:g0 + gn, r * 128:(r + 1) * 128],
                 pp[:, 0:gn * 128], inc="auto")
            eplan[r] = a.n
            gev = a.n
    return ch_bc


def emit_ffn(c: Ctx, cn: Consts, x_in, x_out, w1_d, b1_d, w2_d, b2_d,
             g1_d, b1g_d, g2_d, b2g_d, T):
    H = 4 * D
    HC = H // 128
    ntiles = BPC * T // 128
    v, a, t_, stw = c.s["v"], c.s["a"], c.s["t"], c.s["st"]
    wst = c.scope()
    w1T = c.sb("w1T", [128, KC, H], BF16, wst)
    pw = contextlib.ExitStack()
    c1h = w_quant_t_load(c, w1_d, w1T, H, D, cn, wst, pw)
    pw.close()
    c.barrier()
    w2T = c.sb("w2T", [128, HC, D], BF16, wst)
    pw2 = contextlib.ExitStack()
    c2h = w_quant_t_load(c, w2_d, w2T, D, H, cn, wst, pw2)
    pw2.close()
    c.barrier()
    g1_bc = c.sb("g1_bc", [128, D], BF16, wst)
    b1g_bc = c.sb("b1g_bc", [128, D], BF16, wst)
    b2_bc = c.sb("b2_bc", [128, D], F32, wst)
    b1_fm = c.sb("b1_fm", [128, HC], F32, wst)
    g2_fm = c.sb("g2_fm", [128, HC], F32, wst)
    b2g_fm = c.sb("b2g_fm", [128, HC], F32, wst)
    pbc = contextlib.ExitStack()
    row_broadcast(c, g1_d, D, g1_bc, cn, pbc, rsl=8)
    row_broadcast(c, b1g_d, D, b1g_bc, cn, pbc, rsl=9)
    row_broadcast(c, b2_d, D, b2_bc, cn, pbc, rsl=10)
    pbc.close()
    pcl = contextlib.ExitStack()
    col_load(c, b1_d, H, b1_fm, cn, pcl, rsl=8)
    col_load(c, g2_d, H, g2_fm, cn, pcl, rsl=9)
    col_load(c, b2g_d, H, b2g_fm, cn, pcl, rsl=10)
    pcl.close()
    c.barrier()

    sst = c.scope()
    p1 = contextlib.ExitStack()
    mu_s, rs_s = stats_pass(c, x_in, 0, ntiles, True, sst, p1)
    p1.close()
    c.barrier()

    lst = contextlib.ExitStack()
    h_bf = c.sb("h_bf", [128, HC, 512], BF16, lst)
    a_fm = c.sb("a_fm", [128, KC, 512], BF16, lst)
    xbuf = [c.sb(f"fx{i}", [128, D], F32, lst) for i in range(2)]
    atm = [c.sb(f"fatm{i}", [128, D], BF16, lst) for i in range(2)]
    hsq = c.sb("hsq", [128, 512], BF16, lst)
    t1s = c.sb("t1s", [128, 512], F32, lst)
    row_mu2 = c.sb("row_mu2", [1, 512], F32, lst)
    row_rs2 = c.sb("row_rs2", [1, 512], F32, lst)
    mu2_bc = c.sb("mu2_bc", [128, 512], F32, lst)
    rs2_bc = c.sb("rs2_bc", [128, 512], F32, lst)
    xres, xo = xbuf[0], xbuf[1]
    pst = [c.ps(f"fps{i}", [128, 512], F32, lst) for i in range(2)]
    ps_tp = c.ps("fptp", [128, 1024], BF16, lst)
    ps_sum = c.ps("fsum", [1, 512], F32, lst)
    ps_sumsq = c.ps("fsumsq", [1, 512], F32, lst)
    ps_bc = c.ps("fbc", [128, 512], F32, lst)

    for tg in range(ntiles // 4):
        c.barrier()
        bdone = {}
        for xt in range(4):
            t = tg * 4 + xt
            b, tm = xbuf[xt % 2], atm[xt % 2]
            rsm = c.rot[xt % 2]
            if xt >= 2:
                c.wait("sync", v, bdone[xt - 2])
            ldp = c.dma("sync", b[:], x_in[t * 128:(t + 1) * 128, :], sem=rsm)
            c.wait("vector", rsm, ldp)
            if xt >= 2:
                c.wait("vector", t_, tpl)
            c.op("vector", "tensor_scalar", tm[:], b[:], mu_s[:, t:t + 1],
                 rs_s[:, t:t + 1], ALU.subtract, ALU.mult, inc="auto")
            c.cop("vector", "tensor_tensor", tm[:], tm[:], g1_bc[:], ALU.mult)
            c.cop("vector", "tensor_tensor", tm[:], tm[:], b1g_bc[:], ALU.add)
            tmp_done = v.n
            bdone[xt] = v.n
            c.wait("tensor", v, tmp_done)
            if xt >= 1:
                c.wait("tensor", a, epl)
            for k in range(KC):
                c.op("tensor", "transpose", ps_tp[:, k * 128:(k + 1) * 128],
                     tm[:, k * 128:(k + 1) * 128], cn.ident_b[:],
                     inc=[(t_, 1)] if k == KC - 1 else None)
            tpl = t_.n
            c.wait("scalar", t_, tpl)
            for k in range(KC):
                c.op("scalar", "copy", a_fm[:, k, xt * 128:(xt + 1) * 128],
                     ps_tp[:, k * 128:(k + 1) * 128], inc="auto")
            epl = a.n
        c.wait_all("tensor")
        mmev = {}
        for mo in range(HC):
            pp = pst[mo % 2]
            if mo >= 2:
                c.wait("tensor", a, mmev[mo - 2])
            for k in range(KC):
                c.op("tensor", "matmul", pp[:],
                     w1T[:, k, mo * 128:(mo + 1) * 128], a_fm[:, k, :],
                     start=(k == 0), stop=(k == KC - 1),
                     inc=[(t_, 1)] if k == KC - 1 else None)
            mmp = t_.n
            c.wait("scalar", t_, mmp)
            c.op("scalar", "activation", h_bf[:, mo, :], pp[:], AF.Gelu,
                 bias=b1_fm[:, mo:mo + 1], scale=c1h[:], inc="auto")
            hev = a.n
            c.wait("scalar", t_)
            c.wait("scalar", a, hev)
            c.op("scalar", "activation", hsq[:], h_bf[:, mo, :], AF.Square,
                 inc="auto")
            sqev = a.n
            mmev[mo] = sqev
            c.wait("tensor", a, sqev)
            c.op("tensor", "matmul", ps_sum[:], cn.ones_col_b[:],
                 h_bf[:, mo, :], start=(mo == 0), stop=(mo == HC - 1))
            c.op("tensor", "matmul", ps_sumsq[:], cn.ones_col_b[:], hsq[:],
                 start=(mo == 0), stop=(mo == HC - 1),
                 inc=[(t_, 1)])
        stat_done = t_.n
        c.wait("vector", t_, stat_done)
        c.cop("vector", "tensor_scalar_mul", row_mu2[:], ps_sum[:], 1.0 / H)
        c.cop("vector", "tensor_scalar_mul", row_rs2[:], ps_sumsq[:], 1.0 / H)
        c.cop("vector", "tensor_tensor", t1s[0:1, :], row_mu2[:], row_mu2[:],
              ALU.mult)
        c.cop("vector", "tensor_tensor", row_rs2[:], row_rs2[:], t1s[0:1, :],
              ALU.subtract)
        c.cop("vector", "tensor_scalar_add", row_rs2[:], row_rs2[:], EPS)
        c.wait("scalar", v)
        c.op("scalar", "activation", row_rs2[:], row_rs2[:], AF.Sqrt,
             inc="auto")
        c.wait("vector", a)
        c.cop("vector", "reciprocal", row_rs2[:], row_rs2[:])
        rows_done = v.n
        c.wait("tensor", v, rows_done)
        c.op("tensor", "matmul", ps_bc[:], cn.ones_row[:], row_mu2[:],
             start=True, stop=True, inc=[(t_, 1)])
        bc1 = t_.n
        c.wait("vector", t_, bc1)
        c.cop("vector", "tensor_copy", mu2_bc[:], ps_bc[:])
        mbc_done = v.n
        c.wait("tensor", v, mbc_done)
        c.op("tensor", "matmul", ps_bc[:], cn.ones_row[:], row_rs2[:],
             start=True, stop=True, inc=[(t_, 1)])
        bc2 = t_.n
        c.wait("vector", t_, bc2)
        c.cop("vector", "tensor_copy", rs2_bc[:], ps_bc[:])
        for mo in range(HC):
            c.cop("vector", "tensor_tensor", t1s[:], h_bf[:, mo, :], mu2_bc[:],
                  ALU.subtract)
            c.cop("vector", "tensor_tensor", t1s[:], t1s[:], rs2_bc[:],
                  ALU.mult)
            c.cop("vector", "tensor_scalar", h_bf[:, mo, :], t1s[:],
                  g2_fm[:, mo:mo + 1], b2g_fm[:, mo:mo + 1], ALU.mult, ALU.add)
        h2_done = v.n
        c.wait_all("tensor")
        c.wait("sync", v)
        c.wait("sync", a)
        c.wait("sync", t_)
        for ts in range(4):
            t = tg * 4 + ts
            c.wait("sync", stw)
            xldp = c.dma("sync", xres[:], x_in[t * 128:(t + 1) * 128, :],
                         sem=c.rot[7])
            for half in range(2):
                pp = pst[half]
                for hf in range(HC):
                    c.op("tensor", "matmul", pp[:],
                         h_bf[:, hf, ts * 128:(ts + 1) * 128],
                         w2T[:, hf, half * 512:(half + 1) * 512],
                         start=(hf == 0), stop=(hf == HC - 1),
                         inc=[(t_, 1)] if hf == HC - 1 else None)
            mm2p = t_.n
            c.wait("vector", t_, mm2p)
            c.wait("vector", c.rot[7], xldp)
            c.wait("vector", stw)
            for half in range(2):
                c.op("vector", "scalar_tensor_tensor",
                     xo[:, half * 512:(half + 1) * 512], pst[half][:],
                     c2h[:], xres[:, half * 512:(half + 1) * 512],
                     ALU.mult, ALU.add, inc="auto")
                c.cop("vector", "tensor_tensor",
                      xo[:, half * 512:(half + 1) * 512],
                      xo[:, half * 512:(half + 1) * 512],
                      b2_bc[:, half * 512:(half + 1) * 512], ALU.add)
            ev = v.n
            c.wait("sync", v, ev)
            c.dma("sync", x_out[t * 128:(t + 1) * 128, :], xo[:], sem=stw)
            c.wait("tensor", v, ev)
    lst.close()
    sst.close()
    wst.close()
    c.barrier()


def build_ffn(T=2048):
    c = Ctx("ffn")
    cn = Consts(c)
    x_in = c.dram_in("x", [BPC * T, D])
    w1 = c.dram_in("w1", [4 * D, D])
    b1 = c.dram_in("b1", [4 * D])
    w2 = c.dram_in("w2", [D, 4 * D])
    b2 = c.dram_in("b2", [D])
    g1 = c.dram_in("ln1_g", [D])
    b1g = c.dram_in("ln1_b", [D])
    g2 = c.dram_in("ln2_g", [4 * D])
    b2g = c.dram_in("ln2_b", [4 * D])
    x_out = c.dram_out("y", [BPC * T, D])
    emit_ffn(c, cn, x_in, x_out, w1, b1, w2, b2, g1, b1g, g2, b2g, T)
    return _finish(c)


def emit_moe(c: Ctx, cn: Consts, x_in, x_out, wr_d, wg_d, wgate_d, wup_d,
             wdown_d, T):
    NE, H = 8, 4 * D
    ntiles = T // 128
    v, a, t_, stw = c.s["v"], c.s["a"], c.s["t"], c.s["st"]
    wst = c.scope()
    wrg = c.sb("wrg", [128, KC, 16], F32, wst)
    sw_bc = c.sb("sw_bc", [128, 16], F32, wst)
    sel = c.sb("sel", [8, 1024], F32, wst)
    c.wait("sync", t_)
    c.wait("sync", v)
    c.wait("sync", a)
    c.dma("sync", wrg[:, :, 0:8],
          wr_d.rearrange("(k p) e -> p k e", p=128), sem=c.rot[8])
    ldp = c.dma("sync", wrg[:, :, 8:16],
                wg_d.rearrange("(k p) e -> p k e", p=128), sem=c.rot[8])
    c.op("gpsimd", "memset", sel[:], 1.0, inc="auto")
    c.cop("gpsimd", "affine_select", out=sel[:], in_=sel[:],
          compare_op=ALU.is_ge, fill=0.0, base=0,
          pattern=[[1, 1024]], channel_multiplier=-128)
    c.cop("gpsimd", "affine_select", out=sel[:], in_=sel[:],
          compare_op=ALU.is_ge, fill=0.0, base=127,
          pattern=[[-1, 1024]], channel_multiplier=128)
    selr = c.s["g"].n
    psw = contextlib.ExitStack()
    ppA = c.ps("ppA", [128, 512], F32, psw)
    swcol = c.sb("swcol", [16, 1], F32, psw)
    swrow = c.sb("swrow", [1, 16], F32, psw)
    c.wait("tensor", c.rot[8], ldp)
    c.wait("tensor", c.s["g"], cn.ready)
    c.wait_all("tensor")
    for k in range(KC):
        c.op("tensor", "matmul", ppA[0:16, 0:1], wrg[:, k, :],
             cn.ones_col_f[:], start=(k == 0), stop=(k == KC - 1),
             inc=[(t_, 1)] if k == KC - 1 else None)
    c.wait("scalar", t_)
    c.op("scalar", "copy", swcol[:], ppA[0:16, 0:1], inc="auto")
    c.wait("tensor", a)
    c.op("tensor", "transpose", ppA[0:1, 64:80], swcol[:],
         cn.ident_f[0:16, 0:16], inc=[(t_, 1)])
    c.wait("scalar", t_)
    c.op("scalar", "copy", swrow[:], ppA[0:1, 64:80], inc="auto")
    c.wait("tensor", a)
    c.op("tensor", "matmul", ppA[:, 128:144], cn.ones_row[:], swrow[:],
         start=True, stop=True, inc=[(t_, 1)])
    c.wait("vector", t_)
    c.op("vector", "tensor_copy", sw_bc[:], ppA[:, 128:144], inc="auto")
    psw.close()
    c.barrier()

    for s in range(BPC):
        row0 = s * T
        sst = c.scope()
        x_tm = c.sb("x_tm", [128, ntiles, D], F32, sst)
        xn_fm = c.sb("xn_fm", [128, KC, T], BF16, sst)
        cw_fm = c.sb("cw_fm", [8, T], F32, sst)
        mu_s = c.sb("mu_s", [128, ntiles], F32, sst)
        rs_s = c.sb("rs_s", [128, ntiles], F32, sst)
        nr1_s = c.sb("nr1_s", [128, ntiles], F32, sst)
        p1 = contextlib.ExitStack()
        var_s = c.sb("var_s", [128, ntiles], F32, p1)
        tmp_s = c.sb("tmp_s", [128, ntiles], F32, p1)
        junks = [c.sb(f"mjunk{i}", [128, D], BF16, p1) for i in range(2)]
        vplan, aplan = {}, {}
        for t in range(ntiles):
            rsm = c.rot[t % 2]
            if t >= 2:
                c.wait("sync", v, vplan[t - 2])
                c.wait("sync", a, aplan[t - 2])
            ldp = c.dma("sync", x_tm[:, t, :],
                        x_in[row0 + t * 128: row0 + (t + 1) * 128, :], sem=rsm)
            c.wait("vector", rsm, ldp)
            c.op("vector", "reduce_sum", mu_s[:, t:t + 1], x_tm[:, t, :],
                 AX.X, inc="auto")
            vplan[t] = v.n
            c.wait("scalar", rsm, ldp)
            if t >= 2:
                c.wait("scalar", a, aplan[t - 2])
            c.op("scalar", "activation", junks[t % 2][:], x_tm[:, t, :],
                 AF.Square, accum_out=var_s[:, t:t + 1], inc="auto")
            aplan[t] = a.n
        c.wait("vector", a)
        c.cop("vector", "tensor_scalar_mul", mu_s[:], mu_s[:], 1.0 / D)
        c.cop("vector", "tensor_scalar_mul", var_s[:], var_s[:], 1.0 / D)
        c.cop("vector", "tensor_tensor", tmp_s[:], mu_s[:], mu_s[:], ALU.mult)
        c.cop("vector", "tensor_tensor", var_s[:], var_s[:], tmp_s[:],
              ALU.subtract)
        c.cop("vector", "tensor_scalar_add", tmp_s[:], var_s[:], EPS)
        c.wait("scalar", v)
        c.op("scalar", "activation", tmp_s[:], tmp_s[:], AF.Sqrt, inc="auto")
        c.wait("vector", a)
        c.cop("vector", "reciprocal", rs_s[:], tmp_s[:])
        c.cop("vector", "tensor_scalar_mul", nr1_s[:], rs_s[:], -1.0)
        tmb = [c.sb(f"mxntm{i}", [128, D], BF16, p1) for i in range(2)]
        pst2 = [c.ps(f"mxnps{i}", [128, 1024], BF16, p1) for i in range(2)]
        c.wait("tensor", c.s["g"], cn.ready)
        c.wait("vector", v)
        tplan, eplan = {}, {}
        for t in range(ntiles):
            tm, pp = tmb[t % 2], pst2[t % 2]
            if t >= 2:
                c.wait("vector", t_, tplan[t - 2])
            c.op("vector", "tensor_scalar", tm[:], x_tm[:, t, :],
                 mu_s[:, t:t + 1], rs_s[:, t:t + 1], ALU.subtract, ALU.mult,
                 inc="auto")
            vp = v.n
            c.wait("tensor", v, vp)
            if t >= 2:
                c.wait("tensor", a, eplan[t - 2])
            for k in range(KC):
                c.op("tensor", "transpose", pp[:, k * 128:(k + 1) * 128],
                     tm[:, k * 128:(k + 1) * 128], cn.ident_b[:],
                     inc=[(t_, 1)] if k == KC - 1 else None)
            tplan[t] = t_.n
            c.wait("scalar", t_, tplan[t])
            c.op("scalar", "copy", xn_fm[:, :, t * 128:(t + 1) * 128], pp[:],
                 inc="auto")
            eplan[t] = a.n
        p1.close()
        c.barrier()
        p2 = contextlib.ExitStack()
        pxf = c.ps("pxf", [128, 1024], F32, p2)
        pl16 = c.ps("pl16", [128, 512], F32, p2)
        pcw = c.ps("pcw", [128, 512], F32, p2)
        xf_sb = c.sb("xf_sb", [128, KC, 128], F32, p2)
        lg = c.sb("lg", [128, 16], F32, p2)
        lg10 = c.sb("lg10", [128, 16], F32, p2)
        nm8 = c.sb("nm8", [128, 1], F32, p2)
        rs8 = c.sb("rs8", [128, 1], F32, p2)
        cw_tm = c.sb("cw_tm", [128, 8], F32, p2)
        c.wait_all("tensor")
        prev_lg = None
        prev_cw = None
        for t in range(ntiles):
            prev_ev = None
            for g0 in range(0, KC, 4):
                if prev_ev is not None:
                    c.wait("tensor", v, prev_ev)
                for j in range(4):
                    k = g0 + j
                    c.op("tensor", "transpose", pxf[:, j * 128:(j + 1) * 128],
                         x_tm[:, t, k * 128:(k + 1) * 128], cn.ident_f[:],
                         inc=[(t_, 1)] if j == 3 else None)
                tp = t_.n
                c.wait("vector", t_, tp)
                for j in range(4):
                    k = g0 + j
                    c.op("vector", "tensor_copy", xf_sb[:, k, :],
                         pxf[:, j * 128:(j + 1) * 128], inc="auto")
                prev_ev = v.n
            c.wait("tensor", v, prev_ev)
            if prev_lg is not None:
                c.wait("tensor", v, prev_lg)
            for k in range(KC):
                c.op("tensor", "matmul", pl16[:, 0:16], xf_sb[:, k, :],
                     wrg[:, k, :], start=(k == 0), stop=(k == KC - 1),
                     inc=[(t_, 1)] if k == KC - 1 else None)
            lgp = t_.n
            c.wait("vector", t_, lgp)
            c.cop("vector", "scalar_tensor_tensor", lg[:], sw_bc[:],
                  mu_s[:, t:t + 1], pl16[:, 0:16], ALU.mult, ALU.subtract)
            prev_lg = v.n
            c.cop("vector", "tensor_scalar_mul", lg[:], lg[:],
                  nr1_s[:, t:t + 1])
            c.cop("vector", "reduce_max", nm8[:], lg[:, 0:8], AX.X,
                  negate=True)
            c.wait("scalar", v)
            c.op("scalar", "activation", lg[:, 0:8], lg[:, 0:8], AF.Exp,
                 bias=nm8[:], accum_out=rs8[:], inc="auto")
            c.wait("vector", a)
            c.cop("vector", "reciprocal", rs8[:], rs8[:])
            c.cop("vector", "tensor_scalar_mul", lg[:, 0:8], lg[:, 0:8],
                  rs8[:])
            c.cop("vector", "tensor_scalar_mul", lg10[:, 8:16], lg[:, 8:16],
                  10.0)
            c.cop("vector", "reduce_max", nm8[:], lg10[:, 8:16], AX.X,
                  negate=True)
            c.wait("scalar", v)
            c.op("scalar", "activation", lg10[:, 8:16], lg10[:, 8:16], AF.Exp,
                 bias=nm8[:], accum_out=rs8[:], inc="auto")
            c.wait("vector", a)
            c.cop("vector", "reciprocal", rs8[:], rs8[:])
            c.cop("vector", "tensor_scalar_mul", lg10[:, 8:16],
                  lg10[:, 8:16], rs8[:])
            c.cop("vector", "tensor_tensor", cw_tm[:], lg[:, 0:8],
                  lg10[:, 8:16], ALU.mult)
            c.cop("vector", "reduce_sum", rs8[:], cw_tm[:], AX.X)
            c.cop("vector", "tensor_scalar_add", rs8[:], rs8[:], 1e-9)
            c.cop("vector", "reciprocal", rs8[:], rs8[:])
            c.cop("vector", "tensor_scalar_mul", cw_tm[:], cw_tm[:], rs8[:])
            cw_done = v.n
            c.wait("tensor", v, cw_done)
            if prev_cw is not None:
                c.wait("tensor", a, prev_cw)
            c.op("tensor", "transpose", pcw[0:8, 0:128], cw_tm[:],
                 cn.ident_f[:], inc=[(t_, 1)])
            ctp = t_.n
            c.wait("scalar", t_, ctp)
            c.op("scalar", "copy", cw_fm[:, t * 128:(t + 1) * 128],
                 pcw[0:8, 0:128], inc="auto")
            prev_cw = c.s["a"].n
        p2.close()
        c.barrier()
        p3 = contextlib.ExitStack()
        wgs = [c.sb(f"wgs{i}", [128, KC, 512], BF16, p3) for i in range(2)]
        wus = [c.sb(f"wus{i}", [128, KC, 512], BF16, p3) for i in range(2)]
        wds = [c.sb(f"wds{i}", [128, 4, D], BF16, p3) for i in range(2)]
        h_sb = c.sb("h_sb", [128, 4, T], BF16, p3)
        cw_bc = c.sb("cw_bc", [128, T], BF16, p3)
        sg_sb = c.sb("sg_sb", [128, 512], BF16, p3)
        uc_sb = c.sb("uc_sb", [128, 512], F32, p3)
        ps_g = [c.ps(f"mpg{i}", [128, 512], F32, p3) for i in range(2)]
        ps_u = [c.ps(f"mpu{i}", [128, 512], F32, p3) for i in range(2)]
        ps_d = [c.ps(f"mpd{i}", [128, 512], F32, p3) for i in range(2)]
        ps_cw = c.ps("mpcw", [128, 512], F32, p3)
        c.wait("gpsimd", c.s["g"], selr)
        gidx = 0
        dwplan = {}
        for e in range(NE):
            c.wait_all("tensor")
            prev_ev = None
            for tt in range(T // 512):
                if prev_ev is not None:
                    c.wait("tensor", v, prev_ev)
                c.op("tensor", "matmul", ps_cw[:],
                     sel[:, e * 128:(e + 1) * 128],
                     cw_fm[:, tt * 512:(tt + 1) * 512],
                     start=True, stop=True, inc=[(t_, 1)])
                tp = t_.n
                c.wait("vector", t_, tp)
                c.op("vector", "tensor_copy",
                     cw_bc[:, tt * 512:(tt + 1) * 512], ps_cw[:], inc="auto")
                prev_ev = v.n
            cwb_done = v.n
            for grp in range(8):
                par = gidx % 2
                rsm = c.rot[10 + par]
                wg, wu, wd = wgs[par], wus[par], wds[par]
                if gidx >= 2:
                    c.wait("gpsimd", t_, dwplan[gidx - 2])
                c.dma("gpsimd", wg[:],
                      wgate_d[e].rearrange("(k p) f -> p k f", p=128)
                      [:, :, grp * 512:(grp + 1) * 512], sem=rsm)
                c.dma("gpsimd", wu[:],
                      wup_d[e].rearrange("(k p) f -> p k f", p=128)
                      [:, :, grp * 512:(grp + 1) * 512], sem=rsm)
                wldp = c.dma("gpsimd", wd[:],
                             wdown_d[e][grp * 512:(grp + 1) * 512, :]
                             .rearrange("(k p) f -> p k f", p=128), sem=rsm)
                c.wait("tensor", rsm, wldp)
                c.wait("tensor", v, cwb_done)
                hplan = {}
                sidx = 0
                for mp in range(4):
                    for tt in range(T // 512):
                        pg, pu = ps_g[sidx % 2], ps_u[sidx % 2]
                        if sidx >= 2:
                            c.wait("tensor", v, hplan[sidx - 2])
                        for k in range(KC):
                            c.op("tensor", "matmul", pg[:],
                                 wg[:, k, mp * 128:(mp + 1) * 128],
                                 xn_fm[:, k, tt * 512:(tt + 1) * 512],
                                 start=(k == 0), stop=(k == KC - 1),
                                 inc=[(t_, 1)] if k == KC - 1 else None)
                        gp = t_.n
                        for k in range(KC):
                            c.op("tensor", "matmul", pu[:],
                                 wu[:, k, mp * 128:(mp + 1) * 128],
                                 xn_fm[:, k, tt * 512:(tt + 1) * 512],
                                 start=(k == 0), stop=(k == KC - 1),
                                 inc=[(t_, 1)] if k == KC - 1 else None)
                        up = t_.n
                        c.wait("scalar", t_, gp)
                        c.wait("scalar", a)
                        c.wait("scalar", v)
                        c.op("scalar", "activation", sg_sb[:], pg[:], AF.Silu,
                             inc="auto")
                        sgp = c.s["a"].n
                        c.wait("vector", t_, up)
                        c.wait("vector", v)
                        c.op("vector", "tensor_tensor", uc_sb[:], pu[:],
                             cw_bc[:, tt * 512:(tt + 1) * 512], ALU.mult,
                             inc="auto")
                        c.wait("vector", a, sgp)
                        c.cop("vector", "tensor_tensor",
                              h_sb[:, mp, tt * 512:(tt + 1) * 512], uc_sb[:],
                              sg_sb[:], ALU.mult)
                        hplan[sidx] = v.n
                        sidx += 1
                h_done = v.n
                c.wait("tensor", v, h_done)
                dplan = {}
                didx = 0
                for ts in range(ntiles):
                    for half in range(2):
                        pd = ps_d[didx % 2]
                        if didx >= 2:
                            c.wait("tensor", v, dplan[didx - 2])
                        for hf in range(4):
                            c.op("tensor", "matmul", pd[:],
                                 h_sb[:, hf, ts * 128:(ts + 1) * 128],
                                 wd[:, hf, half * 512:(half + 1) * 512],
                                 start=(hf == 0), stop=(hf == 3),
                                 inc=[(t_, 1)] if hf == 3 else None)
                        dp = t_.n
                        c.wait("vector", t_, dp)
                        c.op("vector", "tensor_tensor",
                             x_tm[:, ts, half * 512:(half + 1) * 512],
                             x_tm[:, ts, half * 512:(half + 1) * 512],
                             pd[:], ALU.add, inc="auto")
                        dplan[didx] = v.n
                        didx += 1
                dwplan[gidx] = t_.n
                gidx += 1
        p3.close()
        c.barrier()
        for t in range(ntiles):
            c.dma("sync", x_out[row0 + t * 128:row0 + (t + 1) * 128, :],
                  x_tm[:, t, :], sem=stw)
        c.barrier()
        sst.close()
    wst.close()


def build_moe(T=2048):
    c = Ctx("moe")
    cn = Consts(c)
    x_in = c.dram_in("x", [BPC * T, D])
    wr = c.dram_in("w_router", [D, 8])
    wg = c.dram_in("w_gating", [D, 8])
    wgate = c.dram_in("w_gate", [8, D, 4 * D])
    wup = c.dram_in("w_up", [8, D, 4 * D])
    wdown = c.dram_in("w_down", [8, 4 * D, D])
    x_out = c.dram_out("y", [BPC * T, D])
    emit_moe(c, cn, x_in, x_out, wr, wg, wgate, wup, wdown, T)
    return _finish(c)


PERF_STOP = None


def _pcol(h):
    return (h % 2) * 512 + (h // 2) * 64


def emit_perf(c: Ctx, cn: Consts, x_in, x_out, omega_d, wq_d, wk_d, wv_d,
              wo_d, bo_d, T):
    ntiles = T // 128
    NH = 16
    LN8 = float(np.log(8.0))
    v, a, t_, stw = c.s["v"], c.s["a"], c.s["t"], c.s["st"]
    wst = c.scope()
    wqT = c.sb("wqT", [128, KC, D], BF16, wst)
    wkT = c.sb("wkT", [128, KC, D], BF16, wst)
    wvT = c.sb("wvT", [128, KC, D], BF16, wst)
    woT = c.sb("woT", [128, KC, D], BF16, wst)
    omg2 = c.sb("omg2", [128, 64], BF16, wst)
    mln8 = c.sb("mln8", [128, 1], F32, wst)
    for wd_, wt_ in ((wq_d, wqT), (wk_d, wkT), (wv_d, wvT), (wo_d, woT)):
        pwl = contextlib.ExitStack()
        w_t_load(c, wd_, wt_, D, D, cn, pwl)
        pwl.close()
    c.op("gpsimd", "memset", mln8[:], -LN8, inc="auto")
    pom = contextlib.ExitStack()
    om_f = c.sb("om_f", [64, 64], F32, pom)
    om_b = c.sb("om_b", [64, 64], BF16, pom)
    pot = c.ps("pot", [128, 128], BF16, pom)
    c.wait("sync", t_)
    c.wait("sync", v)
    c.wait("sync", a)
    ldp = c.dma("sync", om_f[:], omega_d[:], sem=c.rot[8])
    c.wait("vector", c.rot[8], ldp)
    c.cop("vector", "tensor_copy", om_b[:], om_f[:])
    omc = v.n
    c.wait("tensor", v, omc)
    c.wait("tensor", c.s["g"], cn.ready)
    c.wait_all("tensor")
    c.op("tensor", "transpose", pot[0:64, 0:64], om_b[:],
         cn.ident_b[0:64, 0:64], inc=[(t_, 1)])
    otp = t_.n
    c.wait("scalar", t_, otp)
    c.op("scalar", "copy", omg2[0:64, :], pot[0:64, 0:64], inc="auto")
    c.op("scalar", "copy", omg2[64:128, :], pot[0:64, 0:64], inc="auto")
    pom.close()
    c.barrier()
    if PERF_STOP == "w":
        return

    for s in range(BPC):
        row0 = s * T
        sst = c.scope()
        xn_fm = c.sb("xn_fm", [128, KC, T], BF16, sst)
        p1 = contextlib.ExitStack()
        mu_s, rs_s = stats_pass(c, x_in, row0, ntiles, False, p1, p1)
        xn_fm_build(c, x_in, row0, ntiles, mu_s, rs_s, xn_fm, cn, p1)
        p1.close()
        c.barrier()
        if PERF_STOP == "p1":
            sst.close()
            continue
        p3 = contextlib.ExitStack()
        ps_dl = [c.ps(f"psdl{i}", [128, 512], F32, p3) for i in range(2)]
        ps_phi = c.ps("ps_phi", [128, 1024], F32, p3)
        ps_ad = c.ps("ps_ad", [128, 512], F32, p3)
        ps_out = c.ps("ps_out", [128, 1024], F32, p3)
        ps_att = c.ps("ps_att", [128, 1024], BF16, p3)
        attnout_fm = xn_fm
        k_tm = c.sb("k_tm", [128, D], BF16, p3)
        v_tm = c.sb("v_tm", [128, D], BF16, p3)
        q_tm = c.sb("q_tm", [128, D], BF16, p3)
        kfm = c.sb("kfm", [128, KC, 128], BF16, p3)
        qfm = c.sb("qfm", [128, KC, 128], BF16, p3)
        phq = c.sb("phq", [128, KC, 128], BF16, p3)
        phqT = c.sb("phqT", [128, NH * 64], BF16, p3)
        phkT = c.sb("phkT", [128, NH * 64], BF16, p3)
        phkF = c.sb("phkF", [128, KC, 128], BF16, p3)
        at_sb = [c.sb(f"at_sb{i}", [128, 128], BF16, p3) for i in range(2)]
        S_sb = c.sb("S_sb", [128, KC, 65], F32, p3)
        S_bf = c.sb("S_bf", [128, KC, 65], BF16, p3)
        nrmk = c.sb("nrmk", [128, NH], F32, p3)
        kbias = c.sb("kbias", [128, NH], F32, p3)
        nrmq = c.sb("nrmq", [128, NH], F32, p3)
        qbias = c.sb("qbias", [128, NH], F32, p3)
        ebk = c.sb("ebk", [128, NH], F32, p3)
        ebq = c.sb("ebq", [128, NH], F32, p3)
        njunk = c.sb("njunk", [128, D], BF16, p3)
        den_rows = c.sb("den_rows", [1, NH * 128], F32, p3)
        den_tm = c.sb("den_tm", [16, 128], F32, p3)
        recip_c = c.sb("recip_c", [128, 16], F32, p3)
        at_tm = c.sb("at_tm", [128, D], BF16, p3)
        c.wait_all("tensor")
        c.wait_all("scalar")
        c.wait_all("vector")
        c.op("vector", "memset", S_sb[:], 0.0, inc="auto")
        atp_cons = {}
        prev_rtp = None
        nchunks = 1 if PERF_STOP == "chunk0" else ntiles
        for i in range(nchunks):
            kvev = []
            for di, (dst, wslab) in enumerate(((k_tm, wkT), (v_tm, wvT),
                                               (q_tm, wqT))):
                for oo in range(2):
                    pp = ps_out[:, (oo * 512):(oo * 512 + 512)]
                    for k in range(KC):
                        c.op("tensor", "matmul", pp,
                             xn_fm[:, k, i * 128:(i + 1) * 128],
                             wslab[:, k, oo * 512:(oo + 1) * 512],
                             start=(k == 0), stop=(k == KC - 1),
                             inc=[(t_, 1)] if k == KC - 1 else None)
                    tp = t_.n
                    c.wait("vector", t_, tp)
                    c.op("vector", "tensor_copy",
                         dst[:, oo * 512:(oo + 1) * 512], pp, inc="auto")
                    kvev.append(v.n)
                c.wait("tensor", v, kvev[-1])
            kv_done = v.n
            if PERF_STOP == "kv":
                break
            c.wait("scalar", v, kv_done)
            c.wait("scalar", a)
            c.op("scalar", "activation", njunk[:], k_tm[:], AF.Square,
                 inc="auto")
            sqk = c.s["a"].n
            c.wait("vector", a, sqk)
            c.cop("vector", "reduce_sum", nrmk[:],
                  njunk[:].rearrange("p (h d) -> p h d", h=NH), AX.X)
            c.wait("scalar", v)
            c.op("scalar", "activation", njunk[:], q_tm[:], AF.Square,
                 inc="auto")
            sqq = c.s["a"].n
            c.wait("vector", a, sqq)
            c.cop("vector", "reduce_sum", nrmq[:],
                  njunk[:].rearrange("p (h d) -> p h d", h=NH), AX.X)
            c.cop("vector", "tensor_scalar", kbias[:], nrmk[:], -0.5, -LN8,
                  ALU.mult, ALU.add)
            c.cop("vector", "tensor_scalar", qbias[:], nrmq[:], -0.5, -LN8,
                  ALU.mult, ALU.add)
            kb_done = v.n
            c.wait("scalar", v, kb_done)
            c.wait("scalar", a)
            c.op("scalar", "activation", ebk[:],
                 kbias[:].rearrange("p (c two) -> p two c", two=2), AF.Exp,
                 inc="auto")
            c.wait("scalar", a)
            c.op("scalar", "activation", ebq[:],
                 qbias[:].rearrange("p (c two) -> p two c", two=2), AF.Exp,
                 inc="auto")
            eb_done = c.s["a"].n
            if PERF_STOP == "nrm":
                break
            c.wait("tensor", v, kv_done)
            for src_tm, dst_fm, ee in ((k_tm, kfm, "scalar"),
                                       (q_tm, qfm, "vector")):
                c.wait("tensor", a)
                c.wait("tensor", v)
                for h in range(NH):
                    ch, pb = h // 2, 64 * (h % 2)
                    c.op("tensor", "transpose",
                         ps_att[pb:pb + 64, ch * 128:(ch + 1) * 128],
                         src_tm[:, h * 64:(h + 1) * 64], cn.ident_b[:],
                         tile_position=(0, pb), inc=[(t_, 1)])
                tp_ = t_.n
                c.wait(ee, t_, tp_)
                meth = "copy" if ee == "scalar" else "tensor_copy"
                c.op(ee, meth, dst_fm[:], ps_att[:].rearrange(
                    "p (ch f) -> p ch f", ch=KC), inc="auto")
            kfm_done = c.s["a"].n
            kfm_done_v = v.n
            if PERF_STOP == "tfm":
                break
            c.wait("tensor", c.s["a"], kfm_done)
            c.wait("tensor", v, kfm_done_v)
            for h in range(NH):
                ch, pb = h // 2, 64 * (h % 2)
                c.op("tensor", "matmul", ps_phi[:, _pcol(h):_pcol(h) + 64],
                     qfm[pb:pb + 64, ch, :], omg2[pb:pb + 64, :],
                     start=True, stop=True,
                     tile_position=(pb, 0),
                     inc=[(t_, 1)] if h == NH - 1 else None)
            phqT_p = t_.n
            if PERF_STOP == "phqmm":
                break
            c.wait("scalar", t_, phqT_p)
            c.wait("scalar", a, eb_done)
            c.op("scalar", "activation", phqT[:], ps_phi[:], AF.Exp,
                 inc="auto")
            phqT_e = c.s["a"].n
            c.wait("vector", a, phqT_e)
            from concourse.bass import broadcast_tensor_aps as _bta
            _a1, _a2 = _bta(phqT[:].rearrange("p (j d) -> p j d", d=64),
                            ebq[:].rearrange("p (j o) -> p j o", o=1))
            c.cop("vector", "tensor_tensor", _a1, _a1, _a2, ALU.mult)
            phqT_done = v.n
            c.wait("tensor", v, phqT_done)
            for h in range(NH):
                ch, pb = h // 2, 64 * (h % 2)
                c.op("tensor", "transpose",
                     ps_att[pb:pb + 64, ch * 128:(ch + 1) * 128],
                     phqT[:, _pcol(h):_pcol(h) + 64], cn.ident_b[:],
                     tile_position=(0, pb), inc=[(t_, 1)])
            phq_tp = t_.n
            c.wait("scalar", t_, phq_tp)
            c.op("scalar", "copy", phq[:],
                 ps_att[:].rearrange("p (ch f) -> p ch f", ch=KC), inc="auto")
            phq_done = c.s["a"].n
            if PERF_STOP == "phq":
                break
            c.wait("tensor", c.s["a"], phq_done)
            for h in range(NH):
                ch, pb = h // 2, 64 * (h % 2)
                c.op("tensor", "matmul", ps_phi[:, _pcol(h):_pcol(h) + 64],
                     kfm[pb:pb + 64, ch, :], omg2[pb:pb + 64, :],
                     start=True, stop=True,
                     tile_position=(pb, 0),
                     inc=[(t_, 1)] if h == NH - 1 else None)
            phk_p = t_.n
            c.wait("scalar", t_, phk_p)
            c.op("scalar", "activation", phkT[:], ps_phi[:], AF.Exp,
                 inc="auto")
            phkT_e = c.s["a"].n
            c.wait("vector", a, phkT_e)
            _b1, _b2 = _bta(phkT[:].rearrange("p (j d) -> p j d", d=64),
                            ebk[:].rearrange("p (j o) -> p j o", o=1))
            c.cop("vector", "tensor_tensor", _b1, _b1, _b2, ALU.mult)
            phkT_done = v.n
            if PERF_STOP == "phk":
                break
            c.wait("tensor", v, phkT_done)
            for h in range(NH):
                ch, pb = h // 2, 64 * (h % 2)
                c.op("tensor", "transpose",
                     ps_att[pb:pb + 64, ch * 128:(ch + 1) * 128],
                     phkT[:, _pcol(h):_pcol(h) + 64], cn.ident_b[:],
                     tile_position=(0, pb), inc=[(t_, 1)])
            phf_tp = t_.n
            c.wait("scalar", t_, phf_tp)
            c.op("scalar", "copy", phkF[:],
                 ps_att[:].rearrange("p (ch f) -> p ch f", ch=KC), inc="auto")
            phf_done = c.s["a"].n
            if PERF_STOP == "phf":
                break
            if i > 0:
                c.wait("scalar", c.s["v"])
                c.op("scalar", "copy", S_bf[:], S_sb[:], inc="auto")
            sbf_done = c.s["a"].n
            c.wait_all("tensor")
            den_ev = {}
            sadd_pl = {}
            atev = {}
            for h in range(NH):
                ch, pb = h // 2, 64 * (h % 2)
                pdl = ps_dl[h % 2]
                ats = at_sb[h % 2]
                if h >= 2:
                    c.wait("tensor", v, sadd_pl[h - 2])
                    c.wait("tensor", a, den_ev[h - 2])
                c.op("tensor", "matmul", pdl[:, 0:128],
                     phkF[pb:pb + 64, ch, :], phq[pb:pb + 64, ch, :],
                     start=True, stop=True, tile_position=(pb, 0),
                     inc=[(t_, 1)])
                at_p = t_.n
                c.wait("vector", t_, at_p)
                if h >= 2:
                    c.wait("vector", t_, atp_cons.get(h - 2, 0))
                c.op("vector", "tensor_tensor", ats[:], pdl[:, 0:128],
                     cn.tri_keep[:], ALU.mult, inc="auto")
                at_ev = v.n
                atev[h] = at_ev
                c.wait("tensor", v, at_ev)
                first_of_bank = h % 8 == 0
                last_of_bank = h % 8 == 7
                nmm = 2 if i > 0 else 1
                c.op("tensor", "matmul", ps_out[:, h * 64:(h + 1) * 64],
                     ats[:], v_tm[:, h * 64:(h + 1) * 64],
                     start=first_of_bank, stop=(last_of_bank and nmm == 1))
                if i > 0:
                    c.op("tensor", "matmul", ps_out[:, h * 64:(h + 1) * 64],
                         phq[pb:pb + 64, ch, :], S_bf[pb:pb + 64, ch, 0:64],
                         start=False, stop=last_of_bank,
                         tile_position=(pb, 0))
                c.op("tensor", "matmul", pdl[0:1, 128:256], cn.ones_col_b[:],
                     ats[:], start=True, stop=(i == 0))
                if i > 0:
                    c.op("tensor", "matmul", pdl[0:1, 128:256],
                         S_bf[pb:pb + 64, ch, 64:65], phq[pb:pb + 64, ch, :],
                         start=False, stop=True, tile_position=(pb, 0))
                c.op("tensor", "matmul", pdl[pb:pb + 64, 256:320],
                     phkT[:, _pcol(h):_pcol(h) + 64],
                     v_tm[:, h * 64:(h + 1) * 64],
                     start=True, stop=False, tile_position=(0, pb))
                c.op("tensor", "matmul", pdl[pb:pb + 64, 320:321],
                     phkT[:, _pcol(h):_pcol(h) + 64], cn.ones_col_b[:],
                     start=False, stop=True, tile_position=(0, pb),
                     inc=[(t_, 1)])
                upd_p = t_.n
                atp_cons[h] = upd_p
                c.wait("vector", t_, upd_p)
                c.op("vector", "tensor_tensor", S_sb[pb:pb + 64, ch, :],
                     S_sb[pb:pb + 64, ch, :],
                     pdl[pb:pb + 64, 256:321], ALU.add, inc="auto")
                sadd_pl[h] = v.n
                c.wait("scalar", v, sadd_pl[h])
                if i > 0:
                    c.wait("scalar", a, sbf_done)
                c.op("scalar", "copy", den_rows[0:1, h * 128:(h + 1) * 128],
                     pdl[0:1, 128:256], inc="auto")
                den_ev[h] = c.s["a"].n
            prev_den_ev = den_ev[NH - 1]
            heads_done = t_.n
            if PERF_STOP == "heads":
                break
            c.wait("gpsimd", a, prev_den_ev)
            if prev_rtp is not None:
                c.wait("gpsimd", t_, prev_rtp)
            dldp = c.dma("gpsimd", den_tm[:],
                         den_rows.rearrange("o (h t) -> o h t", h=16),
                         sem=c.rot[9])
            c.wait("tensor", c.rot[9], dldp)
            c.op("tensor", "transpose", ps_ad[:, 256:272], den_tm[:],
                 cn.ident_f[0:16, 0:16], inc=[(t_, 1)])
            rtp = t_.n
            prev_rtp = rtp
            c.wait("vector", t_, rtp)
            c.cop("vector", "tensor_scalar_add", recip_c[:], ps_ad[:, 256:272],
                  1e-6)
            c.cop("vector", "reciprocal", recip_c[:], recip_c[:])
            rc_done = v.n
            if PERF_STOP == "recip":
                break
            c.wait("vector", t_, heads_done)
            c.wait("vector", v, rc_done)
            for h in range(NH):
                c.op("vector", "tensor_scalar_mul", at_tm[:, h * 64:(h + 1) * 64],
                     ps_out[:, h * 64:(h + 1) * 64], recip_c[:, h:h + 1],
                     inc="auto")
            att_ev = v.n
            c.wait("tensor", v, att_ev)
            c.wait("tensor", a, phf_done)
            for k in range(KC):
                c.op("tensor", "transpose", ps_att[:, k * 128:(k + 1) * 128],
                     at_tm[:, k * 128:(k + 1) * 128], cn.ident_b[:],
                     inc=[(t_, 1)] if k == KC - 1 else None)
            aot = t_.n
            c.wait("scalar", t_, aot)
            c.op("scalar", "copy", attnout_fm[:, :, i * 128:(i + 1) * 128],
                 ps_att[:], inc="auto")
        p3.close()
        c.barrier()
        if PERF_STOP in ("chunk0", "scan"):
            sst.close()
            continue
        p4 = contextlib.ExitStack()
        po = [c.ps(f"po{i}", [128, 512], F32, p4) for i in range(2)]
        bo_bc = c.sb("bo_bc", [128, D], F32, p4)
        xres = c.sb("xres", [128, D], F32, p4)
        xo = c.sb("xo", [128, D], F32, p4)
        row_broadcast(c, bo_d, D, bo_bc, cn, p4, rsl=8)
        c.wait_all("tensor")
        for t in range(ntiles):
            c.wait("sync", stw)
            xldp = c.dma("sync", xres[:],
                         x_in[row0 + t * 128: row0 + (t + 1) * 128, :],
                         sem=c.rot[7])
            for half in range(2):
                pp = po[half]
                for k in range(KC):
                    c.op("tensor", "matmul", pp[:],
                         attnout_fm[:, k, t * 128:(t + 1) * 128],
                         woT[:, k, half * 512:(half + 1) * 512],
                         start=(k == 0), stop=(k == KC - 1),
                         inc=[(t_, 1)] if k == KC - 1 else None)
            mmp = t_.n
            c.wait("vector", t_, mmp)
            c.wait("vector", c.rot[7], xldp)
            c.wait("vector", stw)
            for half in range(2):
                c.op("vector", "tensor_tensor",
                     xo[:, half * 512:(half + 1) * 512], po[half][:],
                     xres[:, half * 512:(half + 1) * 512], ALU.add, inc="auto")
                c.cop("vector", "tensor_tensor",
                      xo[:, half * 512:(half + 1) * 512],
                      xo[:, half * 512:(half + 1) * 512],
                      bo_bc[:, half * 512:(half + 1) * 512], ALU.add)
            ev = v.n
            c.wait("sync", v, ev)
            c.dma("sync", x_out[row0 + t * 128: row0 + (t + 1) * 128, :],
                  xo[:], sem=stw)
            c.wait("tensor", v, ev)
        p4.close()
        c.barrier()
        sst.close()
    wst.close()


def build_perf(T=2048):
    c = Ctx("perf")
    cn = Consts(c)
    x_in = c.dram_in("x", [BPC * T, D])
    omega = c.dram_in("omega", [64, 64])
    wq = c.dram_in("wq", [D, D])
    wk = c.dram_in("wk", [D, D])
    wv = c.dram_in("wv", [D, D])
    wo = c.dram_in("wo", [D, D])
    bo = c.dram_in("bo", [D])
    x_out = c.dram_out("y", [BPC * T, D])
    emit_perf(c, cn, x_in, x_out, omega, wq, wk, wv, wo, bo, T)
    return _finish(c)


def build_full(T=2048):
    c = Ctx("full")
    cn = Consts(c)
    x_in = c.dram_in("x", [BPC * T, D])
    w = {}
    w["w_qkv"] = c.dram_in("w_qkv", [3 * D, D])
    w["gamma"] = c.dram_in("gamma", [D])
    w["beta"] = c.dram_in("beta", [D])
    w["w1"] = c.dram_in("w1", [4 * D, D])
    w["b1"] = c.dram_in("b1", [4 * D])
    w["w2"] = c.dram_in("w2", [D, 4 * D])
    w["b2"] = c.dram_in("b2", [D])
    w["ln1_g"] = c.dram_in("ln1_g", [D])
    w["ln1_b"] = c.dram_in("ln1_b", [D])
    w["ln2_g"] = c.dram_in("ln2_g", [4 * D])
    w["ln2_b"] = c.dram_in("ln2_b", [4 * D])
    w["omega"] = c.dram_in("omega", [64, 64])
    w["wq"] = c.dram_in("wq", [D, D])
    w["wk"] = c.dram_in("wk", [D, D])
    w["wv"] = c.dram_in("wv", [D, D])
    w["wo"] = c.dram_in("wo", [D, D])
    w["bo"] = c.dram_in("bo", [D])
    w["Wk"] = c.dram_in("Wk", [D, D])
    w["Wq"] = c.dram_in("Wq", [D, D])
    w["Wv"] = c.dram_in("Wv", [D, D])
    w["w_router"] = c.dram_in("w_router", [D, 8])
    w["w_gating"] = c.dram_in("w_gating", [D, 8])
    w["w_gate"] = c.dram_in("w_gate", [8, D, 4 * D])
    w["w_up"] = c.dram_in("w_up", [8, D, 4 * D])
    w["w_down"] = c.dram_in("w_down", [8, 4 * D, D])
    w["in_proj"] = c.dram_in("in_proj", [3 * D, D])
    w["out_proj"] = c.dram_in("out_proj", [D, D])
    y = c.dram_out("y", [BPC * T, D])
    xA = c.dram_tmp("xA", [BPC * T, D])
    xB = c.dram_tmp("xB", [BPC * T, D])

    emit_attn16(c, cn, x_in, xA, w["w_qkv"], T, "sl1",
                gamma_d=w["gamma"], beta_d=w["beta"])
    c.barrier()
    emit_ffn2(c, cn, xA, xB, w["w1"], w["b1"], w["w2"], w["b2"],
              w["ln1_g"], w["ln1_b"], w["ln2_g"], w["ln2_b"], T)
    c.barrier()
    emit_perf(c, cn, xB, xA, w["omega"], w["wq"], w["wk"], w["wv"],
              w["wo"], w["bo"], T)
    c.barrier()
    emit_ffn2(c, cn, xA, xB, w["w1"], w["b1"], w["w2"], w["b2"],
              w["ln1_g"], w["ln1_b"], w["ln2_g"], w["ln2_b"], T)
    c.barrier()
    emit_sl5(c, cn, xB, xA, w["Wk"], w["Wq"], w["Wv"], T)
    c.barrier()
    emit_moe(c, cn, xA, xB, w["w_router"], w["w_gating"], w["w_gate"],
             w["w_up"], w["w_down"], T)
    c.barrier()
    emit_attn16(c, cn, xB, xA, w["in_proj"], T, "sl7", wo_d=w["out_proj"])
    c.barrier()
    emit_moe(c, cn, xA, y, w["w_router"], w["w_gating"], w["w_gate"],
             w["w_up"], w["w_down"], T)
    return _finish(c)


_FUSED_CACHE = {}


def kernel_fused(**inputs):
    f32 = lambda k: np.ascontiguousarray(np.asarray(inputs[k], dtype=np.float32))
    x = f32("L0_attn_a_x")
    T = 2048
    shards = [np.ascontiguousarray(x[2 * cc:2 * cc + 2].reshape(BPC * T, D))
              for cc in range(NCORES)]
    if "nc" not in _FUSED_CACHE:
        _FUSED_CACHE["nc"] = build_full(T=T)
    nc = _FUSED_CACHE["nc"]
    names = {"w_qkv": "L0_attn_a_w_qkv", "gamma": "L0_attn_a_gamma",
             "beta": "L0_attn_a_beta",
             "w1": "L0_ffn_w1", "b1": "L0_ffn_b1", "w2": "L0_ffn_w2",
             "b2": "L0_ffn_b2", "ln1_g": "L0_ffn_ln1_g",
             "ln1_b": "L0_ffn_ln1_b", "ln2_g": "L0_ffn_ln2_g",
             "ln2_b": "L0_ffn_ln2_b",
             "omega": "L0_attn_b_omega", "wq": "L0_attn_b_wq",
             "wk": "L0_attn_b_wk", "wv": "L0_attn_b_wv",
             "wo": "L0_attn_b_wo", "bo": "L0_attn_b_bo",
             "Wk": "L1_attn_a_Wk", "Wq": "L1_attn_a_Wq", "Wv": "L1_attn_a_Wv",
             "w_router": "L1_ffn_w_router", "w_gating": "L1_ffn_w_gating",
             "w_gate": "L1_ffn_w_gate", "w_up": "L1_ffn_w_up",
             "w_down": "L1_ffn_w_down",
             "in_proj": "L1_attn_b_in_proj_weight",
             "out_proj": "L1_attn_b_out_proj_weight"}
    wvals = {k: f32(v) for k, v in names.items()}
    in_maps = [dict(wvals, x=shards[cc]) for cc in range(NCORES)]
    res = run_bass_kernel_spmd(nc, in_maps, core_ids=list(range(NCORES)))
    outs = [np.ascontiguousarray(np.asarray(res.results[cc]["y"],
                                            dtype=np.float32))
            for cc in range(NCORES)]
    return np.concatenate([o.reshape(1, BPC, T, D) for o in outs],
                          axis=0).reshape(16, T, D)


def emit_ffn2(c: Ctx, cn: Consts, x_in, x_out, w1_d, b1_d, w2_d, b2_d,
              g1_d, b1g_d, g2_d, b2g_d, T):
    H = 4 * D
    HC = H // 128
    ntiles = BPC * T // 128
    v, a, t_, stw = c.s["v"], c.s["a"], c.s["t"], c.s["st"]
    wst = c.scope()
    w1T = c.sb("w1T", [128, KC, H], BF16, wst)
    pw = contextlib.ExitStack()
    c1h = w_quant_t_load(c, w1_d, w1T, H, D, cn, wst, pw)
    pw.close()
    c.barrier()
    w2T = c.sb("w2T", [128, HC, D], BF16, wst)
    pw2 = contextlib.ExitStack()
    c2h = w_quant_t_load(c, w2_d, w2T, D, H, cn, wst, pw2)
    pw2.close()
    c.barrier()
    g1_bc = c.sb("g1_bc", [128, D], BF16, wst)
    b1g_bc = c.sb("b1g_bc", [128, D], BF16, wst)
    b1_bc = c.sb("b1_bc", [128, H], BF16, wst)
    combo_bc = c.sb("combo_bc", [128, D], F32, wst)
    g2_fm = c.sb("g2_fm", [128, HC], F32, wst)
    b2g_fm = c.sb("b2g_fm", [128, HC], F32, wst)
    pbc = contextlib.ExitStack()
    row_broadcast(c, g1_d, D, g1_bc, cn, pbc, rsl=8)
    row_broadcast(c, b1g_d, D, b1g_bc, cn, pbc, rsl=9)
    row_broadcast(c, b1_d, H, b1_bc, cn, pbc, rsl=10)
    pbc.close()
    pcl = contextlib.ExitStack()
    col_load(c, g2_d, H, g2_fm, cn, pcl, rsl=8)
    col_load(c, b2g_d, H, b2g_fm, cn, pcl, rsl=9)
    pcl.close()
    c.barrier()
    pp2 = contextlib.ExitStack()
    psb = c.ps("psb", [128, 512], F32, pp2)
    bw2_row = c.sb("bw2_row", [1, D], F32, pp2)
    b2_row = c.sb("b2_row", [1, D], F32, pp2)
    b2g_b = c.sb("b2g_b", [128, HC], BF16, pp2)
    c.cop("vector", "tensor_copy", b2g_b[:], b2g_fm[:])
    c.wait("sync", t_)
    c.wait("sync", v)
    c.wait("sync", a)
    ldb2 = c.dma("sync", b2_row[:], b2_d[None, :], sem=c.rot[8])
    c.wait_all("tensor")
    c.wait("tensor", v)
    for half in range(2):
        for hf in range(HC):
            c.op("tensor", "matmul", psb[0:1, 0:512],
                 b2g_b[:, hf:hf + 1], w2T[:, hf, half * 512:(half + 1) * 512],
                 start=(hf == 0), stop=(hf == HC - 1),
                 inc=[(t_, 1)] if hf == HC - 1 else None)
        tp = t_.n
        c.wait("vector", t_, tp)
        c.op("vector", "tensor_copy", bw2_row[0:1, half * 512:(half + 1) * 512],
             psb[0:1, 0:512], inc="auto")
        c.wait("tensor", v)
    for hf in range(HC):
        c.op("vector", "tensor_scalar_mul", w2T[:, hf, :], w2T[:, hf, :],
             g2_fm[:, hf:hf + 1], inc="auto")
    c.wait("vector", c.rot[8], ldb2)
    c.cop("vector", "tensor_scalar_mul", bw2_row[:], bw2_row[:], c2h[0:1, :])
    c.cop("vector", "tensor_tensor", bw2_row[:], bw2_row[:], b2_row[:],
          ALU.add)
    bwp = v.n
    c.wait("tensor", v, bwp)
    prev = None
    for half in range(2):
        if prev is not None:
            c.wait("tensor", v, prev)
        c.op("tensor", "matmul", psb[:, 0:512], cn.ones_row[:],
             bw2_row[0:1, half * 512:(half + 1) * 512], start=True, stop=True,
             inc=[(t_, 1)])
        tp = t_.n
        c.wait("vector", t_, tp)
        c.op("vector", "tensor_copy", combo_bc[:, half * 512:(half + 1) * 512],
             psb[:, 0:512], inc="auto")
        prev = v.n
    pp2.close()
    c.barrier()

    sst = c.scope()
    p1 = contextlib.ExitStack()
    mu_s, rs_s = stats_pass(c, x_in, 0, ntiles, True, sst, p1)
    p1.close()
    c.barrier()

    lst = contextlib.ExitStack()
    a_fm = c.sb("a_fm", [128, KC, 128], BF16, lst)
    h1_tm = [c.sb(f"h1tm{i}", [128, H], BF16, lst) for i in range(2)]
    h2_fm = c.sb("h2_fm", [128, HC, 128], BF16, lst)
    xbuf = [c.sb(f"fx{i}", [128, D], F32, lst) for i in range(2)]
    atm = c.sb("fatm", [128, D], BF16, lst)
    hjunk = c.sb("hjunk", [128, H], BF16, lst)
    s1 = c.sb("s1", [128, 1], F32, lst)
    s2 = c.sb("s2", [128, 1], F32, lst)
    s1c = c.sb("s1c", [128, 8], F32, lst)
    s2c = c.sb("s2c", [128, 8], F32, lst)
    mu2 = c.sb("mu2c", [128, 1], F32, lst)
    rs2 = c.sb("rs2c", [128, 1], F32, lst)
    xo = c.sb("fxo", [128, D], F32, lst)
    pst = [c.ps(f"fps{i}", [128, 512], F32, lst) for i in range(2)]
    ps_tp = [c.ps(f"ftp{i}", [128, 1024], BF16, lst) for i in range(2)]
    c.barrier()
    mm2ev = [0, 0]
    prev_afm = None
    prev_h2fm = None
    prev_mm1 = 0
    resid_v = {}
    h2tp_slot = {}
    for t in range(ntiles):
        b = xbuf[t % 2]
        h1 = h1_tm[t % 2]
        rsm = c.rot[t % 2]
        if t >= 2:
            c.wait("sync", v, resid_v[t - 2])
        ldp = c.dma("sync", b[:], x_in[t * 128:(t + 1) * 128, :], sem=rsm)
        c.wait("vector", rsm, ldp)
        if t >= 1:
            c.wait("vector", t_, prev_atp)
        c.op("vector", "tensor_scalar", atm[:], b[:], mu_s[:, t:t + 1],
             rs_s[:, t:t + 1], ALU.subtract, ALU.mult, inc="auto")
        c.cop("vector", "tensor_tensor", atm[:], atm[:], g1_bc[:], ALU.mult)
        c.cop("vector", "tensor_tensor", atm[:], atm[:], b1g_bc[:], ALU.add)
        a_done = v.n
        c.wait("tensor", v, a_done)
        if t >= 1:
            c.wait("tensor", a, prev_afm)
        for k in range(KC):
            c.op("tensor", "transpose", ps_tp[0][:, k * 128:(k + 1) * 128],
                 atm[:, k * 128:(k + 1) * 128], cn.ident_b[:],
                 inc=[(t_, 1)] if k == KC - 1 else None)
        prev_atp = t_.n
        c.wait("scalar", t_, prev_atp)
        if t >= 1:
            c.wait("scalar", v, prev_mm1)
        c.op("scalar", "copy", a_fm[:], ps_tp[0][:], inc="auto")
        prev_afm = a.n
        c.wait("tensor", a, prev_afm)
        mm1ev = {}
        mm1ev_v = {}
        for hf8 in range(8):
            pp = pst[hf8 % 2]
            if hf8 >= 2:
                c.wait("tensor", v, mm1ev_v[hf8 - 2])
            elif t >= 1:
                c.wait("tensor", v, mm2ev[hf8])
            for k in range(KC):
                c.op("tensor", "matmul", pp[:],
                     a_fm[:, k, :], w1T[:, k, hf8 * 512:(hf8 + 1) * 512],
                     start=(k == 0), stop=(k == KC - 1),
                     inc=[(t_, 1)] if k == KC - 1 else None)
            mmp = t_.n
            c.wait("vector", t_, mmp)
            if t >= 2 and hf8 == 0:
                c.wait("vector", a)
                c.wait("vector", t_, h2tp_slot[t % 2])
            c.op("vector", "scalar_tensor_tensor",
                 h1[:, hf8 * 512:(hf8 + 1) * 512], pp[:], c1h[:],
                 b1_bc[:, hf8 * 512:(hf8 + 1) * 512], ALU.mult, ALU.add,
                 inc="auto")
            sttp = v.n
            mm1ev_v[hf8] = sttp
            c.wait("scalar", v, sttp)
            c.op("scalar", "activation", h1[:, hf8 * 512:(hf8 + 1) * 512],
                 h1[:, hf8 * 512:(hf8 + 1) * 512], AF.Gelu,
                 accum_out=s1c[:, hf8:hf8 + 1], inc="auto")
            c.wait("scalar", a)
            c.op("scalar", "activation", hjunk[:, 0:512],
                 h1[:, hf8 * 512:(hf8 + 1) * 512], AF.Square,
                 accum_out=s2c[:, hf8:hf8 + 1], inc="auto")
            mm1ev[hf8] = c.s["a"].n
        prev_mm1 = v.n
        gelu_done = c.s["a"].n
        c.wait("vector", a, gelu_done)
        c.cop("vector", "reduce_sum", s1[:], s1c[:], AX.X)
        c.cop("vector", "reduce_sum", s2[:], s2c[:], AX.X)
        c.cop("vector", "tensor_scalar_mul", mu2[:], s1[:], 1.0 / H)
        c.cop("vector", "tensor_scalar_mul", s2[:], s2[:], 1.0 / H)
        c.cop("vector", "tensor_tensor", rs2[:], mu2[:], mu2[:], ALU.mult)
        c.cop("vector", "tensor_tensor", rs2[:], s2[:], rs2[:], ALU.subtract)
        c.cop("vector", "tensor_scalar_add", rs2[:], rs2[:], EPS)
        c.wait("scalar", v)
        c.op("scalar", "activation", rs2[:], rs2[:], AF.Sqrt, inc="auto")
        c.wait("vector", a)
        c.cop("vector", "reciprocal", rs2[:], rs2[:])
        c.cop("vector", "tensor_scalar", h1[:], h1[:], mu2[:], rs2[:],
              ALU.subtract, ALU.mult)
        h2_done = v.n
        c.wait("tensor", v, h2_done)
        if t >= 1:
            c.wait("tensor", v, prev_h2fm)
            c.wait("tensor", a, prev_h2fm_a)
        for g0 in range(0, HC, 8):
            if g0 > 0:
                c.wait("tensor", v, h2ev_v)
                c.wait("tensor", a, h2ev_a)
            for j in range(8):
                hf = g0 + j
                c.op("tensor", "transpose", ps_tp[1][:, j * 128:(j + 1) * 128],
                     h1[:, hf * 128:(hf + 1) * 128], cn.ident_b[:],
                     inc=[(t_, 1)] if j == 7 else None)
            h2tp = t_.n
            c.wait("scalar", t_, h2tp)
            c.op("scalar", "copy", h2_fm[:, g0:g0 + 8, :], ps_tp[1][:],
                 inc="auto")
            h2ev_v = v.n
            h2ev_a = c.s["a"].n
        h2tp_slot[t % 2] = t_.n
        prev_h2fm = v.n
        prev_h2fm_a = c.s["a"].n
        c.wait("tensor", v, prev_h2fm)
        c.wait("tensor", a, prev_h2fm_a)
        c.wait("sync", stw)
        c.wait("sync", v, a_done)
        xldp = c.dma("sync", b[:], x_in[t * 128:(t + 1) * 128, :],
                     sem=c.rot[2 + t % 2])
        for half in range(2):
            pp = pst[half]
            c.wait("tensor", v, mm2ev[half])
            for hf in range(HC):
                c.op("tensor", "matmul", pp[:],
                     h2_fm[:, hf, :], w2T[:, hf, half * 512:(half + 1) * 512],
                     start=(hf == 0), stop=(hf == HC - 1),
                     inc=[(t_, 1)] if hf == HC - 1 else None)
            mm2p = t_.n
            c.wait("vector", t_, mm2p)
            c.wait("vector", c.rot[2 + t % 2], xldp)
            c.wait("vector", stw)
            c.op("vector", "scalar_tensor_tensor",
                 xo[:, half * 512:(half + 1) * 512], pp[:], c2h[:],
                 b[:, half * 512:(half + 1) * 512], ALU.mult, ALU.add,
                 inc="auto")
            c.cop("vector", "tensor_tensor",
                  xo[:, half * 512:(half + 1) * 512],
                  xo[:, half * 512:(half + 1) * 512],
                  combo_bc[:, half * 512:(half + 1) * 512], ALU.add)
            mm2ev[half] = v.n
        ev = v.n
        resid_v[t] = v.n
        c.wait("sync", v, ev)
        c.dma("sync", x_out[t * 128:(t + 1) * 128, :], xo[:], sem=stw)
    lst.close()
    sst.close()
    wst.close()
    c.barrier()


def build_ffn2(T=2048):
    c = Ctx("ffn2")
    cn = Consts(c)
    x_in = c.dram_in("x", [BPC * T, D])
    w1 = c.dram_in("w1", [4 * D, D])
    b1 = c.dram_in("b1", [4 * D])
    w2 = c.dram_in("w2", [D, 4 * D])
    b2 = c.dram_in("b2", [D])
    g1 = c.dram_in("ln1_g", [D])
    b1g = c.dram_in("ln1_b", [D])
    g2 = c.dram_in("ln2_g", [4 * D])
    b2g = c.dram_in("ln2_b", [4 * D])
    x_out = c.dram_out("y", [BPC * T, D])
    emit_ffn2(c, cn, x_in, x_out, w1, b1, w2, b2, g1, b1g, g2, b2g, T)
    return _finish(c)


def attn_mh2(c: Ctx, cn: Consts, q_fm, k_fm, v_tm, T, finish_tile, stack,
             pt_banks=2):
    v, a, t_ = c.s["v"], c.s["a"], c.s["t"]
    ntiles = T // 128
    ps_s = c.ps("mh_s", [128, 2048], F32, stack)
    ps_pt = c.ps("mh_pt", [128, 1024 * pt_banks], BF16, stack)
    ps_out = c.ps("mh_o", [128, 1024], F32, stack)
    p_sb = [c.sb(f"mh_p{i}", [128, 2048], BF16, stack) for i in range(2)]
    pt_sb = c.sb("mh_ptsb", [128, 2048], BF16, stack)
    negm = c.sb("mh_negm", [128, 2], F32, stack)
    pmax = c.sb("mh_pmax", [128, 8], F32, stack)
    rsums = c.sb("mh_rsums", [128, 16], F32, stack)
    recips = c.sb("mh_recips", [128, 16], F32, stack)
    c.wait_all("tensor")
    fin_prev = None
    exp_done = {}
    mx_done = {}
    ptp = {}
    ptev = {}
    pvp = {}

    def softmax_dve(qi, h, sc_chunks):
        nk = (qi + 1) * 128
        nbs = len(sc_chunks)
        par = h % 2
        if h >= 2:
            c.wait("vector", a, exp_done[h - 2])
        for nb in range(nbs):
            w_ = min(512, nk - nb * 512)
            c.wait("vector", t_, sc_chunks[nb])
            if nb == nbs - 1:
                c.op("vector", "tensor_tensor", ps_s[:, (nk - 128):nk],
                     ps_s[:, (nk - 128):nk], cn.causal_add[:], ALU.add,
                     inc="auto")
                c.wait("vector", v)
            c.op("vector", "reduce_max", pmax[:, par * 4 + nb:par * 4 + nb + 1],
                 ps_s[:, nb * 512:nb * 512 + w_], AX.X, inc="auto")
        c.cop("vector", "reduce_max", negm[:, par:par + 1],
              pmax[:, par * 4:par * 4 + nbs], AX.X, negate=True)
        mx_done[h] = c.s["v"].n

    def softmax_act(qi, h):
        nk = (qi + 1) * 128
        par = h % 2
        c.wait("scalar", v, mx_done[h])
        if h >= 2:
            c.wait("scalar", t_, ptp.get(h - 2, 0))
        if rec_prev[0]:
            c.wait("scalar", v, rec_prev[0])
        c.op("scalar", "activation", p_sb[par][:, :nk], ps_s[:, :nk], AF.Exp,
             bias=negm[:, par:par + 1], scale=1.0,
             accum_out=rsums[:, h:h + 1], inc="auto")
        exp_done[h] = c.s["a"].n

    def stage_ptpv(qi, h):
        nk1 = qi + 1
        par = h % 2
        c.wait("tensor", a, exp_done[h])
        for bi, g0 in enumerate(range(0, nk1, 8)):
            gn = min(8, nk1 - g0)
            bank = (pt_bank[0] + bi) % pt_banks
            ee = "vector" if bank == 0 else "scalar"
            es = v if bank == 0 else a
            if ptev_bank[bank]:
                c.wait("tensor", es, ptev_bank[bank])
            for j in range(gn):
                kb = g0 + j
                c.op("tensor", "transpose",
                     ps_pt[:, bank * 1024 + j * 128:bank * 1024 + (j + 1) * 128],
                     p_sb[par][:, kb * 128:(kb + 1) * 128], cn.ident_b[:],
                     inc=[(t_, 1)] if j == gn - 1 else None)
            ptp[h] = t_.n
            c.wait(ee, t_, ptp[h])
            meth = "tensor_copy" if ee == "vector" else "copy"
            c.op(ee, meth, pt_sb[:, g0 * 128:(g0 + gn) * 128],
                 ps_pt[:, bank * 1024:bank * 1024 + gn * 128], inc="auto")
            ptev_bank[bank] = es.n
        pt_bank[0] = (pt_bank[0] + (nk1 + 7) // 8) % pt_banks
        c.wait("tensor", v, ptev_bank[0])
        c.wait("tensor", a, ptev_bank[1])
        first_of_bank = h % 8 == 0
        last_of_bank = h % 8 == 7
        for kb in range(nk1):
            c.op("tensor", "matmul", ps_out[:, 64 * h:64 * h + 64],
                 pt_sb[:, kb * 128:(kb + 1) * 128],
                 v_tm[:, kb, 64 * h:64 * h + 64],
                 start=(first_of_bank and kb == 0),
                 stop=(last_of_bank and kb == qi),
                 inc=[(t_, 1)] if kb == qi else None)
        pvp[h] = t_.n

    rec_prev = [0]
    ptev_bank = [0, 0]
    pt_bank = [0]
    for qi in range(ntiles):
        nk = (qi + 1) * 128
        sc_done = {}
        for h in range(16):
            ch, pb = h // 2, 64 * (h % 2)
            if h >= 1:
                c.wait("tensor", a, exp_done[h - 1])
            elif qi > 0:
                c.wait("tensor", a, exp_done[15])
            if fin_prev is not None:
                c.wait("tensor", v, fin_prev)
                fin_prev = None
            sc_chunks = []
            for nb in range((nk + 511) // 512):
                w_ = min(512, nk - nb * 512)
                c.op("tensor", "matmul", ps_s[:, nb * 512:nb * 512 + w_],
                     q_fm[pb:pb + 64, ch, qi * 128:(qi + 1) * 128],
                     k_fm[pb:pb + 64, ch, nb * 512:nb * 512 + w_],
                     start=True, stop=True, tile_position=(pb, 0),
                     inc=[(t_, 1)])
                sc_chunks.append(t_.n)
            sc_done[h] = t_.n
            softmax_dve(qi, h, sc_chunks)
            if h >= 1:
                stage_ptpv(qi, h - 1)
            softmax_act(qi, h)
        stage_ptpv(qi, 15)
        c.wait("vector", a, exp_done[15])
        c.cop("vector", "reciprocal", recips[:], rsums[:])
        rec_prev[0] = c.s["v"].n
        fin_prev = finish_tile(qi, ps_out, recips, pvp[15],
                               rec_done=rec_prev[0])
    return fin_prev
